# version 07
# baseline (speedup 1.0000x reference)
import contextlib
import numpy as np
import ml_dtypes
import concourse.bass as bass
import concourse.mybir as mybir
from concourse.bass_utils import run_bass_kernel_spmd

F32 = mybir.dt.float32
BF16 = mybir.dt.bfloat16
AF = mybir.ActivationFunctionType
ALU = mybir.AluOpType
AX = mybir.AxisListType
bfdt = ml_dtypes.bfloat16

T = 2048
D = 1024
NT = T // 512
DT = D // 128
MAGIC = 12582912.0
ARENA_COLS = 71040

LAYERS = [0, 1, 2, 3]
STAGES = ('ffn', 'attn')
N_EXPERTS = 16

CH512 = [slice(c * 512, (c + 1) * 512) for c in range(NT)]


class Sem:
    def __init__(self, h):
        self.h = h
        self.n = 0

    def inc(self, k=1):
        self.n += k
        return self.n


class OpRec:
    __slots__ = ("engine", "build", "waits", "uid", "val", "consumed", "is_dma",
                 "dma_sem", "dst_ap", "needs_clear", "epoch", "clear_dep")

    def __init__(self, engine, build, uid, is_dma=False):
        self.engine = engine
        self.build = build
        self.waits = set()
        self.uid = uid
        self.val = None
        self.consumed = False
        self.is_dma = is_dma


def _ap_range(ap):
    try:
        t = ap.tensor
    except AttributeError:
        return None
    import concourse.mybir as mb
    esz = mb.dt.size(ap.dtype)
    dims = [list(x) for x in ap.ap]
    off = ap.offset
    if not isinstance(off, int):
        return None
    if not dims:
        return (t.name, 0, 1, off * esz, (off + 1) * esz)
    pstep, pcount = dims[0]
    if pstep <= 0:
        pstep = 1
    p0 = off // pstep if pstep > 1 else 0
    col0 = off - p0 * pstep
    span = 1
    for st, ct in dims[1:]:
        span += abs(st) * (ct - 1)
    return (t.name, p0, p0 + pcount, col0 * esz, (col0 + span) * esz)


class MK:
    ENGINES = ("sync", "scalar", "vector", "tensor", "gpsimd")
    ESEM = {"vector": "dve", "scalar": "act", "tensor": "te", "gpsimd": "pool",
            "sync": "sp"}

    def __init__(self, nc, es):
        self.nc = nc
        self.es = es
        self.recs = []
        self.consts = {}
        self.semh = {}
        for name in ("te", "dve", "act", "pool", "sp"):
            self.semh[name] = es.enter_context(nc.semaphore("sem_" + name))
        self.NSYNCSEM = 8
        for i in range(self.NSYNCSEM):
            self.semh[f"dmah{i}"] = es.enter_context(nc.semaphore(f"sem_dmah{i}"))
        self.NSWSEM = 64
        for i in range(self.NSWSEM):
            self.semh[f"dmas{i}"] = es.enter_context(nc.semaphore(f"sem_dmas{i}"))
        self._dmah_rr = 0
        self._dmah_prev = [None] * self.NSYNCSEM
        self._dmah_count = [0] * self.NSYNCSEM
        self._dmas_rr = 0
        self._dmas_prev = [None] * 64
        self._dmas_count = [0] * 64
        self._epoch = 0
        self._ep_bar = {}
        self.psum = es.enter_context(nc.psum_tensor("pb", [128, 4096], F32))
        _pb = self.psum.ap() if hasattr(self.psum, "ap") else self.psum
        self.pbank = [_pb[:, i * 512:(i + 1) * 512] for i in range(8)]
        self.pbig = _pb
        _pbb = _pb.bitcast(BF16)
        self.pbank_bf = [_pbb[:, i * 1024:(i + 1) * 1024] for i in range(8)]
        self._rr = 0
        self._uid = 0
        self.segs = {}
        self.track_dram = False
        self.last_op = {e: None for e in self.ENGINES}

    def sb(self, name, shape, dtype):
        return self.es.enter_context(self.nc.sbuf_tensor(name, shape, dtype))

    def rr(self, mod=8):
        i = self._rr % mod
        self._rr += 1
        return i

    def const_input(self, name, arr):
        arr = np.ascontiguousarray(arr)
        self.consts[name] = arr
        dt = {np.dtype(np.float32): F32, np.dtype(bfdt): BF16}[arr.dtype]
        return self.nc.dram_tensor(name, list(arr.shape), dt, kind="ExternalInput").ap()

    def _dep_read(self, rec, ap):
        r = _ap_range(ap)
        if r is None:
            return
        name, p0, p1, b0, b1 = r
        if name not in self.segs:
            if not name.startswith(("arena", "xt", "pb", "c_", "scr_")):
                return
            self.segs[name] = []
        for seg in self.segs[name]:
            if seg[0] < p1 and p0 < seg[1] and seg[2] < b1 and b0 < seg[3]:
                if seg[4] is not None and seg[4] is not rec:
                    rec.waits.add(seg[4])
                seg[5].append(rec)

    def _dep_write(self, rec, ap, writer=None):
        r = _ap_range(ap)
        if r is None:
            return
        name, p0, p1, b0, b1 = r
        if name not in self.segs:
            if not name.startswith(("arena", "xt", "pb", "c_", "scr_")):
                return
            self.segs[name] = []
        wrec = writer if writer is not None else rec
        keep = []
        for seg in self.segs[name]:
            if seg[0] < p1 and p0 < seg[1] and seg[2] < b1 and b0 < seg[3]:
                if seg[4] is not None and seg[4] is not rec and seg[4] is not wrec:
                    rec.waits.add(seg[4])
                for rd in seg[5]:
                    if rd is not rec and rd is not wrec:
                        rec.waits.add(rd)
                if not (p0 <= seg[0] and seg[1] <= p1 and b0 <= seg[2]
                        and seg[3] <= b1):
                    keep.append(seg)
            else:
                keep.append(seg)
        keep.append([p0, p1, b0, b1, wrec, []])
        self.segs[name] = keep

    def _mkrec(self, engine, build, is_dma=False):
        self._uid += 1
        rec = OpRec(engine, build, self._uid, is_dma)
        rec.dst_ap = None
        rec.needs_clear = False
        rec.clear_dep = None
        rec.epoch = self._epoch
        bar = self._ep_bar.get(self._epoch - 1)
        if bar is not None and engine in bar:
            rec.waits.add(bar[engine])
        self.recs.append(rec)
        self.last_op[engine] = rec
        return rec

    def _dep_war(self, rec, ap):
        r = _ap_range(ap)
        if r is None or r[0] not in self.segs:
            return
        name, p0, p1, b0, b1 = r
        for seg in self.segs[name]:
            if seg[0] < p1 and p0 < seg[1] and seg[2] < b1 and b0 < seg[3]:
                if seg[4] is not None and seg[4] is not rec:
                    rec.waits.add(seg[4])
                for rd in seg[5]:
                    if rd is not rec:
                        rec.waits.add(rd)

    def dma(self, out_ap, in_ap, queue="sync", wait_edges=()):
        def build(e, o=out_ap, i=in_ap):
            return e.dma_start(out=o, in_=i)
        rec = self._mkrec(queue, build, is_dma=True)
        rec.dst_ap = out_ap
        if queue == "sync":
            si = self._dmah_rr % self.NSYNCSEM
            self._dmah_rr += 1
            rec.dma_sem = f"dmah{si}"
            self._dmah_count[si] += 16
            rec.val = self._dmah_count[si]
            prev = self._dmah_prev[si]
            if prev is not None:
                rec.waits.add(prev)
            self._dmah_prev[si] = rec
        else:
            si = self._dmas_rr % self.NSWSEM
            self._dmas_rr += 1
            rec.dma_sem = f"dmas{si}"
            self._dmas_count[si] += 16
            rec.val = self._dmas_count[si]
            prev = self._dmas_prev[si]
            if prev is not None:
                rec.waits.add(prev)
                self._dep_war(rec, prev.dst_ap)
            self._dmas_prev[si] = rec
        for w in wait_edges:
            if w is not None:
                rec.waits.add(w)
        self._dep_read(rec, in_ap)
        self._dep_write(rec, out_ap)
        rec.consumed = True
        return rec

    def op(self, engine, method, *args, wait_edges=(), inc=False, **kwargs):
        def build(e, m=method, a=args, k=kwargs):
            return getattr(e, m)(*a, **k)
        rec = self._mkrec(engine, build)
        for w in wait_edges:
            if w is not None:
                rec.waits.add(w)
        aps = [a for a in args if isinstance(a, bass.AP)]
        for a in aps[1:]:
            self._dep_read(rec, a)
        for kw, v in kwargs.items():
            if isinstance(v, bass.AP) and kw != "accum_out":
                self._dep_read(rec, v)
        if method != "nop" and aps:
            self._dep_write(rec, aps[0])
        ao = kwargs.get("accum_out")
        if isinstance(ao, bass.AP):
            self._dep_write(rec, ao)
        return rec

    def fence(self, engines=("tensor", "vector", "scalar", "gpsimd")):
        return [self.last_op[e] for e in engines if self.last_op[e] is not None]

    def mark(self, name):
        if not hasattr(self, "marks"):
            self.marks = []
            self._marksb = self.sb("marksb", [1, 4], F32)
        self.marks.append(name)
        self.op("scalar", "memzero", self._marksb[0:1, 0:4])

    def phase_barrier(self, extra_edges=()):
        edges = self.fence() + list(extra_edges)
        edges += [r for r in self._dmah_prev if r is not None]
        edges += [r for r in self._dmas_prev if r is not None]
        recs = []
        bar = {}
        for eng in ("tensor", "vector", "scalar", "gpsimd", "sync"):
            r = self.op(eng, "nop", wait_edges=edges)
            recs.append(r)
            bar[eng] = r
        self._ep_bar[self._epoch] = bar
        self._epoch += 1
        return recs

    def mm_group(self, bank_i, mms, wait_edges=()):
        n = len(mms)
        recs = []
        for j, (out_ap, lhsT, rhs) in enumerate(mms):
            start, stop = (j == 0), (j == n - 1)
            def build(e, o=out_ap, l=lhsT, r=rhs, st=start, sp=stop):
                return e.matmul(o, l, r, start=st, stop=sp)
            rec = self._mkrec("tensor", build)
            if j == 0:
                for w in wait_edges:
                    if w is not None:
                        rec.waits.add(w)
            self._dep_read(rec, lhsT)
            self._dep_read(rec, rhs)
            recs.append(rec)
        for out_ap in {id(m[0]): m[0] for m in mms}.values():
            self._dep_write(recs[0], out_ap, writer=recs[-1])
        return recs[-1]

    def mm_step(self, out_ap, lhsT, rhs, start, stop, wait_edges=()):
        def bld(e, o=out_ap, l=lhsT, r=rhs, st=start, sp=stop):
            return e.matmul(o, l, r, start=st, stop=sp)
        rec = self._mkrec("tensor", bld)
        for w in wait_edges:
            if w is not None:
                rec.waits.add(w)
        self._dep_read(rec, lhsT)
        self._dep_read(rec, rhs)
        self._dep_write(rec, out_ap)
        return rec

    def bank_te(self, bank_i, method, *args, wait_edges=(), **kwargs):
        return self.op("tensor", method, *args, wait_edges=wait_edges, **kwargs)

    def bank_read(self, bank_i, engine, method, *args, wait_edges=(), **kwargs):
        return self.op(engine, method, *args, wait_edges=wait_edges, **kwargs)

    def finalize(self):
        for rec in self.recs:
            for w in rec.waits:
                w.consumed = True
        for bar in self._ep_bar.values():
            for r in bar.values():
                r.consumed = True
        counters = {}
        semname = {}
        for rec in self.recs:
            if rec.is_dma:
                semname[rec.uid] = rec.dma_sem
                rec.consumed = True
                continue
            sname = self.ESEM[rec.engine]
            semname[rec.uid] = sname
            if rec.consumed:
                counters[sname] = counters.get(sname, 0) + 1
                rec.val = counters[sname]
        hw = {e: {} for e in self.ENGINES}
        plan = []
        for rec in self.recs:
            needs = {}
            for w in rec.waits:
                if w.val is None:
                    continue
                sname = semname[w.uid]
                if w.engine == rec.engine and not w.is_dma and not rec.is_dma:
                    pass
                cur = needs.get(sname, 0)
                if w.val > cur:
                    needs[sname] = w.val
            waits = []
            for sname, val in needs.items():
                if sname.startswith("dma"):
                    waits.append((self.semh[sname], val))
                    continue
                if hw[rec.engine].get(sname, 0) >= val:
                    continue
                hw[rec.engine][sname] = val
                waits.append((self.semh[sname], val))
            plan.append((rec, waits))
        block = self.es.enter_context(self.nc.Block())
        for name in self.ENGINES:
            def runner(eng, _n=name):
                for rec, waits in plan:
                    if rec.engine != _n:
                        continue
                    if rec.is_dma:
                        for sh, val in waits:
                            eng.wait_ge(sh, val)
                        inst = rec.build(eng)
                    else:
                        for sh, val in waits[1:]:
                            eng.wait_ge(sh, val)
                        inst = rec.build(eng)
                        if waits:
                            inst._wait_ge(waits[0][0], waits[0][1])
                    if rec.val is not None:
                        sname = semname[rec.uid]
                        inst.then_inc(self.semh[sname], 16 if rec.is_dma else 1)
            getattr(block, name)(runner)
        self.final_counts = counters


class Plan:

    def __init__(self, ar_ap, base=0):
        self.ar = ar_ap
        self.p = base

    def bf(self, ncols):
        off = (self.p + 15) & ~15
        self.p = off + ncols
        assert self.p <= ARENA_COLS, f"arena overflow: {self.p} > {ARENA_COLS}"
        return self.ar[:, off:off + ncols]

    def f32(self, ncols):
        off = (self.p + 15) & ~15
        self.p = off + 2 * ncols
        assert self.p <= ARENA_COLS, f"arena overflow: {self.p} > {ARENA_COLS}"
        return self.ar[:, off:off + 2 * ncols].bitcast(F32)



FCONSTS = [-20.0, 0.125, 1e-5, 1.0, 0.5, -1.0, 2.0, -0.5, 127.0, -128.0,
           12582912.0, -12582912.0, -3.0e38, 1e-12, 1e-5 ** 0.5]


def host_consts():
    c = {}
    c["fpool"] = np.tile(np.array(FCONSTS, np.float32)[None, :], (128, 1))
    c["ident_f32"] = np.eye(128, dtype=np.float32)
    c["ident_bf"] = np.eye(128, dtype=bfdt)
    c["ones_f32_128x1"] = np.ones((128, 1), dtype=np.float32)
    c["ones_bf_128x1"] = np.ones((128, 1), dtype=bfdt)
    c["ones_f32_1x128"] = np.ones((1, 128), dtype=np.float32)
    ROT, BASE = 32, 10000.0
    inv = 1.0 / BASE ** (np.arange(0, ROT, 2, dtype=np.float64) / ROT)
    fr = np.arange(T, dtype=np.float64)[:, None] * inv[None, :]
    cos, sin = np.cos(fr), np.sin(fr)
    COS = np.ones((128, T), dtype=np.float64)
    SIN = np.zeros((128, T), dtype=np.float64)
    for base in (0, 64):
        COS[base:base + 16] = cos.T
        COS[base + 16:base + 32] = cos.T
        SIN[base:base + 16] = -sin.T
        SIN[base + 16:base + 32] = sin.T
    c["l0_cos"] = COS.astype(bfdt)
    c["l0_sin"] = SIN.astype(bfdt)
    inv1 = 1.0 / 10000.0 ** (np.arange(0, 64, 2, dtype=np.float64) / 64)
    fr1 = np.arange(T, dtype=np.float64)[:, None] * inv1[None, :]
    c1, s1_ = np.cos(fr1), np.sin(fr1)
    COS1 = np.zeros((128, T), dtype=np.float64)
    SIN1 = np.zeros((128, T), dtype=np.float64)
    for r in range(128):
        p = (r % 64) // 2
        COS1[r] = c1[:, p]
        SIN1[r] = -s1_[:, p] if (r % 2 == 0) else s1_[:, p]
    c["l1_cos"] = COS1.astype(bfdt)
    c["l1_sin"] = SIN1.astype(bfdt)
    e64 = np.zeros((2, 128), dtype=np.float32)
    e64[0, 0:64] = 1.0
    e64[1, 64:128] = 1.0
    c["e64"] = e64
    c["e64_t"] = np.ascontiguousarray(e64.T)
    tri = np.triu(np.ones((128, 128), np.float32))
    c["l2_triu"] = tri.astype(bfdt)
    R64 = np.zeros((128, 128), np.float32)
    for blk in (0, 64):
        for i in range(32):
            R64[blk + 32 + i, blk + i] = -1.0
            R64[blk + i, blk + 32 + i] = 1.0
    c["l2_rot64"] = R64.astype(bfdt)
    band = np.full((128, 1408), -1e9, np.float32)
    for i in range(128):
        band[i, i + 384:i + 897] = 0.0
    c["l2_band"] = band.astype(bfdt)
    DH, WIN = 64, 512
    half_idx = np.arange(0, DH, 2, dtype=np.float64)
    invf = 1.0 / 10000.0 ** (half_idx / DH)
    tpos2 = np.arange(T, dtype=np.float64)
    fr2 = tpos2[:, None] * invf[None, :]
    fr2 = np.concatenate([fr2, fr2], axis=-1)
    base2 = (half_idx + 0.4 * DH) / (1.4 * DH)
    power = (tpos2 - T // 2) / (WIN // 2)
    xsc = base2[None, :] ** power[:, None]
    xsc = np.concatenate([xsc, xsc], axis=-1)
    cos2, sin2 = np.cos(fr2), np.sin(fr2)
    def tile128(a):
        return np.concatenate([a.T, a.T], axis=0)
    c["l2_cosq"] = (tile128(cos2 * xsc) * 8.0).astype(bfdt)
    c["l2_sinq"] = (tile128(sin2 * xsc) * 8.0).astype(bfdt)
    c["l2_cosk"] = tile128(cos2 / xsc).astype(bfdt)
    c["l2_sink"] = tile128(sin2 / xsc).astype(bfdt)
    sel = np.zeros((16, 2048), np.float32)
    for n in range(16):
        sel[n, n * 128:(n + 1) * 128] = 1.0
    c["sel16"] = sel.astype(bfdt)
    mv = np.zeros((4, 128, 512), dtype=np.float32)
    for v in range(4):
        qi = v * 128 + np.arange(128)[:, None]
        mv[v] = np.where(np.arange(512)[None, :] <= qi, 0.0, -1e9)
    c["l0_mask"] = np.ascontiguousarray(mv.transpose(1, 0, 2).reshape(128, 4 * 512)).astype(bfdt)
    return c



def emit_layernorm(b, g, xT, xn_tiles, pl, in_edges=()):
    ones_f = g["ones_f32_128x1"]
    onesr_f = g["ones_f32_1x128"]
    rows = pl.f32(2 * T)
    sqw = pl.f32(512 * DT)
    rep = pl.f32(2 * T)
    tmp = pl.f32(T)
    A = rows[0:1, 0:T]
    B = rows[0:1, T:2 * T]
    tmp2 = rep[0:1, 0:T]
    e_sq_mm = None
    for cch, sl in enumerate(CH512):
        ie = list(in_edges) if cch == 0 else []
        b.mm_group(0, [(b.pbank[0][0:1, :], ones_f[:, :], xT[d][:, sl])
                       for d in range(DT)], wait_edges=ie)
        e_list = [b.op("scalar", "activation", sqw[:, d * 512:(d + 1) * 512],
                       xT[d][:, sl], AF.Square,
                       wait_edges=ie + ([e_sq_mm] if e_sq_mm else []), inc=True)
                  for d in range(DT)]
        e_sq_mm = b.mm_group(1, [(b.pbank[1][0:1, :], ones_f[:, :],
                                  sqw[:, d * 512:(d + 1) * 512]) for d in range(DT)],
                             wait_edges=[e_list[-1]])
        b.bank_read(0, "vector", "tensor_copy", A[:, sl], b.pbank[0][0:1, :])
        e_ab = b.bank_read(1, "vector", "tensor_copy", B[:, sl], b.pbank[1][0:1, :])
    b.op("vector", "tensor_scalar", A, A, -1.0 / D, None, ALU.mult,
         wait_edges=[e_ab])
    b.op("vector", "tensor_scalar", B, B, 1.0 / D, None, ALU.mult)
    b.op("vector", "tensor_mul", tmp2, A, A)
    b.op("vector", "tensor_sub", B, B, tmp2)
    e_var = b.op("vector", "tensor_scalar", B, B, 1e-5, None, ALU.add, inc=True)
    e_sqr = b.op("scalar", "activation", B, B, AF.Sqrt,
                 wait_edges=[e_var], inc=True)
    b.op("vector", "reciprocal", B, B, wait_edges=[e_sqr])
    e_stats = b.op("vector", "tensor_mul", A, A, B, inc=True)
    rstd, nmr = B, A
    rstd_rep, nmr_rep = rep[:, 0:T], rep[:, T:2 * T]
    e_rep = None
    for cch, sl in enumerate(CH512):
        b.mm_group(2, [(b.pbank[2][:, :], onesr_f[:, :], rstd[:, sl])],
                   wait_edges=[e_stats])
        b.bank_read(2, "scalar", "copy", rstd_rep[:, sl], b.pbank[2][:, :])
        b.mm_group(3, [(b.pbank[3][:, :], onesr_f[:, :], nmr[:, sl])],
                   wait_edges=[e_stats])
        e_rep = b.bank_read(3, "scalar", "copy", nmr_rep[:, sl], b.pbank[3][:, :])
    edges = []
    for d in range(DT):
        b.op("vector", "tensor_mul", tmp[:, :], xT[d][:, :], rstd_rep[:, :],
             wait_edges=[e_rep] if d == 0 else [])
        e = b.op("vector", "tensor_add", xn_tiles[d][:, :], tmp[:, :],
                 nmr_rep[:, :], inc=True)
        edges.append(e)
    return edges


def wload(b, dst_ap2d, src_ap2d, queue="gpsimd", wait_edges=()):
    rows, cols = src_ap2d.shape
    a = rows // 128
    if a == 1:
        return b.dma(dst_ap2d, src_ap2d, queue=queue, wait_edges=wait_edges)
    src3 = src_ap2d.rearrange("(a p) n -> p a n", p=128)
    dst3 = dst_ap2d.rearrange("p (a n) -> p a n", a=a)
    return b.dma(dst3, src3, queue=queue, wait_edges=wait_edges)



def emit_l0_ffn(b, g, W, xn, sg, pl, xn_edges):
    onesr_f = g["ones_f32_1x128"]
    idf = g["ident_f32"]
    wg = pl.bf(DT * 16)
    lg = pl.f32(T)
    lt = pl.f32(16 * 16)
    cwt = pl.f32(16 * 16)
    cwf = lg
    sm = pl.f32(16)
    expv = pl.f32(16)
    msk = pl.f32(16)
    msk2 = pl.f32(16)
    ninf = pl.f32(16)
    w1buf = pl.bf(8 * 1024)
    w2buf = pl.bf(4 * 1024)
    mh = pl.bf(4 * T)
    gs = pl.bf(512)
    hbuf = pl.bf(512)
    cwrep = pl.bf(T)
    wsu = pl.bf(8 * 128)
    wsg = pl.bf(8 * 128)

    do_router = "router" in STAGES or "ffn" in STAGES
    do_experts = "experts" in STAGES or "ffn" in STAGES
    do_shared = "shared" in STAGES or "ffn" in STAGES
    n_experts = N_EXPERTS
    e_wg = wload(b, wg, W["l0_wg_t"])
    b.op("vector", "memset", ninf, -3.0e38)
    for d in range(DT):
        b.op("vector", "memset", sg[d][:, :], 0.0)
    e_zero = b.op("vector", "nop", inc=True)

    LV = globals().get("RLEVEL", 9)
    for cch, sl in enumerate(CH512) if (do_router and LV >= 2) else []:
        bk = b.rr()
        b.mm_group(bk, [(b.pbank[bk][0:16, :], wg[:, k * 16:(k + 1) * 16],
                         xn[k][:, sl]) for k in range(DT)],
                   wait_edges=[e_wg] + list(xn_edges))
        b.bank_read(bk, "scalar", "copy", lg[0:16, sl], b.pbank[bk][0:16, :])
    for ti in range(16) if (do_router and LV >= 3) else []:
        bk = b.rr()
        e = b.bank_te(bk, "transpose", b.pbank[bk][:, 0:16],
                      lg[0:16, ti * 128:(ti + 1) * 128], idf[0:16, 0:16])
        b.bank_read(bk, "vector", "tensor_copy", lt[:, ti * 16:(ti + 1) * 16],
                    b.pbank[bk][:, 0:16])
    e_cwt = []
    for ti in range(16) if (do_router and LV >= 4) else []:
        lt_t = lt[:, ti * 16:(ti + 1) * 16]
        b.op("vector", "reduce_max", sm[:, 0:1], lt_t, AX.X)
        e_nm = b.op("vector", "tensor_scalar", sm[:, 1:2], sm[:, 0:1], -1.0, None,
                    ALU.mult, inc=True)
        e_exp = b.op("scalar", "activation", expv, lt_t, AF.Exp, bias=sm[:, 1:2],
                     accum_out=sm[:, 2:3], wait_edges=[e_nm], inc=True)
        b.op("vector", "reciprocal", sm[:, 3:4], sm[:, 2:3], wait_edges=[e_exp])
        b.op("vector", "tensor_scalar", msk, lt_t, sm[:, 0:1], None, ALU.is_ge)
        b.op("vector", "scalar_tensor_tensor", msk2, msk, -1.0e9, lt_t,
             ALU.mult, ALU.add)
        b.op("vector", "reduce_max", sm[:, 4:5], msk2, AX.X)
        b.op("vector", "tensor_scalar", msk, lt_t, sm[:, 4:5], None, ALU.is_ge)
        b.op("vector", "tensor_mul", expv, expv, msk)
        e = b.op("vector", "tensor_scalar", cwt[:, ti * 16:(ti + 1) * 16], expv,
                 sm[:, 3:4], None, ALU.mult, inc=True)
        e_cwt.append(e)
    e_cwf = [e_cwt[-1]] if e_cwt else []

    e_up_last = None
    e_dn_last = None
    e_mh_last = None
    for e_i in range(n_experts if do_experts else 0):
        we1 = [e for e in (e_up_last,) if e]
        e_w1 = wload(b, w1buf, W["l0_w1"][e_i * 1024:(e_i + 1) * 1024, :], wait_edges=we1)
        e_w2 = wload(b, w2buf, W["l0_w2"][e_i * 512:(e_i + 1) * 512, :],
                     wait_edges=[e for e in (e_dn_last,) if e])
        e_cwrep = None
        for ti in range(16):
            bk = b.rr()
            lhs = cwt[:, ti * 16 + e_i: ti * 16 + e_i + 1].broadcast_to((128, 128))
            b.mm_group(bk, [(b.pbank[bk][:, 0:128], lhs, idf)],
                       wait_edges=e_cwf)
            e_cwrep = b.bank_read(bk, "scalar", "copy",
                                  cwrep[:, ti * 128:(ti + 1) * 128],
                                  b.pbank[bk][:, 0:128])
        for mp in range(4):
            for cch, sl in enumerate(CH512):
                bkU = b.rr(); bkG = b.rr()
                e_up_last = b.mm_group(bkU, [
                    (b.pbank[bkU][:, :], w1buf[:, k * 1024 + mp * 128:k * 1024 + mp * 128 + 128],
                     xn[k][:, sl]) for k in range(DT)], wait_edges=[e_w1])
                e_up_last = b.mm_group(bkG, [
                    (b.pbank[bkG][:, :], w1buf[:, k * 1024 + 512 + mp * 128:k * 1024 + 512 + mp * 128 + 128],
                     xn[k][:, sl]) for k in range(DT)], wait_edges=[e_w1])
                e_gs = b.bank_read(bkG, "scalar", "activation", gs, b.pbank[bkG][:, :],
                                   AF.Silu)
                b.bank_read(bkU, "vector", "scalar_tensor_tensor", hbuf,
                            b.pbank[bkU][:, :], 1.0, gs, ALU.mult, ALU.mult,
                            wait_edges=[e_gs, e_cwrep] +
                                       ([e_dn_last] if e_dn_last else []))
                e_mh_last = b.op("vector", "tensor_mul", mh[:, mp * T + sl.start: mp * T + sl.stop],
                                 hbuf, cwrep[:, sl], inc=True)
        for do in range(8):
            for cch, sl in enumerate(CH512):
                bk = b.rr()
                e_dn_last = b.mm_group(bk, [
                    (b.pbank[bk][:, :], w2buf[:, hk * 1024 + do * 128:hk * 1024 + do * 128 + 128],
                     mh[:, hk * T + sl.start: hk * T + sl.stop]) for hk in range(4)],
                    wait_edges=[e_w2, e_mh_last])
                b.bank_read(bk, "vector", "scalar_tensor_tensor", sg[do][:, sl],
                            b.pbank[bk][:, :], 1.0, sg[do][:, sl], ALU.mult, ALU.add)
    for half in range(2 if do_shared else 0):
        e_hs_last = None
        for mpl in range(4):
            mp = half * 4 + mpl
            e_wu = wload(b, wsu, W["l0_ws1"][:, mp * 128:(mp + 1) * 128],
                         wait_edges=[e for e in (e_up_last,) if e])
            e_wg2 = wload(b, wsg, W["l0_ws1"][:, 1024 + mp * 128:1024 + (mp + 1) * 128],
                          wait_edges=[e for e in (e_up_last,) if e])
            for cch, sl in enumerate(CH512):
                bkU = b.rr(); bkG = b.rr()
                b.mm_group(bkU, [(b.pbank[bkU][:, :], wsu[:, k * 128:(k + 1) * 128],
                                  xn[k][:, sl]) for k in range(DT)], wait_edges=[e_wu])
                e_up_last = b.mm_group(bkG, [(b.pbank[bkG][:, :], wsg[:, k * 128:(k + 1) * 128],
                                              xn[k][:, sl]) for k in range(DT)],
                                       wait_edges=[e_wg2])
                e_gs = b.bank_read(bkG, "scalar", "activation", gs,
                                   b.pbank[bkG][:, :], AF.Silu)
                e_hs_last = b.bank_read(bkU, "vector", "tensor_copy", hbuf,
                                        b.pbank[bkU][:, :])
                e_hs_last = b.op("vector", "tensor_mul",
                                 mh[:, mpl * T + sl.start: mpl * T + sl.stop],
                                 hbuf, gs, wait_edges=[e_gs], inc=True)
        e_ws2 = wload(b, w2buf, W["l0_ws2"][half * 512:(half + 1) * 512, :],
                      wait_edges=[e for e in (e_dn_last,) if e])
        for do in range(8):
            for cch, sl in enumerate(CH512):
                bk = b.rr()
                e_dn_last = b.mm_group(bk, [
                    (b.pbank[bk][:, :], w2buf[:, hk * 1024 + do * 128:hk * 1024 + do * 128 + 128],
                     mh[:, hk * T + sl.start: hk * T + sl.stop]) for hk in range(4)],
                    wait_edges=[e_ws2, e_hs_last])
                b.bank_read(bk, "vector", "scalar_tensor_tensor", sg[do][:, sl],
                            b.pbank[bk][:, :], 1.0, sg[do][:, sl], ALU.mult, ALU.add)
    e_z = b.fence(engines=("vector",))
    edges = [b.op("scalar", "activation", sg[d][:, :], sg[d][:, :], AF.Silu,
                  wait_edges=e_z, inc=True) for d in range(DT)]
    return edges



SWAP16 = list(range(16, 32)) + list(range(0, 16))


def emit_l0_attn(b, g, W, xn, xT, sg, pl, in_edges):
    o_all = pl.bf(8 * T)
    mark = pl.p
    qk = pl.bf(2 * T)
    v_tok = pl.bf(16 * 128)
    P = pl.bf(T)
    PT = pl.bf(16 * 128)
    qraw = pl.bf(512)
    shf = pl.bf(512)
    t1 = pl.bf(512)
    t2 = pl.bf(512)
    sums = pl.f32(16)
    diag = pl.bf(128)
    wqb = pl.bf(8 * 128)
    wkb = pl.bf(8 * 128)
    wvb = pl.bf(8 * 128)
    cosT = pl.bf(T)
    sinT = pl.bf(T)
    masks = pl.bf(4 * 512)
    e_ct = [b.dma(cosT, W["ct_l0_cos"], queue="sync"),
            b.dma(sinT, W["ct_l0_sin"], queue="sync"),
            b.dma(masks, W["ct_l0_mask"], queue="sync")]
    idb = g["ident_bf"]
    e_q_mm = None
    e_v_evac = None
    e_pv_last = None
    e_tr_last = None
    e_sc_last = None
    e_sum = None
    e_qread = None
    for pair in range(8):
        wed = [e for e in (e_q_mm,) if e]
        e_wq = wload(b, wqb, W["l0_wq"][:, pair * 128:(pair + 1) * 128], wait_edges=wed)
        e_wk = wload(b, wkb, W["l0_wkv"][:, pair * 128:(pair + 1) * 128], wait_edges=wed)
        e_wv = wload(b, wvb, W["l0_wkv"][:, 1024 + pair * 128:1024 + (pair + 1) * 128],
                     wait_edges=wed)
        for ti in range(16):
            bk = b.rr(7)
            b.mm_group(bk, [(b.pbank[bk][:, 0:128], xn[k][:, ti * 128:(ti + 1) * 128],
                             wvb[:, k * 128:(k + 1) * 128]) for k in range(DT)],
                       wait_edges=[e_wv] + list(in_edges) +
                                  ([e_pv_last] if (e_pv_last and ti == 0) else []))
            e_v_evac = b.bank_read(bk, "vector", "tensor_copy",
                                   v_tok[:, ti * 128:(ti + 1) * 128], b.pbank[bk][:, 0:128])
        for wbuf, e_w, dst_off in ((wqb, e_wq, 0), (wkb, e_wk, T)):
            for cch, sl in enumerate(CH512):
                bk = b.rr(7)
                e_q_mm = b.mm_group(bk, [(b.pbank[bk][:, :], wbuf[:, k * 128:(k + 1) * 128],
                                          xn[k][:, sl]) for k in range(DT)],
                                    wait_edges=[e_w])
                e_r = b.bank_read(bk, "scalar", "copy", qraw, b.pbank[bk][:, :],
                                  wait_edges=[e_qread] if e_qread else [])
                b.op("vector", "stream_shuffle", shf, qraw, SWAP16, wait_edges=[e_r])
                e_qread = b.op("vector", "tensor_mul", t1, qraw, cosT[:, sl], inc=True)
                b.op("vector", "tensor_mul", t2, shf, sinT[:, sl])
                dsl = slice(dst_off + sl.start, dst_off + sl.stop)
                b.op("vector", "tensor_add", qk[:, dsl], t1, t2, inc=True,
                     wait_edges=[e_sc_last] if e_sc_last else [])
        e_rope = b.fence(engines=("vector",))
        if pair == 0 and globals().get("DBG_L1"):
            QKD = b.nc.dram_tensor("qk_dbg", [128, 2 * T], BF16,
                                   kind="ExternalOutput").ap()
            b.dma(QKD, qk, queue="sync", wait_edges=e_rope)
        for hh in range(2):
            h = 2 * pair + hh
            r0 = hh * 64
            qrows = qk[r0:r0 + 64, 0:T]
            krows = qk[r0:r0 + 64, T:2 * T]
            pvbk = 7
            for ti in range(16):
                cdiag = (ti * 128) // 512
                e_exp = None
                for c in range(cdiag + 1):
                    bk = b.rr(7)
                    mms = [(b.pbank[bk][:, :], qrows[:, ti * 128:(ti + 1) * 128],
                            krows[:, c * 512:(c + 1) * 512])]
                    if c == cdiag:
                        v = ti % 4
                        mms.append((b.pbank[bk][:, :], idb,
                                    masks[:, v * 512:(v + 1) * 512]))
                    e_sc_last = b.mm_group(bk, mms,
                                           wait_edges=list(e_rope) if (ti == 0 and c == 0) else ())
                    ew = [e for e in (e_tr_last, e_sum) if e]
                    e_exp = b.bank_read(bk, "scalar", "activation",
                                        P[:, c * 512:(c + 1) * 512], b.pbank[bk][:, :],
                                        AF.Exp, bias=-20.0, scale=0.125,
                                        accum_out=sums[:, c:c + 1], wait_edges=ew)
                e_sum = b.op("vector", "reduce_sum", sums[:, 8:9],
                             sums[:, 0:cdiag + 1], AX.X, wait_edges=[e_exp])
                b.op("vector", "reciprocal", sums[:, 9:10], sums[:, 8:9])
                e_diag = b.op("vector", "tensor_scalar", diag, idb, sums[:, 9:10],
                              None, ALU.mult, inc=True,
                              wait_edges=[e_tr_last] if e_tr_last else [])
                ntj = ti + 1
                for grp in range(0, ntj, 4):
                    bk = b.rr(7)
                    nj = min(4, ntj - grp)
                    for j in range(nj):
                        b.mm_group(bk, [(b.pbank[bk][:, j * 128:(j + 1) * 128],
                                         P[:, (grp + j) * 128:(grp + j + 1) * 128], diag)],
                                   wait_edges=[e_diag] if j == 0 else ())
                    e_tr_last = b.bank_read(bk, "vector", "tensor_copy",
                                            PT[:, grp * 128:grp * 128 + nj * 128],
                                            b.pbank[bk][:, 0:nj * 128])
                b.mm_group(pvbk, [(b.pbank[pvbk][0:64, (ti % 4) * 128:(ti % 4) * 128 + 128],
                                   v_tok[:, tjt * 128 + r0:tjt * 128 + r0 + 64],
                                   PT[:, tjt * 128:(tjt + 1) * 128])
                                  for tjt in range(ntj)],
                           wait_edges=[e_tr_last, e_v_evac])
                if ti % 4 == 3:
                    blk = ti // 4
                    e_pv_last = b.bank_read(
                        pvbk, "vector", "tensor_copy",
                        o_all[r0:r0 + 64, pair * T + blk * 512: pair * T + (blk + 1) * 512],
                        b.pbank[pvbk][0:64, :])
    pl.p = mark
    e_all = b.fence()
    wob = pl.bf(8 * 1024)
    upd = pl.f32(512)
    e_wo = wload(b, wob, W["l0_wout"], wait_edges=e_all)
    e_upd = None
    for do in range(DT):
        for cch, sl in enumerate(CH512):
            bk = b.rr(7)
            b.mm_group(bk, [(b.pbank[bk][:, :], wob[:, kp * 1024 + do * 128:kp * 1024 + do * 128 + 128],
                             o_all[:, kp * T + sl.start: kp * T + sl.stop])
                            for kp in range(8)], wait_edges=[e_wo])
            b.bank_read(bk, "vector", "tensor_mul", upd, b.pbank[bk][:, :],
                        sg[do][:, sl], wait_edges=list(in_edges))
            e_upd = b.op("vector", "tensor_add", xT[do][:, sl], xT[do][:, sl], upd,
                         inc=True)
    return [e_upd]



def build_program(layers=(0,), dump=()):
    nc = bass.Bass("TRN2", target_bir_lowering=False, debug=False)
    with contextlib.ExitStack() as es:
        b = MK(nc, es)
        X = nc.dram_tensor("x_in", [D, T], F32, kind="ExternalInput").ap()
        OUT = nc.dram_tensor("x_out", [D, T], F32, kind="ExternalOutput").ap()
        W = {}
        def win(name, shape):
            W[name] = nc.dram_tensor(name, list(shape), F32, kind="ExternalInput").ap()
        if 0 in layers:
            win("l0_wq", (D, D)); win("l0_wkv", (D, 2 * D)); win("l0_wout", (D, D))
            win("l0_wg_t", (D, 16)); win("l0_w1", (16 * 1024, 1024))
            win("l0_w2", (16 * 512, 1024)); win("l0_ws1", (D, 2048)); win("l0_ws2", (D, D))
        if 3 in layers:
            win("l3_win_t", (D, 4096)); win("l3_wxp_t", (2048, 96))
            win("l3_wdt_t", (64, 2048)); win("l3_wout_t", (2048, D))
            win("l3_alog", (2048, 16))
            win("l3_lnw", (128, 8)); win("l3_lnb", (128, 8))
            win("l3_convw", (128, 64)); win("l3_convb", (128, 16))
            win("l3_bdt", (128, 16)); win("l3_dsk", (128, 16))
            win("l3f_u1", (D, 256)); win("l3f_v1", (256, 4096))
            win("l3f_u2", (4096, 256)); win("l3f_v2", (256, D))
            win("l3f_b1", (128, 32)); win("l3f_b2", (128, 8))
        if 2 in layers:
            win("l2_wq_t", (D, D)); win("l2_wk_t", (D, D)); win("l2_wv_t", (D, D))
            win("l2_wo_t", (D, D)); win("l2_fm_t", (128, 512))
            win("l2_fmb", (128, 4)); win("l2_rms", (128, 128))
            win("l2f_wqkv", (2 * D, 1536)); win("l2f_wo", (2 * 512, D))
            win("l2f_w1a", (2 * D, HID_PAD)); win("l2f_w1g", (2 * D, HID_PAD))
            win("l2f_w2", (2 * HID_PAD, D))
            win("l2f_qsks", (128, 4)); win("l2f_lng", (128, 16))
            win("l2f_lnb", (128, 16)); win("l2f_b1a", (128, 44))
            win("l2f_b1g", (128, 44)); win("l2f_b2", (128, 16))
        if 1 in layers:
            win("l1_wq_t", (D, D)); win("l1_wk_t", (D, D)); win("l1_wv_t", (D, D))
            win("l1_wo_t", (D, D)); win("l1_sqk", (128, 8))
            win("l1_wg_t", (D, 4096)); win("l1_wu_t", (D, 4096)); win("l1_wd_t", (4096, D))
            for qn, shp in (("scr_l1_qg", (D, 4096)), ("scr_l1_qu", (D, 4096)),
                            ("scr_l1_qd", (4096, D))):
                W[qn] = nc.dram_tensor(qn, list(shp), BF16, kind="Internal").ap()
        carr = host_consts()
        BIGC = ("l0_cos", "l0_sin", "l1_cos", "l1_sin", "l0_mask", "sel16",
                "l2_band", "l2_cosq", "l2_sinq", "l2_cosk", "l2_sink")
        g = {}
        e_c = []
        for name, arr in carr.items():
            ap = b.const_input(name, arr)
            if name in BIGC:
                W["ct_" + name] = ap
                continue
            dt = {np.dtype(np.float32): F32, np.dtype(bfdt): BF16}[arr.dtype]
            t = b.sb("c_" + name, list(arr.shape), dt)
            g[name] = t.ap() if hasattr(t, "ap") else t
            e_c.append(b.dma(g[name], ap, queue="sync"))
        for i, v in enumerate(FCONSTS):
            nc.const_aps.register(F32, float(v), g["fpool"][:, i:i + 1])
        if 3 in layers:
            for nm, cols in (("l3_lnw", 8), ("l3_lnb", 8), ("l3_convw", 64),
                             ("l3_convb", 16), ("l3_bdt", 16), ("l3_dsk", 16),
                             ("l3f_b1", 32), ("l3f_b2", 8)):
                tt = b.sb("c_" + nm, [128, cols], F32)
                ap_ = tt.ap() if hasattr(tt, "ap") else tt
                b.dma(ap_, W[nm], queue="sync")
                W[nm + "_sb"] = ap_
        if 2 in layers:
            for nm, cols in (("l2_fmb", 4), ("l2_rms", 128), ("l2f_qsks", 4),
                             ("l2f_lng", 16), ("l2f_lnb", 16), ("l2f_b1a", 44),
                             ("l2f_b1g", 44), ("l2f_b2", 16)):
                tt = b.sb("c_" + nm, [128, cols], F32)
                ap_ = tt.ap() if hasattr(tt, "ap") else tt
                b.dma(ap_, W[nm], queue="sync")
                W[nm + "_sb"] = ap_
        if 1 in layers:
            sqk_sb = b.sb("c_l1_sqk", [128, 8], F32)
            b.dma(sqk_sb.ap() if hasattr(sqk_sb, "ap") else sqk_sb, W["l1_sqk"],
                  queue="sync")
            W["l1_sqk_sb"] = sqk_sb.ap() if hasattr(sqk_sb, "ap") else sqk_sb
        xTt = [b.sb(f"xt{d}", [128, T], F32) for d in range(DT)]
        xT = [t.ap() for t in xTt]
        arena = b.sb("arena", [128, ARENA_COLS], BF16).ap()
        e_x = [b.dma(xT[d], X[d * 128:(d + 1) * 128, :], queue="sync")
               for d in range(DT)]
        xn = [arena[:, d * T:(d + 1) * T] for d in range(DT)]
        sg = [arena[:, (DT + d) * T:(DT + d + 1) * T] for d in range(DT)]
        edges = b.phase_barrier(extra_edges=e_x + e_c)
        dbg_out = {}
        for li in layers:
            base = 2 * DT * T
            if li == 0:
                b.mark("l0.ln")
                e_xn = emit_layernorm(b, g, xT, xn, Plan(arena, base), in_edges=edges)
                b.phase_barrier()
                if any(x in STAGES for x in ("ffn", "router", "experts", "shared")):
                    b.mark("l0.ffn")
                    e_sg = emit_l0_ffn(b, g, W, xn, sg, Plan(arena, base), e_xn)
                    b.phase_barrier()
                else:
                    e_sg = []
                if "attn" in STAGES:
                    b.mark("l0.attn")
                    edges = emit_l0_attn(b, g, W, xn, xT, sg, Plan(arena, base),
                                         e_xn + e_sg)
                edges = edges + b.phase_barrier()
            elif li == 3:
                b.mark("l3")
                edges = emit_l3(b, g, W, xT, Plan(arena, 0), edges)
                edges = edges + b.phase_barrier()
            elif li == 2:
                b.mark("l2.ln")
                e_xn = emit_layernorm(b, g, xT, xn, Plan(arena, base), in_edges=edges)
                b.phase_barrier()
                sgr = arena[:, DT * T:2 * DT * T]
                b.mark("l2.ffn")
                e_sg = emit_l2_ffn(b, g, W, xn, sgr, sg, Plan(arena, base), e_xn)
                b.phase_barrier()
                b.mark("l2.attn")
                edges = emit_l2_attn(b, g, W, xn, xT, sg, Plan(arena, base),
                                     e_xn + e_sg)
                edges = edges + b.phase_barrier()
            elif li == 1:
                b.mark("l1.ln")
                e_xn = emit_layernorm(b, g, xT, xn, Plan(arena, base), in_edges=edges)
                b.phase_barrier()
                b.mark("l1.ffn")
                e_sg = emit_l1_ffn(b, g, W, xn, sg, Plan(arena, base), e_xn)
                b.phase_barrier()
                b.mark("l1.attn")
                edges = emit_l1_attn(b, g, W, xn, xT, sg, Plan(arena, base),
                                     e_xn + e_sg)
                edges = edges + b.phase_barrier()
            else:
                raise NotImplementedError(li)
            b.mark(f"l{li}.end")
            if "xn" in dump and li == max(layers):
                XN = nc.dram_tensor("xn_out", [D, T], BF16, kind="ExternalOutput").ap()
                for d in range(DT):
                    b.dma(XN[d * 128:(d + 1) * 128, :], xn[d], queue="sync",
                          wait_edges=edges)
            if "sg" in dump and li == max(layers):
                SG = nc.dram_tensor("sg_out", [D, T], BF16, kind="ExternalOutput").ap()
                for d in range(DT):
                    b.dma(SG[d * 128:(d + 1) * 128, :], sg[d], queue="sync",
                          wait_edges=edges)
        e_out = [b.dma(OUT[d * 128:(d + 1) * 128, :], xT[d], queue="sync",
                       wait_edges=edges) for d in range(DT)]
        b.op("sync", "nop", wait_edges=e_out)
        b.finalize()
    global LAST_MARKS
    LAST_MARKS = list(getattr(b, "marks", []))
    return nc, b.consts


def host_weights_l1(inputs):
    w = {}
    for k, nm in (("L1_attn_Wq", "l1_wq_t"), ("L1_attn_Wk", "l1_wk_t"),
                  ("L1_attn_Wv", "l1_wv_t"), ("L1_attn_Wo", "l1_wo_t")):
        w[nm] = np.ascontiguousarray(np.asarray(inputs[k]).T, np.float32)
    sq = (np.asarray(inputs["L1_attn_s_qk"], np.float32) * 32.0).reshape(16, 64)
    col = np.zeros((128, 8), np.float32)
    for p in range(8):
        col[0:64, p] = sq[2 * p]
        col[64:128, p] = sq[2 * p + 1]
    w["l1_sqk"] = col
    w["l1_wg_t"] = np.ascontiguousarray(np.asarray(inputs["L1_ffn_w_gate"]).T, np.float32)
    w["l1_wu_t"] = np.ascontiguousarray(np.asarray(inputs["L1_ffn_w_up"]).T, np.float32)
    w["l1_wd_t"] = np.ascontiguousarray(np.asarray(inputs["L1_ffn_w_down"]).T, np.float32)
    return w


def host_weights_l2(inputs):
    w = {}
    for k, nm in (("L2_attn_Wq", "l2_wq_t"), ("L2_attn_Wk", "l2_wk_t"),
                  ("L2_attn_Wv", "l2_wv_t"), ("L2_attn_Wo", "l2_wo_t")):
        w[nm] = np.ascontiguousarray(np.asarray(inputs[k]).T, np.float32)
    w["l2_fm_t"] = np.ascontiguousarray(np.concatenate(
        [np.asarray(inputs[k]).T for k in ("L2_attn_fmq_w1", "L2_attn_fmq_w2",
                                           "L2_attn_fmk_w1", "L2_attn_fmk_w2")],
        axis=1), np.float32)
    fmb = np.stack([np.asarray(inputs[k]) for k in
                    ("L2_attn_fmq_b1", "L2_attn_fmq_b2",
                     "L2_attn_fmk_b1", "L2_attn_fmk_b2")], axis=1)
    w["l2_fmb"] = np.ascontiguousarray(fmb, np.float32)
    w["l2_rms"] = np.ascontiguousarray(
        np.tile(np.asarray(inputs["L2_attn_rms_w"], np.float32)[None, :], (128, 1)))
    wqkv = np.asarray(inputs["L2_ffn_wqkv"], np.float32)
    w["l2f_wqkv"] = np.ascontiguousarray(wqkv.reshape(2 * 1024, 1536))
    wo = np.asarray(inputs["L2_ffn_wo"], np.float32)
    w["l2f_wo"] = np.ascontiguousarray(wo.reshape(2 * 512, 1024))
    w1 = np.asarray(inputs["L2_ffn_w1"], np.float32)
    HID = 2730
    w1a = np.zeros((2, 1024, HID_PAD), np.float32)
    w1g = np.zeros((2, 1024, HID_PAD), np.float32)
    w1a[:, :, :HID] = w1[:, :, :HID]
    w1g[:, :, :HID] = w1[:, :, HID:]
    w["l2f_w1a"] = np.ascontiguousarray(w1a.reshape(2 * 1024, HID_PAD))
    w["l2f_w1g"] = np.ascontiguousarray(w1g.reshape(2 * 1024, HID_PAD))
    w2 = np.asarray(inputs["L2_ffn_w2"], np.float32)
    w2p = np.zeros((2, HID_PAD, 1024), np.float32)
    w2p[:, :HID] = w2
    w["l2f_w2"] = np.ascontiguousarray(w2p.reshape(2 * HID_PAD, 1024))
    qs = np.asarray(inputs["L2_ffn_q_scale"], np.float32)
    ks = np.asarray(inputs["L2_ffn_k_scale"], np.float32)
    qsks = np.zeros((128, 4), np.float32)
    for l in range(2):
        qsks[:, 2 * l] = np.tile(qs[l], 2)
        qsks[:, 2 * l + 1] = np.tile(ks[l], 2)
    w["l2f_qsks"] = qsks
    lng = np.asarray(inputs["L2_ffn_ln_g"], np.float32).reshape(2, 8, 128)
    lnb = np.asarray(inputs["L2_ffn_ln_b"], np.float32).reshape(2, 8, 128)
    w["l2f_lng"] = np.ascontiguousarray(
        lng.transpose(2, 0, 1).reshape(128, 16)[:, [l * 8 + d for l in range(2) for d in range(8)]])
    w["l2f_lnb"] = np.ascontiguousarray(
        lnb.transpose(2, 0, 1).reshape(128, 16)[:, [l * 8 + d for l in range(2) for d in range(8)]])
    b1 = np.asarray(inputs["L2_ffn_b1"], np.float32)
    b1a = np.zeros((2, HID_PAD), np.float32)
    b1g = np.zeros((2, HID_PAD), np.float32)
    b1a[:, :HID] = b1[:, :HID]
    b1g[:, :HID] = b1[:, HID:]
    w["l2f_b1a"] = np.ascontiguousarray(
        b1a.reshape(2, 22, 128).transpose(2, 0, 1).reshape(128, 44))
    w["l2f_b1g"] = np.ascontiguousarray(
        b1g.reshape(2, 22, 128).transpose(2, 0, 1).reshape(128, 44))
    b2 = np.asarray(inputs["L2_ffn_b2"], np.float32)
    w["l2f_b2"] = np.ascontiguousarray(
        b2.reshape(2, 8, 128).transpose(2, 0, 1).reshape(128, 16))
    return w


def host_weights_l3(inputs):
    w = {}
    w["l3_win_t"] = np.ascontiguousarray(np.asarray(inputs["L3_attn_W_in"]).T, np.float32)
    w["l3_wxp_t"] = np.ascontiguousarray(np.asarray(inputs["L3_attn_W_xproj"]).T, np.float32)
    w["l3_wdt_t"] = np.ascontiguousarray(np.asarray(inputs["L3_attn_W_dt"]).T, np.float32)
    w["l3_wout_t"] = np.ascontiguousarray(np.asarray(inputs["L3_attn_W_out"]).T, np.float32)
    w["l3_alog"] = np.ascontiguousarray(np.asarray(inputs["L3_attn_A_log"], np.float32))
    def colsh(v, nt):
        return np.ascontiguousarray(np.asarray(v, np.float32).reshape(nt, 128).T)
    w["l3_lnw"] = colsh(inputs["L3_attn_ln_w"], 8)
    w["l3_lnb"] = colsh(inputs["L3_attn_ln_b"], 8)
    cw = np.asarray(inputs["L3_attn_conv_w"], np.float32).reshape(16, 128, 4)
    w["l3_convw"] = np.ascontiguousarray(cw.transpose(1, 0, 2).reshape(128, 64))
    w["l3_convb"] = colsh(inputs["L3_attn_conv_b"], 16)
    w["l3_bdt"] = colsh(inputs["L3_attn_b_dt"], 16)
    w["l3_dsk"] = colsh(inputs["L3_attn_Dskip"], 16)
    w["l3f_u1"] = np.ascontiguousarray(np.asarray(inputs["L3_ffn_U1"], np.float32))
    w["l3f_v1"] = np.ascontiguousarray(np.asarray(inputs["L3_ffn_V1"], np.float32))
    w["l3f_u2"] = np.ascontiguousarray(np.asarray(inputs["L3_ffn_U2"], np.float32))
    w["l3f_v2"] = np.ascontiguousarray(np.asarray(inputs["L3_ffn_V2"], np.float32))
    w["l3f_b1"] = colsh(inputs["L3_ffn_b1"], 32)
    w["l3f_b2"] = colsh(inputs["L3_ffn_b2"], 8)
    return w


def host_weights_l0(inputs):
    w = {}
    w["l0_wq"] = np.ascontiguousarray(inputs["L0_attn_w_q"], np.float32)
    w["l0_wkv"] = np.ascontiguousarray(inputs["L0_attn_w_kv"], np.float32)
    w["l0_wout"] = np.ascontiguousarray(inputs["L0_attn_w_out"], np.float32)
    w["l0_wg_t"] = np.ascontiguousarray(np.asarray(inputs["L0_ffn_Wg"]).T, np.float32)
    w["l0_w1"] = np.ascontiguousarray(np.asarray(inputs["L0_ffn_W1"]).reshape(16 * 1024, 1024), np.float32)
    w["l0_w2"] = np.ascontiguousarray(np.asarray(inputs["L0_ffn_W2"]).reshape(16 * 512, 1024), np.float32)
    w["l0_ws1"] = np.ascontiguousarray(inputs["L0_ffn_Ws1"], np.float32)
    w["l0_ws2"] = np.ascontiguousarray(inputs["L0_ffn_Ws2"], np.float32)
    return w



_CACHE = {}


def kernel(**inputs):
    import numpy as _np
    layers = tuple(LAYERS)
    if layers not in _CACHE:
        _CACHE[layers] = build_program(layers=layers)
    nc, consts = _CACHE[layers]
    w = {}
    if 0 in layers:
        w.update(host_weights_l0(inputs))
    if 1 in layers:
        w.update(host_weights_l1(inputs))
    if 2 in layers:
        w.update(host_weights_l2(inputs))
    if 3 in layers:
        w.update(host_weights_l3(inputs))
    x = _np.asarray(inputs["L0_attn_x"], _np.float32)
    in_maps = []
    for c in range(8):
        m = dict(consts)
        m.update(w)
        m["x_in"] = _np.ascontiguousarray(x[c].T)
        in_maps.append(m)
    res = run_bass_kernel_spmd(nc, in_maps, core_ids=list(range(8)))
    out = _np.stack([_np.asarray(r["x_out"]).T for r in res.results], axis=0)
    return out



SWAPPAIR = [j ^ 1 for j in range(32)]


def emit_l1_attn(b, g, W, xn, xT, sg, pl, in_edges):
    o_all = pl.bf(8 * T)
    mark = pl.p
    qk = pl.bf(2 * T)
    v_tok = pl.bf(16 * 128)
    P = pl.bf(T)
    PT = pl.bf(16 * 128)
    qraw = pl.bf(512)
    shf = pl.bf(512)
    t1 = pl.bf(512)
    t2 = qraw
    sq = pl.f32(512)
    n2 = P
    rn_rep = pl.bf(512)
    sums = pl.f32(16)
    diag = pl.bf(128)
    wqb = pl.bf(8 * 128)
    wkb = pl.bf(8 * 128)
    wvb = pl.bf(8 * 128)
    cosT = pl.bf(T)
    sinT = pl.bf(T)
    e_ct = [b.dma(cosT, W["ct_l1_cos"], queue="sync"),
            b.dma(sinT, W["ct_l1_sin"], queue="sync")]
    idb = g["ident_bf"]
    e64 = g["e64"]
    e64t = g["e64_t"]
    e_q_mm = None
    e_pv_last = None
    e_tr_last = None
    e_sc_last = None
    e_sum = None
    e_qread = None
    sqk = W["l1_sqk_sb"]
    for pair in range(8):
        wed = [e for e in (e_q_mm,) if e]
        e_wq = wload(b, wqb, W["l1_wq_t"][:, pair * 128:(pair + 1) * 128], wait_edges=wed)
        e_wk = wload(b, wkb, W["l1_wk_t"][:, pair * 128:(pair + 1) * 128], wait_edges=wed)
        e_wv = wload(b, wvb, W["l1_wv_t"][:, pair * 128:(pair + 1) * 128], wait_edges=wed)
        for ti in range(16):
            bk = b.rr(7)
            b.mm_group(bk, [(b.pbank[bk][:, 0:128], xn[k][:, ti * 128:(ti + 1) * 128],
                             wvb[:, k * 128:(k + 1) * 128]) for k in range(DT)],
                       wait_edges=[e_wv] + list(in_edges))
            b.bank_read(bk, "vector", "tensor_copy",
                        v_tok[:, ti * 128:(ti + 1) * 128], b.pbank[bk][:, 0:128])
        for wbuf, e_w, dst_off in ((wqb, e_wq, 0), (wkb, e_wk, T)):
            for cch, sl in enumerate(CH512):
                bk = b.rr(7)
                e_q_mm = b.mm_group(bk, [(b.pbank[bk][:, :], wbuf[:, k * 128:(k + 1) * 128],
                                          xn[k][:, sl]) for k in range(DT)],
                                    wait_edges=[e_w])
                e_r = b.bank_read(bk, "scalar", "copy", qraw, b.pbank[bk][:, :],
                                  wait_edges=[e_qread] if e_qread else [])
                b.op("vector", "stream_shuffle", shf, qraw, SWAPPAIR, wait_edges=[e_r])
                e_qread = b.op("vector", "tensor_mul", t1, qraw, cosT[:, sl], inc=True)
                b.op("vector", "tensor_mul", t2, shf, sinT[:, sl])
                dsl = slice(dst_off + sl.start, dst_off + sl.stop)
                b.op("vector", "tensor_add", qk[:, dsl], t1, t2, inc=True,
                     wait_edges=[e_sc_last] if e_sc_last else [])
                e_sq = b.op("scalar", "activation", sq, qk[:, dsl], AF.Square, inc=True)
                bk2 = b.rr(7)
                b.mm_group(bk2, [(b.pbank[bk2][0:2, :], e64t, sq)], wait_edges=[e_sq])
                b.bank_read(bk2, "vector", "tensor_copy", n2[0:2, sl],
                            b.pbank[bk2][0:2, :])
            e_n2 = b.op("scalar", "activation", n2[0:2, 0:T], n2[0:2, 0:T], AF.Sqrt,
                        inc=True)
            e_rn = b.op("vector", "tensor_scalar", n2[0:2, 0:T], n2[0:2, 0:T], 1e-12,
                        None, ALU.max, wait_edges=[e_n2], inc=True)
            for cch, sl in enumerate(CH512):
                e_rc = b.op("vector", "reciprocal", sq[0:2, :], n2[0:2, sl],
                            wait_edges=[e_rn], inc=True)
                bk2 = b.rr(7)
                b.mm_group(bk2, [(b.pbank[bk2][:, :], e64, sq[0:2, :])],
                           wait_edges=[e_rc])
                e_rep = b.bank_read(bk2, "vector", "tensor_copy", rn_rep,
                                    b.pbank[bk2][:, :])
                dsl = slice(dst_off + sl.start, dst_off + sl.stop)
                e_sc_last = b.op("vector", "scalar_tensor_tensor", qk[:, dsl],
                                 qk[:, dsl], sqk[:, pair:pair + 1], rn_rep,
                                 ALU.mult, ALU.mult, wait_edges=[e_rep], inc=True)
        e_rope = b.fence(engines=("vector",))
        if pair == 0 and globals().get("DBG_L1"):
            QKD = b.nc.dram_tensor("qk_dbg", [128, 2 * T], BF16,
                                   kind="ExternalOutput").ap()
            b.dma(QKD, qk, queue="sync", wait_edges=e_rope)
        for hh in range(2):
            h = 2 * pair + hh
            r0 = hh * 64
            qrows = qk[r0:r0 + 64, 0:T]
            krows = qk[r0:r0 + 64, T:2 * T]
            pvbk = 7
            for ti in range(16):
                e_exp = None
                for c in range(4):
                    bk = c
                    b.mm_group(bk, [(b.pbank[bk][:, :],
                                     qrows[:, ti * 128:(ti + 1) * 128],
                                     krows[:, c * 512:(c + 1) * 512])],
                               wait_edges=(list(e_rope) if (ti == 0 and c == 0) else ()))
                e_mx = b.op("vector", "reduce_max", sums[:, 10:11],
                            b.pbig[:, 0:2048], AX.X,
                            wait_edges=[b.last_op["tensor"]])
                e_nm = b.op("vector", "tensor_scalar", sums[:, 11:12], sums[:, 10:11],
                            -8.0, None, ALU.mult, inc=True)
                for c in range(4):
                    ew = [e_nm] + [e for e in (e_tr_last, e_sum) if e]
                    e_exp = b.bank_read(c, "scalar", "activation",
                                        P[:, c * 512:(c + 1) * 512], b.pbank[c][:, :],
                                        AF.Exp, bias=sums[:, 11:12], scale=8.0,
                                        accum_out=sums[:, c:c + 1], wait_edges=ew)
                e_sum = b.op("vector", "reduce_sum", sums[:, 8:9], sums[:, 0:4], AX.X,
                             wait_edges=[e_exp])
                b.op("vector", "reciprocal", sums[:, 9:10], sums[:, 8:9])
                e_diag = b.op("vector", "tensor_scalar", diag, idb, sums[:, 9:10],
                              None, ALU.mult, inc=True,
                              wait_edges=[e_tr_last] if e_tr_last else [])
                for grp in range(0, 16, 4):
                    bk = 4 + (grp // 4) % 2
                    for j in range(4):
                        b.mm_group(bk, [(b.pbank[bk][:, j * 128:(j + 1) * 128],
                                         P[:, (grp + j) * 128:(grp + j + 1) * 128],
                                         diag)],
                                   wait_edges=[e_diag] if j == 0 else ())
                    e_tr_last = b.bank_read(bk, "vector", "tensor_copy",
                                            PT[:, grp * 128:(grp + 4) * 128],
                                            b.pbank[bk][:, :])
                b.mm_group(pvbk, [(b.pbank[pvbk][0:64, (ti % 4) * 128:(ti % 4) * 128 + 128],
                                   v_tok[:, tjt * 128 + r0:tjt * 128 + r0 + 64],
                                   PT[:, tjt * 128:(tjt + 1) * 128])
                                  for tjt in range(16)],
                           wait_edges=[e_tr_last])
                if ti % 4 == 3:
                    blk = ti // 4
                    e_pv_last = b.bank_read(
                        pvbk, "vector", "tensor_copy",
                        o_all[r0:r0 + 64, pair * T + blk * 512: pair * T + (blk + 1) * 512],
                        b.pbank[pvbk][0:64, :])
            if pair == 0 and hh == 0 and globals().get("DBG_L1"):
                PD = b.nc.dram_tensor("p_dbg", [128, T], BF16,
                                      kind="ExternalOutput").ap()
                b.dma(PD, P, queue="sync", wait_edges=[e_pv_last])
    if globals().get("DBG_L1"):
        OD = b.nc.dram_tensor("o_dbg", [128, 8 * T], BF16,
                              kind="ExternalOutput").ap()
        b.dma(OD, o_all, queue="sync", wait_edges=b.fence())
    pl.p = mark
    e_all = b.fence()
    wob = pl.bf(8 * 1024)
    upd = pl.f32(512)
    e_wo = wload(b, wob, W["l1_wo_t"], wait_edges=e_all)
    e_upd = None
    for do in range(DT):
        for cch, sl in enumerate(CH512):
            bk = b.rr(7)
            b.mm_group(bk, [(b.pbank[bk][:, :], wob[:, kp * 1024 + do * 128:kp * 1024 + do * 128 + 128],
                             o_all[:, kp * T + sl.start: kp * T + sl.stop])
                            for kp in range(8)], wait_edges=[e_wo])
            b.bank_read(bk, "vector", "tensor_mul", upd, b.pbank[bk][:, :],
                        sg[do][:, sl], wait_edges=list(in_edges))
            e_upd = b.op("vector", "tensor_add", xT[do][:, sl], xT[do][:, sl], upd,
                         inc=True)
    return [e_upd]



def emit_token_absmax(b, g, tiles_slices, pl_bank, out_col, tmp_cols, idf):
    ncomb = 0
    e_red = None
    for i4 in range(0, len(tiles_slices), 4):
        grp = tiles_slices[i4:i4 + 4]
        bk = b.rr(6)
        for j, ap in enumerate(grp):
            b.op("tensor", "transpose", b.pbank_bf[bk][:, j * 128:(j + 1) * 128],
                 ap, idf)
        e_red = b.bank_read(bk, "vector", "tensor_reduce",
                            tmp_cols[:, ncomb:ncomb + 1],
                            b.pbank_bf[bk][:, 0:len(grp) * 128], AX.X, ALU.max,
                            apply_absolute_value=True)
        ncomb += 1
    b.op("vector", "tensor_copy", out_col, tmp_cols[:, 0:1], wait_edges=[e_red])
    for i in range(1, ncomb):
        b.op("vector", "tensor_max", out_col, out_col, tmp_cols[:, i:i + 1])
    return b.op("vector", "nop", inc=True)


def emit_l1_quant_weights(b, g, W, pl, swrec):
    wf = pl.f32(1024)
    qv = pl.f32(1024)
    qb = pl.bf(1024)
    acc = pl.f32(40)
    ones_f = g["ones_f32_128x1"]
    onesr_f = g["ones_f32_1x128"]
    edges = []
    for mi, wname, qname in ((0, "l1_wg_t", "scr_l1_qg"), (1, "l1_wu_t", "scr_l1_qu"),
                             (2, "l1_wd_t", "scr_l1_qd")):
        src = W[wname]
        dst = W[qname]
        R, C = src.shape
        tiles = [(ri, cj) for ri in range(R // 128) for cj in range(C // 1024)]
        e_red = None
        for i, (ri, cj) in enumerate(tiles):
            e_l = b.dma(wf, src[ri * 128:(ri + 1) * 128, cj * 1024:(cj + 1) * 1024],
                        queue="sync")
            e_red = b.op("vector", "tensor_reduce", acc[:, i:i + 1], wf, AX.X,
                         ALU.add, apply_absolute_value=True, wait_edges=[e_l],
                         inc=True)
        e_s = b.op("vector", "reduce_sum", acc[:, 36:37], acc[:, 0:len(tiles)], AX.X,
                   wait_edges=[e_red], inc=True)
        bk = b.rr()
        b.mm_group(bk, [(b.pbank[bk][0:1, 0:1], ones_f, acc[:, 36:37])],
                   wait_edges=[e_s])
        e_t = b.bank_read(bk, "vector", "tensor_copy", acc[0:1, 38:39],
                          b.pbank[bk][0:1, 0:1])
        b.op("vector", "tensor_scalar", acc[0:1, 38:39], acc[0:1, 38:39],
             1.0 / (R * C), None, ALU.mult, wait_edges=[e_t])
        e_m = b.op("vector", "tensor_scalar", acc[0:1, 38:39], acc[0:1, 38:39],
                   1e-5, None, ALU.max, inc=True)
        bk = b.rr()
        b.mm_group(bk, [(b.pbank[bk][:, 0:1], onesr_f, acc[0:1, 38:39])],
                   wait_edges=[e_m])
        e_sw = b.bank_read(bk, "vector", "tensor_copy", swrec[:, mi:mi + 1],
                           b.pbank[bk][:, 0:1])
        edges.append(e_sw)
        b.op("vector", "reciprocal", acc[:, 39:40], swrec[:, mi:mi + 1],
             wait_edges=[e_sw])
        e_rq = b.op("vector", "nop", inc=True)
        for ri, cj in tiles:
            dsl = (slice(ri * 128, (ri + 1) * 128), slice(cj * 1024, (cj + 1) * 1024))
            e_l = b.dma(wf, src[dsl[0], dsl[1]], queue="sync")
            e_c = b.op("scalar", "activation", qv, wf, AF.Copy, scale=acc[:, 39:40],
                       wait_edges=[e_l, e_rq], inc=True)
            b.op("vector", "tensor_scalar", qv, qv, MAGIC, -MAGIC, ALU.add, ALU.add,
                 wait_edges=[e_c])
            e_q = b.op("vector", "tensor_scalar", qb, qv, 1.0, -1.0, ALU.min,
                       ALU.max, inc=True)
            e_st = b.dma(dst[dsl[0], dsl[1]], qb, queue="sync", wait_edges=[e_q])
            edges.append(e_st)
    return swrec, edges


def emit_l1_ffn(b, g, W, xn, sg, pl, xn_edges):
    swrec_region = pl.f32(4)
    plq = Plan(pl.ar, pl.p)
    swrec, e_qw = emit_l1_quant_weights(b, g, W, plq, swrec_region)
    b.mark("l1.ffn.mm")
    idf = g["ident_f32"]
    idb = g["ident_bf"]
    gu = pl.bf(32 * 512)
    xq = pl.bf(8 * 512)
    qwb = pl.bf(8 * 512)
    qdb = pl.bf(32 * 128)
    strep = pl.bf(512)
    rstrep = pl.bf(4 * 512)
    cols = pl.f32(32)
    tsc = pl.f32(512)
    tsc2 = pl.f32(512)
    e_dn_mm = None
    e_g_mm = None
    e_xq_mm = None
    for cch, sl in enumerate(CH512):
        rst_c = rstrep[:, cch * 512:(cch + 1) * 512]
        for tsub in range(4):
            t0 = sl.start + tsub * 128
            slices = [xn[k][:, t0:t0 + 128] for k in range(DT)]
            e_amax = emit_token_absmax(b, g, slices, None, cols[:, tsub:tsub + 1],
                                       cols[:, 8:16], idb)
        b.op("vector", "nop", wait_edges=[e_amax])
        for tsub in range(4):
            c = cols[:, tsub:tsub + 1]
            b.op("vector", "tensor_scalar", c, c, 1e-5, None, ALU.max)
            b.op("vector", "reciprocal", cols[:, 4 + tsub:5 + tsub], c)
            b.op("vector", "tensor_scalar", cols[:, 4 + tsub:5 + tsub],
                 cols[:, 4 + tsub:5 + tsub], 127.0, None, ALU.mult)
            b.op("vector", "tensor_scalar", c, c, 1.0 / 127.0, None, ALU.mult)
        e_cols = b.op("vector", "nop", inc=True)
        bk = b.rr(6)
        for tsub in range(4):
            b.mm_group(bk, [(b.pbank[bk][:, tsub * 128:(tsub + 1) * 128],
                             cols[:, 4 + tsub:5 + tsub].broadcast_to((128, 128)),
                             idf)], wait_edges=[e_cols] if tsub == 0 else ())
        e_strep = b.bank_read(bk, "vector", "tensor_copy", strep, b.pbank[bk][:, :])
        bk = b.rr(6)
        for tsub in range(4):
            b.mm_group(bk, [(b.pbank[bk][:, tsub * 128:(tsub + 1) * 128],
                             cols[:, tsub:tsub + 1].broadcast_to((128, 128)),
                             idf)], wait_edges=[e_cols] if tsub == 0 else ())
        e_rstrep = b.bank_read(bk, "vector", "tensor_copy", rst_c, b.pbank[bk][:, :])
        e_xq = None
        for k in range(DT):
            b.op("vector", "tensor_mul", tsc, xn[k][:, sl], strep,
                 wait_edges=[e_strep])
            b.op("vector", "tensor_scalar", tsc, tsc, MAGIC, -MAGIC, ALU.add, ALU.add)
            e_xq = b.op("vector", "tensor_scalar", xq[:, k * 512:(k + 1) * 512], tsc,
                        127.0, -128.0, ALU.min, ALU.max, inc=True)
        for mb in range(8):
            for which, qsrc in enumerate(("scr_l1_qg", "scr_l1_qu")):
                e_w = wload(b, qwb, W[qsrc][:, mb * 512:(mb + 1) * 512],
                            wait_edges=[e for e in (e_g_mm,) if e] + list(e_qw))
                for mt in range(4):
                    bk = b.rr(6)
                    e_g_mm = b.mm_group(bk, [
                        (b.pbank[bk][:, :],
                         qwb[:, k * 512 + mt * 128: k * 512 + mt * 128 + 128],
                         xq[:, k * 512:(k + 1) * 512]) for k in range(DT)],
                        wait_edges=[e_w, e_xq])
                    m_i = mb * 4 + mt
                    msl = slice(m_i * 512, (m_i + 1) * 512)
                    if which == 0:
                        b.bank_read(bk, "vector", "tensor_scalar", tsc,
                                    b.pbank[bk][:, :], swrec[:, 0:1], None, ALU.mult)
                        e_t = b.op("vector", "tensor_mul", tsc, tsc, rst_c,
                                   wait_edges=[e_rstrep], inc=True)
                        b.op("scalar", "activation", gu[:, msl], tsc, AF.Silu,
                             wait_edges=[e_t], inc=True)
                    else:
                        b.bank_read(bk, "vector", "tensor_scalar", tsc2,
                                    b.pbank[bk][:, :], swrec[:, 1:2], None, ALU.mult)
                        b.op("vector", "tensor_mul", tsc2, tsc2, rst_c)
                        b.op("vector", "tensor_mul", gu[:, msl], gu[:, msl], tsc2,
                             wait_edges=[b.last_op["scalar"]], inc=True)
        for tsub in range(4):
            t0 = tsub * 128
            slices = [gu[:, m_i * 512 + t0: m_i * 512 + t0 + 128]
                      for m_i in range(32)]
            e_amax = emit_token_absmax(b, g, slices, None, cols[:, tsub:tsub + 1],
                                       cols[:, 8:16], idb)
        for tsub in range(4):
            c = cols[:, tsub:tsub + 1]
            b.op("vector", "tensor_scalar", c, c, 1e-5, None, ALU.max,
                 wait_edges=[e_amax] if tsub == 0 else ())
            b.op("vector", "reciprocal", cols[:, 4 + tsub:5 + tsub], c)
            b.op("vector", "tensor_scalar", cols[:, 4 + tsub:5 + tsub],
                 cols[:, 4 + tsub:5 + tsub], 127.0, None, ALU.mult)
            b.op("vector", "tensor_scalar", c, c, 1.0 / 127.0, None, ALU.mult)
        e_cols = b.op("vector", "nop", inc=True)
        bk = b.rr(6)
        for tsub in range(4):
            b.mm_group(bk, [(b.pbank[bk][:, tsub * 128:(tsub + 1) * 128],
                             cols[:, 4 + tsub:5 + tsub].broadcast_to((128, 128)),
                             idf)], wait_edges=[e_cols] if tsub == 0 else ())
        e_strep = b.bank_read(bk, "vector", "tensor_copy", strep, b.pbank[bk][:, :])
        bk = b.rr(6)
        for tsub in range(4):
            b.mm_group(bk, [(b.pbank[bk][:, tsub * 128:(tsub + 1) * 128],
                             cols[:, tsub:tsub + 1].broadcast_to((128, 128)),
                             idf)], wait_edges=[e_cols] if tsub == 0 else ())
        e_rstrep = b.bank_read(bk, "vector", "tensor_copy", rst_c, b.pbank[bk][:, :])
        e_guq = None
        for m_i in range(32):
            msl = slice(m_i * 512, (m_i + 1) * 512)
            b.op("vector", "tensor_mul", tsc, gu[:, msl], strep,
                 wait_edges=[e_strep])
            b.op("vector", "tensor_scalar", tsc, tsc, MAGIC, -MAGIC, ALU.add, ALU.add)
            e_guq = b.op("vector", "tensor_scalar", gu[:, msl], tsc, 127.0, -128.0,
                         ALU.min, ALU.max, inc=True)
        for do in range(DT):
            e_wd = wload(b, qdb, W["scr_l1_qd"][:, do * 128:(do + 1) * 128],
                         wait_edges=[e for e in (e_dn_mm,) if e] + list(e_qw))
            bk = b.rr(6)
            e_dn_mm = b.mm_group(bk, [
                (b.pbank[bk][:, :], qdb[:, k * 128:(k + 1) * 128],
                 gu[:, k * 512:(k + 1) * 512]) for k in range(32)],
                wait_edges=[e_wd, e_guq])
            b.bank_read(bk, "vector", "tensor_scalar", tsc, b.pbank[bk][:, :],
                        swrec[:, 2:3], None, ALU.mult)
            e_t = b.op("vector", "tensor_mul", tsc, tsc, rst_c,
                       wait_edges=[e_rstrep], inc=True)
            b.op("scalar", "activation", sg[do][:, sl], tsc, AF.Silu,
                 wait_edges=[e_t], inc=True)
    return b.fence(engines=("scalar",))



def emit_l2_attn(b, g, W, xn, xT, sg, pl, in_edges):
    o_all = pl.bf(8 * T)
    mark = pl.p
    qf = pl.bf(T)
    kf = pl.bf(T)
    ktok = pl.bf(T)
    vtok = pl.bf(T)
    otok = pl.bf(T)
    Ssb = pl.bf(128)
    qraw = pl.bf(512)
    t1 = pl.f32(512)
    atm = pl.bf(128)
    sqs = pl.f32(128)
    cstat = pl.f32(8)
    wqb = pl.bf(8 * 128)
    wkb = pl.bf(8 * 128)
    wvb = pl.bf(8 * 128)
    fmw = pl.bf(4 * 128)
    fmb = W["l2_fmb_sb"]
    rmsrep = W["l2_rms_sb"]
    idb = g["ident_bf"]
    triu = g["l2_triu"]
    e_fm = wload(b, fmw, W["l2_fm_t"])
    e_q_mm = None
    e_o_tr = None
    SCALE = 128.0 ** -0.5
    for h in range(8):
        wed = [e for e in (e_q_mm,) if e]
        e_wq = wload(b, wqb, W["l2_wq_t"][:, h * 128:(h + 1) * 128], wait_edges=wed)
        e_wk = wload(b, wkb, W["l2_wk_t"][:, h * 128:(h + 1) * 128], wait_edges=wed)
        e_wv = wload(b, wvb, W["l2_wv_t"][:, h * 128:(h + 1) * 128], wait_edges=wed)
        for ti in range(16):
            bk = b.rr(6)
            b.mm_group(bk, [(b.pbank[bk][:, 0:128], xn[k][:, ti * 128:(ti + 1) * 128],
                             wvb[:, k * 128:(k + 1) * 128]) for k in range(DT)],
                       wait_edges=[e_wv] + list(in_edges))
            b.bank_read(bk, "vector", "tensor_copy",
                        vtok[:, ti * 128:(ti + 1) * 128], b.pbank[bk][:, 0:128])
        for which, (wbuf, e_w, dstbuf) in enumerate(
                ((wqb, e_wq, qf), (wkb, e_wk, kf))):
            for cch, sl in enumerate(CH512):
                bk = b.rr(6)
                e_q_mm = b.mm_group(bk, [(b.pbank[bk][:, :], wbuf[:, k * 128:(k + 1) * 128],
                                          xn[k][:, sl]) for k in range(DT)],
                                    wait_edges=[e_w])
                e_r = b.bank_read(bk, "scalar", "copy", qraw, b.pbank[bk][:, :])
                bkA = b.rr(6); bkB = b.rr(6)
                fo = which * 2
                b.mm_group(bkA, [(b.pbank[bkA][:, :], fmw[:, fo * 128:(fo + 1) * 128],
                                  qraw)], wait_edges=[e_fm, e_r])
                b.mm_group(bkB, [(b.pbank[bkB][:, :], fmw[:, (fo + 1) * 128:(fo + 2) * 128],
                                  qraw)], wait_edges=[e_fm, e_r])
                sc2 = SCALE if which == 0 else 1.0
                b.bank_read(bkA, "vector", "tensor_scalar", t1, b.pbank[bkA][:, :],
                            fmb[:, 2 * which:2 * which + 1], sc2, ALU.add, ALU.mult)
                b.bank_read(bkB, "vector", "scalar_tensor_tensor", dstbuf[:, sl],
                            b.pbank[bkB][:, :], fmb[:, 2 * which + 1:2 * which + 2],
                            t1, ALU.add, ALU.mult)
        e_kf = b.fence(engines=("vector",))
        for grp in range(0, 16, 4):
            bk = b.rr(6)
            for j in range(4):
                b.op("tensor", "transpose",
                     b.pbank_bf[bk][:, j * 128:(j + 1) * 128],
                     kf[:, (grp + j) * 128:(grp + j + 1) * 128], idb,
                     wait_edges=(e_kf if (grp == 0 and j == 0) else ()))
            b.bank_read(bk, "vector", "tensor_copy",
                        ktok[:, grp * 128:(grp + 4) * 128],
                        b.pbank_bf[bk][:, 0:512])
        e_kt = b.fence(engines=("vector",))
        b.op("vector", "memset", Ssb, 0.0)
        e_S = b.op("vector", "nop", inc=True)
        prev_rec2 = None
        for n in range(16):
            cs = slice(n * 128, (n + 1) * 128)
            bk = b.rr(5)
            b.mm_group(bk, [(b.pbank[bk][:, 0:128], kf[:, cs], qf[:, cs])],
                       wait_edges=e_kt if n == 0 else ())
            e_at = b.bank_read(bk, "vector", "tensor_mul", atm, b.pbank[bk][:, 0:128],
                               triu)
            mms = []
            if n > 0:
                mms.append((b.pbank[7][:, 0:128], qf[:, cs], Ssb))
            mms.append((b.pbank[7][:, 0:128], atm, vtok[:, cs]))
            b.mm_group(7, mms, wait_edges=[e_at, e_S])
            def bld(e, st=(n == 0), sp=(n == 15), l=ktok[:, cs], r=vtok[:, cs]):
                return e.matmul(b.pbank[6][:, 0:128], l, r, start=st, stop=sp)
            rec = b._mkrec("tensor", bld)
            rec.consumed = True
            b._dep_read(rec, ktok[:, cs])
            b._dep_read(rec, vtok[:, cs])
            if n == 0:
                b._dep_write(rec, b.pbank[6][:, 0:128])
            if prev_rec2 is not None:
                rec.waits.add(prev_rec2)
            rec2 = b.op("scalar", "copy", Ssb, b.pbank[6][:, 0:128], inc=True,
                        wait_edges=[rec])
            prev_rec2 = rec2
            e_S = rec2
            e_sq = b.op("scalar", "activation", sqs, b.pbank[7][:, 0:128], AF.Square,
                        accum_out=cstat[:, 0:1], wait_edges=[b.last_op["tensor"]],
                        inc=True)
            b.op("vector", "tensor_scalar", cstat[:, 1:2], cstat[:, 0:1],
                 1.0 / 128.0, 1e-5, ALU.mult, ALU.add, wait_edges=[e_sq])
            e_sr = b.op("scalar", "activation", cstat[:, 1:2], cstat[:, 1:2],
                        AF.Sqrt, inc=True)
            e_rc = b.op("vector", "reciprocal", cstat[:, 2:3], cstat[:, 1:2],
                        wait_edges=[e_sr], inc=True)
            b.op("vector", "scalar_tensor_tensor", otok[:, cs], b.pbank[7][:, 0:128],
                 cstat[:, 2:3], rmsrep, ALU.mult, ALU.mult, wait_edges=[e_rc],
                 inc=True)
        e_on = b.fence(engines=("vector",))
        for grp in range(0, 16, 4):
            bk = b.rr(5)
            for j in range(4):
                b.op("tensor", "transpose",
                     b.pbank_bf[bk][:, j * 128:(j + 1) * 128],
                     otok[:, (grp + j) * 128:(grp + j + 1) * 128], idb,
                     wait_edges=(e_on if (grp == 0 and j == 0) else ()))
            e_o_tr = b.bank_read(bk, "vector", "tensor_copy",
                                 o_all[:, h * T + grp * 128: h * T + (grp + 4) * 128],
                                 b.pbank_bf[bk][:, 0:512])
    pl.p = mark
    e_all = b.fence()
    wob = pl.bf(8 * 1024)
    upd = pl.f32(512)
    e_wo = wload(b, wob, W["l2_wo_t"], wait_edges=e_all)
    e_upd = None
    for do in range(DT):
        for cch, sl in enumerate(CH512):
            bk = b.rr(7)
            b.mm_group(bk, [(b.pbank[bk][:, :], wob[:, kp * 1024 + do * 128:kp * 1024 + do * 128 + 128],
                             o_all[:, kp * T + sl.start: kp * T + sl.stop])
                            for kp in range(8)], wait_edges=[e_wo])
            b.bank_read(bk, "vector", "tensor_mul", upd, b.pbank[bk][:, :],
                        sg[do][:, sl], wait_edges=list(in_edges))
            e_upd = b.op("vector", "tensor_add", xT[do][:, sl], xT[do][:, sl], upd,
                         inc=True)
    return [e_upd]



HID_PAD = 2816
NH = HID_PAD // 128


def emit_l2_ffn(b, g, W, xn, sgr, sg, pl, xn_edges):
    y = pl.bf(8 * T)
    yt = [y[:, d * T:(d + 1) * T] for d in range(8)]
    shc0 = pl.bf(4 * 512)
    mark0 = pl.p
    idb = g["ident_bf"]
    rot64 = g["l2_rot64"]
    e64 = g["e64"]
    e64t = g["e64_t"]
    onesr_f = g["ones_f32_1x128"]
    ones_f = g["ones_f32_128x1"]
    qsks = W["l2f_qsks_sb"]
    lng = W["l2f_lng_sb"]
    lnb = W["l2f_lnb_sb"]
    b1a = W["l2f_b1a_sb"]
    b1g = W["l2f_b1g_sb"]
    b2c = W["l2f_b2_sb"]
    e_y = []
    for d in range(8):
        e_y.append(b.op("vector", "tensor_copy", yt[d], xn[d][:, :],
                        wait_edges=list(xn_edges) if d == 0 else [], inc=True))
    e_upd = e_y

    def build_shift_c0(waits):
        ee = []
        for j in range(4):
            b.op("vector", "memset", shc0[:, j * 512:j * 512 + 1], 0.0,
                 wait_edges=waits if j == 0 else [])
            ee.append(b.op("vector", "tensor_copy",
                           shc0[:, j * 512 + 1:(j + 1) * 512],
                           yt[4 + j][:, 0:511], inc=True))
        return ee

    def hs(d, lo, hi):
        if d < 4:
            return yt[d][:, lo:hi]
        if lo == 0:
            assert hi <= 512
            return shc0[:, (d - 4) * 512:(d - 4) * 512 + (hi - lo)]
        return yt[d][:, lo - 1:hi - 1]

    for l in range(2):
        b.mark(f"l2f.attn{l}")
        pl.p = mark0
        qk = pl.bf(2 * T)
        v_tok = pl.bf(16 * 128)
        P = pl.bf(2 * 512)
        PT = pl.bf(8 * 128)
        o_all = sgr[:, 0:8192]
        qnb = pl.bf(512)
        qraw = pl.f32(512)
        t1 = pl.f32(512)
        sq = pl.f32(512)
        n2 = pl.bf(T)
        sums = pl.f32(16)
        diag = pl.bf(128)
        wqb = pl.bf(8 * 128)
        wkb = pl.bf(8 * 128)
        wvb = pl.bf(8 * 128)
        trg = sgr[:, 8192:8192 + 4096]
        band = sgr[:, 12288:12288 + 1408]
        e_band = b.dma(band, W["ct_l2_band"], queue="sync")
        e_sh = build_shift_c0(e_upd)
        e_q_mm = None
        e_pv_last = None
        e_tr_last = None
        e_sum = None
        e_sc_last = None
        wsrc = W["l2f_wqkv"]
        for pair in range(4):
            wed = [e for e in (e_q_mm,) if e]
            e_wq = wload(b, wqb, wsrc[l * 1024:(l + 1) * 1024, pair * 128:(pair + 1) * 128],
                         wait_edges=wed)
            e_wk = wload(b, wkb, wsrc[l * 1024:(l + 1) * 1024, 512 + pair * 128:512 + (pair + 1) * 128],
                         wait_edges=wed)
            e_wv = wload(b, wvb, wsrc[l * 1024:(l + 1) * 1024, 1024 + pair * 128:1024 + (pair + 1) * 128],
                         wait_edges=wed)
            for ti in range(16):
                bk = b.rr(7)
                b.mm_group(bk, [(b.pbank[bk][:, 0:128],
                                 hs(k, ti * 128, (ti + 1) * 128),
                                 wvb[:, k * 128:(k + 1) * 128]) for k in range(DT)],
                           wait_edges=[e_wv] + e_sh)
                b.bank_read(bk, "vector", "tensor_copy",
                            v_tok[:, ti * 128:(ti + 1) * 128], b.pbank[bk][:, 0:128])
            for which, (wbuf, e_w, dst_off) in enumerate(
                    ((wqb, e_wq, 0), (wkb, e_wk, T))):
                for cch, sl in enumerate(CH512):
                    bk = b.rr(7)
                    e_q_mm = b.mm_group(bk, [(b.pbank[bk][:, :],
                                              wbuf[:, k * 128:(k + 1) * 128],
                                              hs(k, sl.start, sl.stop))
                                             for k in range(DT)],
                                        wait_edges=[e_w] + e_sh)
                    dsl = slice(dst_off + sl.start, dst_off + sl.stop)
                    e_r = b.bank_read(bk, "vector", "tensor_copy", qk[:, dsl],
                                      b.pbank[bk][:, :],
                                      wait_edges=[e_sc_last] if e_sc_last else [])
                    e_sq = b.op("scalar", "activation", sq, qk[:, dsl], AF.Square,
                                wait_edges=[e_r], inc=True)
                    bk2 = b.rr(7)
                    b.mm_group(bk2, [(b.pbank[bk2][0:2, :], e64t, sq)],
                               wait_edges=[e_sq])
                    b.bank_read(bk2, "vector", "tensor_copy", n2[0:2, sl],
                                b.pbank[bk2][0:2, :])
                e_n2 = b.op("scalar", "activation", n2[0:2, 0:T], n2[0:2, 0:T],
                            AF.Sqrt, inc=True)
                e_rn = b.op("vector", "tensor_scalar", n2[0:2, 0:T], n2[0:2, 0:T],
                            1e-12, None, ALU.max, wait_edges=[e_n2], inc=True)
                e_tq = [b.dma(trg[:, 0:T], W["ct_l2_cos" + ("q" if which == 0 else "k")],
                              queue="sync"),
                        b.dma(trg[:, T:2 * T], W["ct_l2_sin" + ("q" if which == 0 else "k")],
                              queue="sync")]
                for cch, sl in enumerate(CH512):
                    dsl = slice(dst_off + sl.start, dst_off + sl.stop)
                    e_rc = b.op("vector", "reciprocal", sq[0:2, :], n2[0:2, sl],
                                wait_edges=[e_rn], inc=True)
                    bk2 = b.rr(7)
                    b.mm_group(bk2, [(b.pbank[bk2][:, :], e64, sq[0:2, :])],
                               wait_edges=[e_rc])
                    e_rep = b.bank_read(bk2, "vector", "tensor_copy", qraw,
                                        b.pbank[bk2][:, :])
                    e_qn = b.op("vector", "scalar_tensor_tensor", qnb, qk[:, dsl],
                                qsks[:, 2 * l + which:2 * l + which + 1], qraw,
                                ALU.mult, ALU.mult, wait_edges=[e_rep], inc=True)
                    bk3 = b.rr(7)
                    b.mm_group(bk3, [(b.pbank[bk3][:, :], rot64, qnb)],
                               wait_edges=[e_qn])
                    b.op("vector", "tensor_mul", t1, qnb, trg[:, sl.start:sl.stop],
                         wait_edges=e_tq)
                    e_rt = b.bank_read(bk3, "vector", "scalar_tensor_tensor", qraw,
                                       b.pbank[bk3][:, :], 1.0,
                                       trg[:, T + sl.start:T + sl.stop],
                                       ALU.mult, ALU.mult)
                    e_sc_last = b.op("vector", "tensor_add", qk[:, dsl], t1, qraw,
                                     wait_edges=[e_rt], inc=True)
            e_rope = b.fence(engines=("vector",))
            for hh in range(2):
                h = 2 * pair + hh
                r0 = hh * 64
                qrows = qk[r0:r0 + 64, 0:T]
                krows = qk[r0:r0 + 64, T:2 * T]
                pvbk = 7
                for ti in range(16):
                    w = ti // 4
                    v_ = ti % 4
                    nch = 1 if w == 0 else 2
                    e_exp = None
                    for ci in range(nch):
                        kc = w - 1 + ci if w > 0 else 0
                        moff = (3 - v_) * 128 + (512 if (w == 0 or ci == 1) else 0)
                        bk = ci
                        b.mm_group(bk, [(b.pbank[bk][:, :],
                                         qrows[:, ti * 128:(ti + 1) * 128],
                                         krows[:, kc * 512:(kc + 1) * 512]),
                                        (b.pbank[bk][:, :], idb,
                                         band[:, moff:moff + 512])],
                                   wait_edges=(list(e_rope) + [e_band])
                                   if (ti == 0 and ci == 0) else ())
                    e_mx = b.op("vector", "reduce_max", sums[:, 10:11],
                                b.pbig[:, 0:nch * 512], AX.X,
                                wait_edges=[b.last_op["tensor"]])
                    e_nm = b.op("vector", "tensor_scalar", sums[:, 11:12],
                                sums[:, 10:11], -1.0, None, ALU.mult, inc=True)
                    for ci in range(nch):
                        ew = [e_nm] + [e for e in (e_tr_last, e_sum) if e]
                        e_exp = b.bank_read(ci, "scalar", "activation",
                                            P[:, ci * 512:(ci + 1) * 512],
                                            b.pbank[ci][:, :], AF.Exp,
                                            bias=sums[:, 11:12], scale=1.0,
                                            accum_out=sums[:, ci:ci + 1],
                                            wait_edges=ew)
                    e_sum = b.op("vector", "reduce_sum", sums[:, 8:9],
                                 sums[:, 0:nch], AX.X, wait_edges=[e_exp])
                    b.op("vector", "reciprocal", sums[:, 9:10], sums[:, 8:9])
                    e_diag = b.op("vector", "tensor_scalar", diag, idb,
                                  sums[:, 9:10], None, ALU.mult, inc=True,
                                  wait_edges=[e_tr_last] if e_tr_last else [])
                    ntj = nch * 4
                    for grp in range(0, ntj, 4):
                        bk = 4 + (grp // 4) % 2
                        for j in range(4):
                            b.mm_group(bk, [(b.pbank[bk][:, j * 128:(j + 1) * 128],
                                             P[:, (grp + j) * 128:(grp + j + 1) * 128],
                                             diag)],
                                       wait_edges=[e_diag] if j == 0 else ())
                        e_tr_last = b.bank_read(bk, "vector", "tensor_copy",
                                                PT[:, grp * 128:(grp + 4) * 128],
                                                b.pbank[bk][:, :])
                    kbase = 4 * (w - 1) if w > 0 else 0
                    b.mm_group(pvbk, [
                        (b.pbank[pvbk][0:64, v_ * 128:v_ * 128 + 128],
                         v_tok[:, (kbase + tjt) * 128 + r0:(kbase + tjt) * 128 + r0 + 64],
                         PT[:, tjt * 128:(tjt + 1) * 128]) for tjt in range(ntj)],
                        wait_edges=[e_tr_last])
                    if v_ == 3:
                        blk = ti // 4
                        e_pv_last = b.bank_read(
                            pvbk, "vector", "tensor_copy",
                            o_all[r0:r0 + 64, pair * T + blk * 512:pair * T + (blk + 1) * 512],
                            b.pbank[pvbk][0:64, :])
        e_allo = b.fence()
        wo2 = sgr[:, 8192:8192 + 4096]
        tevu = pl.f32(512)
        e_wo = wload(b, wo2, W["l2f_wo"][l * 512:(l + 1) * 512, :], wait_edges=e_allo)
        e_upd = []
        for do in range(DT):
            for cch, sl in enumerate(CH512):
                bk = b.rr(7)
                b.mm_group(bk, [(b.pbank[bk][:, :],
                                 wo2[:, kp * 1024 + do * 128:kp * 1024 + do * 128 + 128],
                                 o_all[:, kp * T + sl.start:kp * T + sl.stop])
                                for kp in range(4)], wait_edges=[e_wo])
                b.bank_read(bk, "vector", "tensor_copy", tevu, b.pbank[bk][:, :])
                e_u = b.op("vector", "tensor_add", yt[do][:, sl], yt[do][:, sl],
                           tevu, inc=True)
                e_upd.append(e_u)
        b.mark(f"l2f.mlp{l}")
        pl.p = mark0
        xn2 = sgr[:, 0:4096]
        hbuf = sgr[:, 4096:4096 + 5632]
        w1b = sgr[:, 9728:9728 + 1024]
        w2b = sgr[:, 10752:10752 + 1408]
        rows = pl.f32(2 * 512)
        rep = pl.f32(2 * 512)
        sqw = pl.f32(512)
        tev = pl.f32(512)
        tev2 = pl.f32(512)
        e_sh2 = build_shift_c0(e_upd)
        A = rows[0:1, 0:512]
        B = rows[0:1, 512:1024]
        tmp2 = rep[0:1, 0:512]
        e_w1_mm = None
        e_w2_mm = None
        for cch, sl in enumerate(CH512):
            b.mm_group(0, [(b.pbank[0][0:1, :], g["ones_bf_128x1"],
                            hs(d, sl.start, sl.stop)) for d in range(8)],
                       wait_edges=(e_sh2 + e_upd) if cch == 0 else ())
            b.bank_read(0, "vector", "tensor_copy", A, b.pbank[0][0:1, :])
            last_mm = None
            for d in range(8):
                e_s2 = b.op("scalar", "activation", sqw, hs(d, sl.start, sl.stop),
                            AF.Square, inc=True)
                last_mm = b.mm_step(b.pbank[1][0:1, :], ones_f, sqw,
                                    start=(d == 0), stop=(d == 7),
                                    wait_edges=[e_s2])
            e_b = b.op("vector", "tensor_copy", B, b.pbank[1][0:1, :],
                       wait_edges=[last_mm], inc=True)
            b.op("vector", "tensor_scalar", A, A, -1.0 / D, None, ALU.mult)
            b.op("vector", "tensor_scalar", B, B, 1.0 / D, None, ALU.mult)
            b.op("vector", "tensor_mul", tmp2, A, A)
            b.op("vector", "tensor_sub", B, B, tmp2)
            e_var = b.op("vector", "tensor_scalar", B, B, 1e-5, None, ALU.add,
                         inc=True)
            e_sqr = b.op("scalar", "activation", B, B, AF.Sqrt, wait_edges=[e_var],
                         inc=True)
            b.op("vector", "reciprocal", B, B, wait_edges=[e_sqr])
            e_stats = b.op("vector", "tensor_mul", A, A, B, inc=True)
            bk = b.rr(6)
            b.mm_group(bk, [(b.pbank[bk][:, :], onesr_f, B)], wait_edges=[e_stats])
            e_r1 = b.bank_read(bk, "vector", "tensor_copy", rep[:, 0:512],
                               b.pbank[bk][:, :])
            bk = b.rr(6)
            b.mm_group(bk, [(b.pbank[bk][:, :], onesr_f, A)], wait_edges=[e_stats])
            e_r2 = b.bank_read(bk, "vector", "tensor_copy", rep[:, 512:1024],
                               b.pbank[bk][:, :])
            e_xn2 = None
            for d in range(8):
                b.op("vector", "tensor_mul", tev, hs(d, sl.start, sl.stop),
                     rep[:, 0:512], wait_edges=[e_r1, e_r2])
                b.op("vector", "tensor_add", tev, tev, rep[:, 512:1024])
                e_xn2 = b.op("vector", "tensor_scalar", xn2[:, d * 512:(d + 1) * 512],
                             tev, lng[:, l * 8 + d:l * 8 + d + 1],
                             lnb[:, l * 8 + d:l * 8 + d + 1], ALU.mult, ALU.add,
                             inc=True)
            for half in range(2):
                hk0 = half * 11
                for i in range(11):
                    hi = hk0 + i
                    e_wa = wload(b, w1b[:, 0:8 * 128],
                                 W["l2f_w1a"][l * 1024:(l + 1) * 1024,
                                              hi * 128:(hi + 1) * 128],
                                 wait_edges=[e for e in (e_w1_mm,) if e])
                    bkA = b.rr(6)
                    b.mm_group(bkA, [(b.pbank[bkA][:, :],
                                      w1b[:, k * 128:(k + 1) * 128],
                                      xn2[:, k * 512:(k + 1) * 512])
                                     for k in range(8)],
                               wait_edges=[e_wa, e_xn2])
                    e_ta = b.bank_read(bkA, "vector", "tensor_scalar", tev,
                                       b.pbank[bkA][:, :],
                                       b1a[:, l * 22 + hi:l * 22 + hi + 1], None,
                                       ALU.add)
                    e_wg2 = wload(b, w2b[:, 0:8 * 128] if False else w1b[:, 0:8 * 128],
                                  W["l2f_w1g"][l * 1024:(l + 1) * 1024,
                                               hi * 128:(hi + 1) * 128],
                                  wait_edges=[e_ta])
                    bkG = b.rr(6)
                    e_w1_mm = b.mm_group(bkG, [(b.pbank[bkG][:, :],
                                                w1b[:, k * 128:(k + 1) * 128],
                                                xn2[:, k * 512:(k + 1) * 512])
                                               for k in range(8)],
                                         wait_edges=[e_wg2, e_xn2])
                    e_gg = b.bank_read(bkG, "scalar", "activation", tev2,
                                       b.pbank[bkG][:, :], AF.Gelu,
                                       bias=b1g[:, l * 22 + hi:l * 22 + hi + 1])
                    b.op("vector", "tensor_mul", hbuf[:, i * 512:(i + 1) * 512],
                         tev, tev2, wait_edges=[e_ta, e_gg], inc=True)
                e_h = b.fence(engines=("vector",))
                for do in range(DT):
                    e_wd = wload(b, w2b,
                                 W["l2f_w2"][l * HID_PAD + hk0 * 128:
                                             l * HID_PAD + (hk0 + 11) * 128,
                                             do * 128:(do + 1) * 128],
                                 wait_edges=[e for e in (e_w2_mm,) if e])
                    bk = b.rr(6)
                    e_w2_mm = b.mm_group(bk, [
                        (b.pbank[bk][:, :], w2b[:, k * 128:(k + 1) * 128],
                         hbuf[:, k * 512:(k + 1) * 512]) for k in range(11)],
                        wait_edges=[e_wd] + e_h)
                    if half == 0:
                        b.bank_read(bk, "vector", "tensor_copy", tev,
                                    b.pbank[bk][:, :])
                    else:
                        b.bank_read(bk, "vector", "tensor_scalar", tev,
                                    b.pbank[bk][:, :],
                                    b2c[:, l * 8 + do:l * 8 + do + 1], None,
                                    ALU.add)
                    e_u = b.op("vector", "tensor_add", yt[do][:, sl], yt[do][:, sl],
                               tev, inc=True)
            e_upd = [e_u]
    e_fin = b.fence(engines=("vector",))
    edges = [b.op("scalar", "activation", sg[d][:, :], y[:, d * T:(d + 1) * T],
                  AF.Silu, wait_edges=e_fin, inc=True) for d in range(DT)]
    return edges



def emit_l3(b, g, W, xT, pl, in_edges):
    CL = 256
    NCH = T // CL
    NB = 4
    idb = g["ident_bf"]
    onesb = g["ones_bf_128x1"]
    ones_f = g["ones_f32_128x1"]
    onesr_f = g["ones_f32_1x128"]
    sel16_buf = pl.bf(2048)
    sel16 = sel16_buf[0:16, :]
    e_sel = b.dma(sel16, W["ct_sel16"], queue="sync")
    xnc = pl.bf(8 * CL)
    xln = pl.bf(8 * CL)
    sgc = pl.bf(8 * CL)
    Psb = pl.bf(2 * CL)
    Qsb = pl.bf(2 * CL)
    xcb = pl.bf(16 * (CL + 8))
    zs = pl.bf(16 * CL)
    xconv = pl.bf(16 * CL)
    ygated = pl.bf(16 * CL)
    prd = pl.bf(CL)
    hst = prd
    prB = pl.bf(CL)
    prC = pl.bf(CL)
    dt = pl.f32(CL)
    sp1 = pl.f32(CL)
    sp2 = pl.f32(CL)
    dtu = pl.f32(CL)
    dA = pl.bf(NB * CL)
    dBu = pl.bf(NB * CL)
    hsc = pl.bf(NB * CL)
    tmpb = pl.f32(NB * CL)
    pt12 = pl.f32(2 * CL)
    pt1 = pt12[:, 0:CL]
    pt2 = pt12[:, CL:2 * CL]
    Brep = pl.bf(16 * CL)
    Crep = pl.bf(16 * CL)
    yac = pl.f32(CL)
    state = pl.f32(256)
    rows = pl.f32(2 * CL)
    rep = pl.f32(2 * CL)
    sqw = pl.f32(CL)
    tev = pl.f32(CL)
    asb = pl.f32(256)
    wu1 = pl.bf(8 * 256)
    wv1 = pl.bf(2 * 4096)
    wu2r = [pl.bf(256), pl.bf(256)]
    wv2 = pl.bf(2 * 1024)
    winb = pl.bf(8 * 512)
    wxp = pl.bf(16 * 96)
    wdt = pl.bf(2048)
    wob = pl.bf(16 * 128)
    lnw = W["l3_lnw_sb"]
    lnbc = W["l3_lnb_sb"]
    cwc = W["l3_convw_sb"]
    cbc = W["l3_convb_sb"]
    bdtc = W["l3_bdt_sb"]
    dskc = W["l3_dsk_sb"]
    f1b = W["l3f_b1_sb"]
    f2b = W["l3f_b2_sb"]
    e_al = b.dma(asb.rearrange("p (a n) -> p a n", a=16),
                 W["l3_alog"].rearrange("(a p) n -> p a n", p=128),
                 queue="sync")
    e_ax = b.op("scalar", "activation", asb, asb, AF.Exp, wait_edges=[e_al],
                inc=True)
    e_A = b.op("vector", "tensor_scalar", asb, asb, -1.0, None, ALU.mult,
               wait_edges=[e_ax], inc=True)
    b.op("vector", "memset", state, 0.0)
    e_st0 = b.op("vector", "nop", inc=True)
    e_wu1 = wload(b, wu1, W["l3f_u1"])
    e_wv1 = wload(b, wv1, W["l3f_v1"])

    e_wv2 = wload(b, wv2, W["l3f_v2"])
    e_wxp = wload(b, wxp, W["l3_wxp_t"])
    e_wdt = b.dma(wdt[0:64, :], W["l3_wdt_t"], queue="gpsimd")
    A_ = rows[0:1, 0:CL]
    B_ = rows[0:1, CL:2 * CL]
    tmp2 = rep[0:1, 0:CL]
    e_upd = list(in_edges)
    e_win_mm = None
    e_wo_mm = None
    chunks = [slice(c * CL, (c + 1) * CL) for c in range(NCH)]
    for cch, sl in enumerate(chunks):
        def ln_chunk(src_tiles, dst, gcol, bcol, waits):
            ones1 = onesb if src_tiles(0).dtype == BF16 else ones_f
            b.mm_group(0, [(b.pbank[0][0:1, 0:CL], ones1, src_tiles(d))
                           for d in range(8)], wait_edges=waits)
            b.bank_read(0, "vector", "tensor_copy", A_, b.pbank[0][0:1, 0:CL])
            last_mm = None
            for d in range(8):
                e_s2 = b.op("scalar", "activation", sqw, src_tiles(d), AF.Square,
                            wait_edges=waits if d == 0 else [], inc=True)
                last_mm = b.mm_step(b.pbank[1][0:1, 0:CL], ones_f, sqw,
                                    start=(d == 0), stop=(d == 7),
                                    wait_edges=[e_s2])
            b.op("vector", "tensor_copy", B_, b.pbank[1][0:1, 0:CL],
                 wait_edges=[last_mm])
            b.op("vector", "tensor_scalar", A_, A_, -1.0 / D, None, ALU.mult)
            b.op("vector", "tensor_scalar", B_, B_, 1.0 / D, None, ALU.mult)
            b.op("vector", "tensor_mul", tmp2, A_, A_)
            b.op("vector", "tensor_sub", B_, B_, tmp2)
            e_var = b.op("vector", "tensor_scalar", B_, B_, 1e-5, None, ALU.add,
                         inc=True)
            e_sqr = b.op("scalar", "activation", B_, B_, AF.Sqrt,
                         wait_edges=[e_var], inc=True)
            b.op("vector", "reciprocal", B_, B_, wait_edges=[e_sqr])
            e_stats = b.op("vector", "tensor_mul", A_, A_, B_, inc=True)
            bk = b.rr(6)
            b.mm_group(bk, [(b.pbank[bk][:, 0:CL], onesr_f, B_)],
                       wait_edges=[e_stats])
            e_r1 = b.bank_read(bk, "vector", "tensor_copy", rep[:, 0:CL],
                               b.pbank[bk][:, 0:CL])
            bk = b.rr(6)
            b.mm_group(bk, [(b.pbank[bk][:, 0:CL], onesr_f, A_)],
                       wait_edges=[e_stats])
            e_r2 = b.bank_read(bk, "vector", "tensor_copy", rep[:, CL:2 * CL],
                               b.pbank[bk][:, 0:CL])
            ee = None
            for d in range(8):
                b.op("vector", "tensor_mul", tev, src_tiles(d), rep[:, 0:CL],
                     wait_edges=[e_r1, e_r2])
                if gcol is None:
                    ee = b.op("vector", "tensor_add", dst(d), tev,
                              rep[:, CL:2 * CL], inc=True)
                else:
                    b.op("vector", "tensor_add", tev, tev, rep[:, CL:2 * CL])
                    ee = b.op("vector", "tensor_scalar", dst(d), tev,
                              gcol(d), bcol(d), ALU.mult, ALU.add, inc=True)
            return ee
        e_xnc = ln_chunk(lambda d: xT[d][:, sl],
                         lambda d: xnc[:, d * CL:(d + 1) * CL], None, None,
                         e_upd)
        for m in range(2):
            bk = b.rr(6)
            b.mm_group(bk, [(b.pbank[bk][:, 0:CL],
                             wu1[:, k * 256 + m * 128:k * 256 + m * 128 + 128],
                             xnc[:, k * CL:(k + 1) * CL]) for k in range(8)],
                       wait_edges=[e_wu1, e_xnc])
            b.bank_read(bk, "vector", "tensor_copy", Psb[:, m * CL:(m + 1) * CL],
                        b.pbank[bk][:, 0:CL])
        e_P = b.fence(engines=("vector",))
        last_q = [None, None]
        for mt in range(32):
            bkH = b.rr(6)
            b.mm_group(bkH, [(b.pbank[bkH][:, 0:CL],
                              wv1[:, k * 4096 + mt * 128:k * 4096 + mt * 128 + 128],
                              Psb[:, k * CL:(k + 1) * CL]) for k in range(2)],
                       wait_edges=[e_wv1] + e_P)
            e_g = b.bank_read(bkH, "scalar", "activation", hst,
                              b.pbank[bkH][:, 0:CL], AF.Gelu_apprx_tanh,
                              bias=f1b[:, mt:mt + 1])
            wbuf = wu2r[mt % 2]
            e_wu2 = wload(b, wbuf, W["l3f_u2"][mt * 128:(mt + 1) * 128, :],
                          wait_edges=[last_q[1]] if last_q[1] is not None else [])
            for m in range(2):
                last_q[m] = b.mm_step(
                    b.pbank[6 + m][:, 0:CL],
                    wbuf[:, m * 128:(m + 1) * 128],
                    hst, start=(mt == 0), stop=(mt == 31),
                    wait_edges=[e_wu2, e_g])
        for m in range(2):
            last_q[m].consumed = True
            b.op("vector", "tensor_copy", Qsb[:, m * CL:(m + 1) * CL],
                 b.pbank[6 + m][:, 0:CL], wait_edges=[last_q[m]], inc=True)
        e_Q = b.fence(engines=("vector",))
        for do in range(DT):
            bk = b.rr(6)
            b.mm_group(bk, [(b.pbank[bk][:, 0:CL],
                             wv2[:, k * 1024 + do * 128:k * 1024 + do * 128 + 128],
                             Qsb[:, k * CL:(k + 1) * CL]) for k in range(2)],
                       wait_edges=[e_wv2] + e_Q)
            e_t = b.bank_read(bk, "vector", "tensor_scalar", tev,
                              b.pbank[bk][:, 0:CL], f2b[:, do:do + 1], None,
                              ALU.add)
            b.op("scalar", "activation", sgc[:, do * CL:(do + 1) * CL], tev,
                 AF.Silu, wait_edges=[e_t], inc=True)
        e_sgc = b.fence(engines=("scalar",))
        e_xln = ln_chunk(lambda d: xnc[:, d * CL:(d + 1) * CL],
                         lambda d: xln[:, d * CL:(d + 1) * CL],
                         lambda d: lnw[:, d:d + 1], lambda d: lnbc[:, d:d + 1],
                         [e_xnc])
        e_carry = []
        for dti in range(16):
            c0 = dti * (CL + 8)
            if cch == 0:
                e_c0 = b.op("vector", "memset", xcb[:, c0:c0 + 3], 0.0, inc=True)
            else:
                e_c0 = b.op("vector", "tensor_copy", xcb[:, c0:c0 + 3],
                            xcb[:, c0 + CL:c0 + CL + 3], inc=True)
            e_carry.append(e_c0)
        for slab in range(8):
            e_w = wload(b, winb, W["l3_win_t"][:, slab * 512:(slab + 1) * 512],
                        wait_edges=[e for e in (e_win_mm,) if e])
            for mt in range(4):
                gi = slab * 4 + mt
                bk = b.rr(6)
                e_win_mm = b.mm_group(bk, [
                    (b.pbank[bk][:, 0:CL],
                     winb[:, k * 512 + mt * 128:k * 512 + mt * 128 + 128],
                     xln[:, k * CL:(k + 1) * CL]) for k in range(8)],
                    wait_edges=[e_w, e_xln])
                if gi < 16:
                    b.bank_read(bk, "vector", "tensor_copy",
                                xcb[:, gi * (CL + 8) + 3:gi * (CL + 8) + 3 + CL],
                                b.pbank[bk][:, 0:CL],
                                wait_edges=[e_carry[gi]])
                else:
                    t_ = gi - 16
                    b.bank_read(bk, "scalar", "activation",
                                zs[:, t_ * CL:(t_ + 1) * CL],
                                b.pbank[bk][:, 0:CL], AF.Silu)
        e_xz = b.fence()
        e_conv = []
        for dti in range(16):
            c0 = dti * (CL + 8)
            b.op("vector", "tensor_scalar", tev, xcb[:, c0:c0 + CL],
                 cwc[:, dti * 4:dti * 4 + 1], None, ALU.mult,
                 wait_edges=e_xz if dti == 0 else [])
            for kk in range(1, 4):
                b.op("vector", "scalar_tensor_tensor", tev,
                     xcb[:, c0 + kk:c0 + kk + CL],
                     cwc[:, dti * 4 + kk:dti * 4 + kk + 1], tev,
                     ALU.mult, ALU.add)
            e_c = b.op("scalar", "activation", xconv[:, dti * CL:(dti + 1) * CL],
                       tev, AF.Silu, bias=cbc[:, dti:dti + 1], inc=True)
            e_conv.append(e_c)
        bkd = b.rr(6)
        b.mm_group(bkd, [(b.pbank[bkd][0:64, 0:CL], wxp[:, k * 96:k * 96 + 64],
                          xconv[:, k * CL:(k + 1) * CL]) for k in range(16)],
                   wait_edges=[e_wxp, e_conv[-1]])
        b.bank_read(bkd, "vector", "tensor_copy", prd[0:64, :],
                    b.pbank[bkd][0:64, 0:CL])
        bkB = b.rr(6)
        b.mm_group(bkB, [(b.pbank[bkB][0:16, 0:CL], wxp[:, k * 96 + 64:k * 96 + 80],
                          xconv[:, k * CL:(k + 1) * CL]) for k in range(16)],
                   wait_edges=[e_wxp])
        b.bank_read(bkB, "vector", "tensor_copy", prB[0:16, :],
                    b.pbank[bkB][0:16, 0:CL])
        bkC = b.rr(6)
        b.mm_group(bkC, [(b.pbank[bkC][0:16, 0:CL], wxp[:, k * 96 + 80:k * 96 + 96],
                          xconv[:, k * CL:(k + 1) * CL]) for k in range(16)],
                   wait_edges=[e_wxp])
        b.bank_read(bkC, "vector", "tensor_copy", prC[0:16, :],
                    b.pbank[bkC][0:16, 0:CL])
        e_pr = b.fence(engines=("vector",))
        e_brep = None
        for n in range(16):
            bk = b.rr(6)
            b.mm_group(bk, [(b.pbank[bk][:, 0:CL], sel16[:, n * 128:(n + 1) * 128],
                             prB[0:16, :])], wait_edges=e_pr + [e_sel])
            b.bank_read(bk, "scalar", "copy", Brep[:, n * CL:(n + 1) * CL],
                        b.pbank[bk][:, 0:CL])
            bk = b.rr(6)
            b.mm_group(bk, [(b.pbank[bk][:, 0:CL], sel16[:, n * 128:(n + 1) * 128],
                             prC[0:16, :])], wait_edges=e_pr)
            e_brep = b.bank_read(bk, "scalar", "copy", Crep[:, n * CL:(n + 1) * CL],
                                 b.pbank[bk][:, 0:CL])
        e_yg = None
        for dti in range(16):
            bk = b.rr(6)
            b.mm_group(bk, [(b.pbank[bk][:, 0:CL], wdt[0:64, dti * 128:(dti + 1) * 128],
                             prd[0:64, :])], wait_edges=[e_wdt] + e_pr)
            e_t0 = b.bank_read(bk, "vector", "tensor_scalar", sp1,
                               b.pbank[bk][:, 0:CL], bdtc[:, dti:dti + 1], None,
                               ALU.add)
            e_a1 = b.op("scalar", "activation", sp2, sp1, AF.Abs,
                        wait_edges=[e_t0], inc=True)
            b.op("scalar", "activation", sp2, sp2, AF.Exp, scale=-1.0)
            e_a2 = b.op("scalar", "activation", sp2, sp2, AF.Ln, bias=1.0,
                        inc=True)
            b.op("vector", "tensor_relu", dt, sp1, wait_edges=[e_a2])
            e_dt = b.op("vector", "tensor_add", dt, dt, sp2, inc=True)
            e_dtu = b.op("gpsimd", "tensor_mul", dtu, dt,
                         xconv[:, dti * CL:(dti + 1) * CL],
                         wait_edges=[e_dt, e_conv[dti]], inc=True)
            for nb in range(16 // NB):
                for j in range(NB):
                    n = nb * NB + j
                    b.op("scalar", "activation", dA[:, j * CL:(j + 1) * CL],
                         dt, AF.Exp, scale=asb[:, dti * 16 + n:dti * 16 + n + 1],
                         wait_edges=[e_dt, e_A], inc=True)
                dtu3 = dtu.rearrange("p (o t) -> p o t", o=1).broadcast_to(
                    (128, NB, CL))
                dBu3 = dBu.rearrange("p (o t) -> p o t", o=NB)
                Brep3 = Brep[:, nb * NB * CL:(nb + 1) * NB * CL].rearrange(
                    "p (o t) -> p o t", o=NB)
                e_dbu = b.op("vector", "tensor_tensor", dBu3, dtu3, Brep3,
                             ALU.mult, wait_edges=[e_dtu, e_brep], inc=True)
                e_scan = []
                for j in range(NB):
                    n = nb * NB + j
                    e_s = b.op("vector", "tensor_tensor_scan",
                               hsc[:, j * CL:(j + 1) * CL],
                               dA[:, j * CL:(j + 1) * CL],
                               dBu[:, j * CL:(j + 1) * CL],
                               state[:, dti * 16 + n:dti * 16 + n + 1],
                               ALU.mult, ALU.add,
                               wait_edges=[e_dbu, e_st0], inc=True)
                    e_scan.append(e_s)
                hsc3 = hsc.rearrange("p (o t) -> p o t", o=NB)
                c0 = dti * 16 + nb * NB
                b.op("vector", "tensor_copy",
                     state[:, c0:c0 + NB].rearrange("p (o t) -> p o t", o=NB),
                     hsc3[:, :, CL - 1:CL], wait_edges=e_scan)
                tmp3o = tmpb.rearrange("p (o t) -> p o t", o=NB)
                Crep3 = Crep[:, nb * NB * CL:(nb + 1) * NB * CL].rearrange(
                    "p (o t) -> p o t", o=NB)
                e_tm = b.op("gpsimd", "tensor_tensor", tmp3o, hsc3, Crep3,
                            ALU.mult, wait_edges=e_scan, inc=True)
                e_p12 = b.op("gpsimd", "tensor_add", pt12, tmpb[:, 0:2 * CL],
                             tmpb[:, 2 * CL:4 * CL], wait_edges=[e_tm], inc=True)
                if nb == 0:
                    e_y = b.op("vector", "tensor_add", yac, pt1, pt2,
                               wait_edges=[e_p12], inc=True)
                else:
                    b.op("vector", "tensor_add", pt1, pt1, pt2,
                         wait_edges=[e_p12])
                    e_y = b.op("vector", "tensor_add", yac, yac, pt1, inc=True)
            b.op("vector", "scalar_tensor_tensor", yac,
                 xconv[:, dti * CL:(dti + 1) * CL], dskc[:, dti:dti + 1], yac,
                 ALU.mult, ALU.add, wait_edges=[e_y])
            e_yg = b.op("vector", "tensor_mul", ygated[:, dti * CL:(dti + 1) * CL],
                        yac, zs[:, dti * CL:(dti + 1) * CL], inc=True)
        e_fin = b.fence(engines=("vector",))
        for do in range(DT):
            e_wo = wload(b, wob, W["l3_wout_t"][:, do * 128:(do + 1) * 128],
                         wait_edges=[e for e in (e_wo_mm,) if e])
            bk = b.rr(6)
            e_wo_mm = b.mm_group(bk, [
                (b.pbank[bk][:, 0:CL], wob[:, k * 128:(k + 1) * 128],
                 ygated[:, k * CL:(k + 1) * CL]) for k in range(16)],
                wait_edges=[e_wo] + e_fin)
            b.bank_read(bk, "vector", "tensor_add", tev, b.pbank[bk][:, 0:CL],
                        xnc[:, do * CL:(do + 1) * CL])
            b.op("vector", "tensor_mul", tev, tev, sgc[:, do * CL:(do + 1) * CL],
                 wait_edges=e_sgc)
            e_u = b.op("vector", "tensor_add", xT[do][:, sl], xT[do][:, sl], tev,
                       inc=True)
        e_upd = [e_u]
    return e_upd


# version 08
# speedup vs baseline: 1.0453x; 1.0453x over previous
import contextlib
import numpy as np
import ml_dtypes
import concourse.bass as bass
import concourse.mybir as mybir
from concourse.bass_utils import run_bass_kernel_spmd

F32 = mybir.dt.float32
BF16 = mybir.dt.bfloat16
AF = mybir.ActivationFunctionType
ALU = mybir.AluOpType
AX = mybir.AxisListType
bfdt = ml_dtypes.bfloat16

T = 2048
D = 1024
NT = T // 512
DT = D // 128
MAGIC = 12582912.0
ARENA_COLS = 71040

LAYERS = [0, 1, 2, 3]
STAGES = ('ffn', 'attn')
N_EXPERTS = 16

CH512 = [slice(c * 512, (c + 1) * 512) for c in range(NT)]


class Sem:
    def __init__(self, h):
        self.h = h
        self.n = 0

    def inc(self, k=1):
        self.n += k
        return self.n


class OpRec:
    __slots__ = ("engine", "build", "waits", "uid", "val", "consumed", "is_dma",
                 "dma_sem", "dst_ap", "needs_clear", "epoch", "clear_dep")

    def __init__(self, engine, build, uid, is_dma=False):
        self.engine = engine
        self.build = build
        self.waits = set()
        self.uid = uid
        self.val = None
        self.consumed = False
        self.is_dma = is_dma


def _ap_range(ap):
    try:
        t = ap.tensor
    except AttributeError:
        return None
    import concourse.mybir as mb
    esz = mb.dt.size(ap.dtype)
    dims = [list(x) for x in ap.ap]
    off = ap.offset
    if not isinstance(off, int):
        return None
    if not dims:
        return (t.name, 0, 1, off * esz, (off + 1) * esz)
    pstep, pcount = dims[0]
    if pstep <= 0:
        pstep = 1
    p0 = off // pstep if pstep > 1 else 0
    col0 = off - p0 * pstep
    span = 1
    for st, ct in dims[1:]:
        span += abs(st) * (ct - 1)
    return (t.name, p0, p0 + pcount, col0 * esz, (col0 + span) * esz)


class MK:
    ENGINES = ("sync", "scalar", "vector", "tensor", "gpsimd")
    ESEM = {"vector": "dve", "scalar": "act", "tensor": "te", "gpsimd": "pool",
            "sync": "sp"}

    def __init__(self, nc, es):
        self.nc = nc
        self.es = es
        self.recs = []
        self.consts = {}
        self.semh = {}
        for name in ("te", "dve", "act", "pool", "sp"):
            self.semh[name] = es.enter_context(nc.semaphore("sem_" + name))
        self.NSYNCSEM = 8
        for i in range(self.NSYNCSEM):
            self.semh[f"dmah{i}"] = es.enter_context(nc.semaphore(f"sem_dmah{i}"))
        self.NSWSEM = 64
        for i in range(self.NSWSEM):
            self.semh[f"dmas{i}"] = es.enter_context(nc.semaphore(f"sem_dmas{i}"))
        self._dmah_rr = 0
        self._dmah_prev = [None] * self.NSYNCSEM
        self._dmah_count = [0] * self.NSYNCSEM
        self._dmas_rr = 0
        self._dmas_prev = [None] * 64
        self._dmas_count = [0] * 64
        self._epoch = 0
        self._ep_bar = {}
        self.psum = es.enter_context(nc.psum_tensor("pb", [128, 4096], F32))
        _pb = self.psum.ap() if hasattr(self.psum, "ap") else self.psum
        self.pbank = [_pb[:, i * 512:(i + 1) * 512] for i in range(8)]
        self.pbig = _pb
        _pbb = _pb.bitcast(BF16)
        self.pbank_bf = [_pbb[:, i * 1024:(i + 1) * 1024] for i in range(8)]
        self._rr = 0
        self._uid = 0
        self.segs = {}
        self.track_dram = False
        self.last_op = {e: None for e in self.ENGINES}

    def sb(self, name, shape, dtype):
        return self.es.enter_context(self.nc.sbuf_tensor(name, shape, dtype))

    def rr(self, mod=8):
        i = self._rr % mod
        self._rr += 1
        return i

    def const_input(self, name, arr):
        arr = np.ascontiguousarray(arr)
        self.consts[name] = arr
        dt = {np.dtype(np.float32): F32, np.dtype(bfdt): BF16}[arr.dtype]
        return self.nc.dram_tensor(name, list(arr.shape), dt, kind="ExternalInput").ap()

    def _dep_read(self, rec, ap):
        r = _ap_range(ap)
        if r is None:
            return
        name, p0, p1, b0, b1 = r
        if name not in self.segs:
            if not name.startswith(("arena", "xt", "pb", "c_", "scr_")):
                return
            self.segs[name] = []
        for seg in self.segs[name]:
            if seg[0] < p1 and p0 < seg[1] and seg[2] < b1 and b0 < seg[3]:
                if seg[4] is not None and seg[4] is not rec:
                    rec.waits.add(seg[4])
                seg[5].append(rec)

    def _dep_write(self, rec, ap, writer=None):
        r = _ap_range(ap)
        if r is None:
            return
        name, p0, p1, b0, b1 = r
        if name not in self.segs:
            if not name.startswith(("arena", "xt", "pb", "c_", "scr_")):
                return
            self.segs[name] = []
        wrec = writer if writer is not None else rec
        keep = []
        for seg in self.segs[name]:
            if seg[0] < p1 and p0 < seg[1] and seg[2] < b1 and b0 < seg[3]:
                if seg[4] is not None and seg[4] is not rec and seg[4] is not wrec:
                    rec.waits.add(seg[4])
                for rd in seg[5]:
                    if rd is not rec and rd is not wrec:
                        rec.waits.add(rd)
                if not (p0 <= seg[0] and seg[1] <= p1 and b0 <= seg[2]
                        and seg[3] <= b1):
                    keep.append(seg)
            else:
                keep.append(seg)
        keep.append([p0, p1, b0, b1, wrec, []])
        self.segs[name] = keep

    def _mkrec(self, engine, build, is_dma=False):
        self._uid += 1
        rec = OpRec(engine, build, self._uid, is_dma)
        rec.dst_ap = None
        rec.needs_clear = False
        rec.clear_dep = None
        rec.epoch = self._epoch
        bar = self._ep_bar.get(self._epoch - 1)
        if bar is not None and engine in bar:
            rec.waits.add(bar[engine])
        self.recs.append(rec)
        self.last_op[engine] = rec
        return rec

    def _dep_war(self, rec, ap):
        r = _ap_range(ap)
        if r is None or r[0] not in self.segs:
            return
        name, p0, p1, b0, b1 = r
        for seg in self.segs[name]:
            if seg[0] < p1 and p0 < seg[1] and seg[2] < b1 and b0 < seg[3]:
                if seg[4] is not None and seg[4] is not rec:
                    rec.waits.add(seg[4])
                for rd in seg[5]:
                    if rd is not rec:
                        rec.waits.add(rd)

    def dma(self, out_ap, in_ap, queue="sync", wait_edges=()):
        def build(e, o=out_ap, i=in_ap):
            return e.dma_start(out=o, in_=i)
        rec = self._mkrec(queue, build, is_dma=True)
        rec.dst_ap = out_ap
        if queue == "sync":
            si = self._dmah_rr % self.NSYNCSEM
            self._dmah_rr += 1
            rec.dma_sem = f"dmah{si}"
            self._dmah_count[si] += 16
            rec.val = self._dmah_count[si]
            prev = self._dmah_prev[si]
            if prev is not None:
                rec.waits.add(prev)
            self._dmah_prev[si] = rec
        else:
            si = self._dmas_rr % self.NSWSEM
            self._dmas_rr += 1
            rec.dma_sem = f"dmas{si}"
            self._dmas_count[si] += 16
            rec.val = self._dmas_count[si]
            prev = self._dmas_prev[si]
            if prev is not None:
                rec.waits.add(prev)
                self._dep_war(rec, prev.dst_ap)
            self._dmas_prev[si] = rec
        for w in wait_edges:
            if w is not None:
                rec.waits.add(w)
        self._dep_read(rec, in_ap)
        self._dep_write(rec, out_ap)
        rec.consumed = True
        return rec

    def op(self, engine, method, *args, wait_edges=(), inc=False, **kwargs):
        def build(e, m=method, a=args, k=kwargs):
            return getattr(e, m)(*a, **k)
        rec = self._mkrec(engine, build)
        for w in wait_edges:
            if w is not None:
                rec.waits.add(w)
        aps = [a for a in args if isinstance(a, bass.AP)]
        for a in aps[1:]:
            self._dep_read(rec, a)
        for kw, v in kwargs.items():
            if isinstance(v, bass.AP) and kw != "accum_out":
                self._dep_read(rec, v)
        if method != "nop" and aps:
            self._dep_write(rec, aps[0])
        ao = kwargs.get("accum_out")
        if isinstance(ao, bass.AP):
            self._dep_write(rec, ao)
        return rec

    def fence(self, engines=("tensor", "vector", "scalar", "gpsimd")):
        return [self.last_op[e] for e in engines if self.last_op[e] is not None]

    def mark(self, name):
        if not hasattr(self, "marks"):
            self.marks = []
            self._marksb = self.sb("marksb", [1, 4], F32)
        self.marks.append(name)
        self.op("gpsimd", "memset", self._marksb[0:1, 0:4], 0.0)

    def phase_barrier(self, extra_edges=()):
        edges = self.fence() + list(extra_edges)
        edges += [r for r in self._dmah_prev if r is not None]
        edges += [r for r in self._dmas_prev if r is not None]
        recs = []
        bar = {}
        for eng in ("tensor", "vector", "scalar", "gpsimd", "sync"):
            r = self.op(eng, "nop", wait_edges=edges)
            recs.append(r)
            bar[eng] = r
        self._ep_bar[self._epoch] = bar
        self._epoch += 1
        return recs

    def mm_group(self, bank_i, mms, wait_edges=()):
        n = len(mms)
        recs = []
        for j, (out_ap, lhsT, rhs) in enumerate(mms):
            start, stop = (j == 0), (j == n - 1)
            def build(e, o=out_ap, l=lhsT, r=rhs, st=start, sp=stop):
                return e.matmul(o, l, r, start=st, stop=sp)
            rec = self._mkrec("tensor", build)
            if j == 0:
                for w in wait_edges:
                    if w is not None:
                        rec.waits.add(w)
            self._dep_read(rec, lhsT)
            self._dep_read(rec, rhs)
            recs.append(rec)
        for out_ap in {id(m[0]): m[0] for m in mms}.values():
            self._dep_write(recs[0], out_ap, writer=recs[-1])
        return recs[-1]

    def mm_step(self, out_ap, lhsT, rhs, start, stop, wait_edges=()):
        def bld(e, o=out_ap, l=lhsT, r=rhs, st=start, sp=stop):
            return e.matmul(o, l, r, start=st, stop=sp)
        rec = self._mkrec("tensor", bld)
        for w in wait_edges:
            if w is not None:
                rec.waits.add(w)
        self._dep_read(rec, lhsT)
        self._dep_read(rec, rhs)
        self._dep_write(rec, out_ap)
        return rec

    def bank_te(self, bank_i, method, *args, wait_edges=(), **kwargs):
        return self.op("tensor", method, *args, wait_edges=wait_edges, **kwargs)

    def bank_read(self, bank_i, engine, method, *args, wait_edges=(), **kwargs):
        return self.op(engine, method, *args, wait_edges=wait_edges, **kwargs)

    def finalize(self):
        for rec in self.recs:
            for w in rec.waits:
                w.consumed = True
        for bar in self._ep_bar.values():
            for r in bar.values():
                r.consumed = True
        counters = {}
        semname = {}
        for rec in self.recs:
            if rec.is_dma:
                semname[rec.uid] = rec.dma_sem
                rec.consumed = True
                continue
            sname = self.ESEM[rec.engine]
            semname[rec.uid] = sname
            if rec.consumed:
                counters[sname] = counters.get(sname, 0) + 1
                rec.val = counters[sname]
        hw = {e: {} for e in self.ENGINES}
        plan = []
        for rec in self.recs:
            needs = {}
            for w in rec.waits:
                if w.val is None:
                    continue
                sname = semname[w.uid]
                if w.engine == rec.engine and not w.is_dma and not rec.is_dma:
                    pass
                cur = needs.get(sname, 0)
                if w.val > cur:
                    needs[sname] = w.val
            waits = []
            for sname, val in needs.items():
                if sname.startswith("dma"):
                    waits.append((self.semh[sname], val))
                    continue
                if hw[rec.engine].get(sname, 0) >= val:
                    continue
                hw[rec.engine][sname] = val
                waits.append((self.semh[sname], val))
            plan.append((rec, waits))
        block = self.es.enter_context(self.nc.Block())
        for name in self.ENGINES:
            def runner(eng, _n=name):
                for rec, waits in plan:
                    if rec.engine != _n:
                        continue
                    if rec.is_dma:
                        for sh, val in waits:
                            eng.wait_ge(sh, val)
                        inst = rec.build(eng)
                    else:
                        for sh, val in waits[1:]:
                            eng.wait_ge(sh, val)
                        inst = rec.build(eng)
                        if waits:
                            inst._wait_ge(waits[0][0], waits[0][1])
                    if rec.val is not None:
                        sname = semname[rec.uid]
                        inst.then_inc(self.semh[sname], 16 if rec.is_dma else 1)
            getattr(block, name)(runner)
        self.final_counts = counters


class Plan:

    def __init__(self, ar_ap, base=0):
        self.ar = ar_ap
        self.p = base

    def bf(self, ncols):
        off = (self.p + 15) & ~15
        self.p = off + ncols
        assert self.p <= ARENA_COLS, f"arena overflow: {self.p} > {ARENA_COLS}"
        return self.ar[:, off:off + ncols]

    def f32(self, ncols):
        off = (self.p + 15) & ~15
        self.p = off + 2 * ncols
        assert self.p <= ARENA_COLS, f"arena overflow: {self.p} > {ARENA_COLS}"
        return self.ar[:, off:off + 2 * ncols].bitcast(F32)



FCONSTS = [-20.0, 0.125, 1e-5, 1.0, 0.5, -1.0, 2.0, -0.5, 127.0, -128.0,
           12582912.0, -12582912.0, -3.0e38, 1e-12, 1e-5 ** 0.5]


def host_consts():
    c = {}
    c["fpool"] = np.tile(np.array(FCONSTS, np.float32)[None, :], (128, 1))
    c["ident_f32"] = np.eye(128, dtype=np.float32)
    c["ident_bf"] = np.eye(128, dtype=bfdt)
    c["ones_f32_128x1"] = np.ones((128, 1), dtype=np.float32)
    c["ones_bf_128x1"] = np.ones((128, 1), dtype=bfdt)
    c["ones_f32_1x128"] = np.ones((1, 128), dtype=np.float32)
    ROT, BASE = 32, 10000.0
    inv = 1.0 / BASE ** (np.arange(0, ROT, 2, dtype=np.float64) / ROT)
    fr = np.arange(T, dtype=np.float64)[:, None] * inv[None, :]
    cos, sin = np.cos(fr), np.sin(fr)
    COS = np.ones((128, T), dtype=np.float64)
    SIN = np.zeros((128, T), dtype=np.float64)
    for base in (0, 64):
        COS[base:base + 16] = cos.T
        COS[base + 16:base + 32] = cos.T
        SIN[base:base + 16] = -sin.T
        SIN[base + 16:base + 32] = sin.T
    c["l0_cos"] = COS.astype(bfdt)
    c["l0_sin"] = SIN.astype(bfdt)
    inv1 = 1.0 / 10000.0 ** (np.arange(0, 64, 2, dtype=np.float64) / 64)
    fr1 = np.arange(T, dtype=np.float64)[:, None] * inv1[None, :]
    c1, s1_ = np.cos(fr1), np.sin(fr1)
    COS1 = np.zeros((128, T), dtype=np.float64)
    SIN1 = np.zeros((128, T), dtype=np.float64)
    for r in range(128):
        p = (r % 64) // 2
        COS1[r] = c1[:, p]
        SIN1[r] = -s1_[:, p] if (r % 2 == 0) else s1_[:, p]
    c["l1_cos"] = COS1.astype(bfdt)
    c["l1_sin"] = SIN1.astype(bfdt)
    e64 = np.zeros((2, 128), dtype=np.float32)
    e64[0, 0:64] = 1.0
    e64[1, 64:128] = 1.0
    c["e64"] = e64
    c["e64_t"] = np.ascontiguousarray(e64.T)
    tri = np.triu(np.ones((128, 128), np.float32))
    c["l2_triu"] = tri.astype(bfdt)
    R64 = np.zeros((128, 128), np.float32)
    for blk in (0, 64):
        for i in range(32):
            R64[blk + 32 + i, blk + i] = -1.0
            R64[blk + i, blk + 32 + i] = 1.0
    c["l2_rot64"] = R64.astype(bfdt)
    band = np.full((128, 1408), -1e9, np.float32)
    for i in range(128):
        band[i, i + 384:i + 897] = 0.0
    c["l2_band"] = band.astype(bfdt)
    DH, WIN = 64, 512
    half_idx = np.arange(0, DH, 2, dtype=np.float64)
    invf = 1.0 / 10000.0 ** (half_idx / DH)
    tpos2 = np.arange(T, dtype=np.float64)
    fr2 = tpos2[:, None] * invf[None, :]
    fr2 = np.concatenate([fr2, fr2], axis=-1)
    base2 = (half_idx + 0.4 * DH) / (1.4 * DH)
    power = (tpos2 - T // 2) / (WIN // 2)
    xsc = base2[None, :] ** power[:, None]
    xsc = np.concatenate([xsc, xsc], axis=-1)
    cos2, sin2 = np.cos(fr2), np.sin(fr2)
    def tile128(a):
        return np.concatenate([a.T, a.T], axis=0)
    c["l2_cosq"] = (tile128(cos2 * xsc) * 8.0).astype(bfdt)
    c["l2_sinq"] = (tile128(sin2 * xsc) * 8.0).astype(bfdt)
    c["l2_cosk"] = tile128(cos2 / xsc).astype(bfdt)
    c["l2_sink"] = tile128(sin2 / xsc).astype(bfdt)
    sel = np.zeros((16, 2048), np.float32)
    for n in range(16):
        sel[n, n * 128:(n + 1) * 128] = 1.0
    c["sel16"] = sel.astype(bfdt)
    mv = np.zeros((4, 128, 512), dtype=np.float32)
    for v in range(4):
        qi = v * 128 + np.arange(128)[:, None]
        mv[v] = np.where(np.arange(512)[None, :] <= qi, 0.0, -1e9)
    c["l0_mask"] = np.ascontiguousarray(mv.transpose(1, 0, 2).reshape(128, 4 * 512)).astype(bfdt)
    return c



def emit_layernorm(b, g, xT, xn_tiles, pl, in_edges=()):
    ones_f = g["ones_f32_128x1"]
    onesr_f = g["ones_f32_1x128"]
    rows = pl.f32(2 * T)
    sqw = pl.f32(512 * DT)
    rep = pl.f32(2 * T)
    tmp = pl.f32(T)
    A = rows[0:1, 0:T]
    B = rows[0:1, T:2 * T]
    tmp2 = rep[0:1, 0:T]
    e_sq_mm = None
    for cch, sl in enumerate(CH512):
        ie = list(in_edges) if cch == 0 else []
        b.mm_group(0, [(b.pbank[0][0:1, :], ones_f[:, :], xT[d][:, sl])
                       for d in range(DT)], wait_edges=ie)
        e_list = [b.op("scalar", "activation", sqw[:, d * 512:(d + 1) * 512],
                       xT[d][:, sl], AF.Square,
                       wait_edges=ie + ([e_sq_mm] if e_sq_mm else []), inc=True)
                  for d in range(DT)]
        e_sq_mm = b.mm_group(1, [(b.pbank[1][0:1, :], ones_f[:, :],
                                  sqw[:, d * 512:(d + 1) * 512]) for d in range(DT)],
                             wait_edges=[e_list[-1]])
        b.bank_read(0, "vector", "tensor_copy", A[:, sl], b.pbank[0][0:1, :])
        e_ab = b.bank_read(1, "vector", "tensor_copy", B[:, sl], b.pbank[1][0:1, :])
    b.op("vector", "tensor_scalar", A, A, -1.0 / D, None, ALU.mult,
         wait_edges=[e_ab])
    b.op("vector", "tensor_scalar", B, B, 1.0 / D, None, ALU.mult)
    b.op("vector", "tensor_mul", tmp2, A, A)
    b.op("vector", "tensor_sub", B, B, tmp2)
    e_var = b.op("vector", "tensor_scalar", B, B, 1e-5, None, ALU.add, inc=True)
    e_sqr = b.op("scalar", "activation", B, B, AF.Sqrt,
                 wait_edges=[e_var], inc=True)
    b.op("vector", "reciprocal", B, B, wait_edges=[e_sqr])
    e_stats = b.op("vector", "tensor_mul", A, A, B, inc=True)
    rstd, nmr = B, A
    rstd_rep, nmr_rep = rep[:, 0:T], rep[:, T:2 * T]
    e_rep = None
    for cch, sl in enumerate(CH512):
        b.mm_group(2, [(b.pbank[2][:, :], onesr_f[:, :], rstd[:, sl])],
                   wait_edges=[e_stats])
        b.bank_read(2, "scalar", "copy", rstd_rep[:, sl], b.pbank[2][:, :])
        b.mm_group(3, [(b.pbank[3][:, :], onesr_f[:, :], nmr[:, sl])],
                   wait_edges=[e_stats])
        e_rep = b.bank_read(3, "scalar", "copy", nmr_rep[:, sl], b.pbank[3][:, :])
    edges = []
    for d in range(DT):
        b.op("vector", "tensor_mul", tmp[:, :], xT[d][:, :], rstd_rep[:, :],
             wait_edges=[e_rep] if d == 0 else [])
        e = b.op("vector", "tensor_add", xn_tiles[d][:, :], tmp[:, :],
                 nmr_rep[:, :], inc=True)
        edges.append(e)
    return edges


def wload(b, dst_ap2d, src_ap2d, queue="gpsimd", wait_edges=()):
    rows, cols = src_ap2d.shape
    a = rows // 128
    if a == 1:
        return b.dma(dst_ap2d, src_ap2d, queue=queue, wait_edges=wait_edges)
    src3 = src_ap2d.rearrange("(a p) n -> p a n", p=128)
    dst3 = dst_ap2d.rearrange("p (a n) -> p a n", a=a)
    return b.dma(dst3, src3, queue=queue, wait_edges=wait_edges)



def emit_l0_ffn(b, g, W, xn, sg, pl, xn_edges):
    onesr_f = g["ones_f32_1x128"]
    idf = g["ident_f32"]
    wg = pl.bf(DT * 16)
    lg = pl.f32(T)
    lt = pl.f32(16 * 16)
    cwt = pl.f32(16 * 16)
    cwf = lg
    sm = pl.f32(16)
    expv = pl.f32(16)
    msk = pl.f32(16)
    msk2 = pl.f32(16)
    ninf = pl.f32(16)
    w1buf = pl.bf(8 * 1024)
    w2buf = pl.bf(4 * 1024)
    mh = pl.bf(4 * T)
    gs = pl.bf(512)
    hbuf = pl.bf(512)
    cwrep = pl.bf(T)
    wsu = pl.bf(8 * 128)
    wsg = pl.bf(8 * 128)

    do_router = "router" in STAGES or "ffn" in STAGES
    do_experts = "experts" in STAGES or "ffn" in STAGES
    do_shared = "shared" in STAGES or "ffn" in STAGES
    n_experts = N_EXPERTS
    e_wg = wload(b, wg, W["l0_wg_t"])
    b.op("vector", "memset", ninf, -3.0e38)
    for d in range(DT):
        b.op("vector", "memset", sg[d][:, :], 0.0)
    e_zero = b.op("vector", "nop", inc=True)

    LV = globals().get("RLEVEL", 9)
    for cch, sl in enumerate(CH512) if (do_router and LV >= 2) else []:
        bk = b.rr()
        b.mm_group(bk, [(b.pbank[bk][0:16, :], wg[:, k * 16:(k + 1) * 16],
                         xn[k][:, sl]) for k in range(DT)],
                   wait_edges=[e_wg] + list(xn_edges))
        b.bank_read(bk, "scalar", "copy", lg[0:16, sl], b.pbank[bk][0:16, :])
    for ti in range(16) if (do_router and LV >= 3) else []:
        bk = b.rr()
        e = b.bank_te(bk, "transpose", b.pbank[bk][:, 0:16],
                      lg[0:16, ti * 128:(ti + 1) * 128], idf[0:16, 0:16])
        b.bank_read(bk, "vector", "tensor_copy", lt[:, ti * 16:(ti + 1) * 16],
                    b.pbank[bk][:, 0:16])
    e_cwt = []
    for ti in range(16) if (do_router and LV >= 4) else []:
        lt_t = lt[:, ti * 16:(ti + 1) * 16]
        b.op("vector", "reduce_max", sm[:, 0:1], lt_t, AX.X)
        e_nm = b.op("vector", "tensor_scalar", sm[:, 1:2], sm[:, 0:1], -1.0, None,
                    ALU.mult, inc=True)
        e_exp = b.op("scalar", "activation", expv, lt_t, AF.Exp, bias=sm[:, 1:2],
                     accum_out=sm[:, 2:3], wait_edges=[e_nm], inc=True)
        b.op("vector", "reciprocal", sm[:, 3:4], sm[:, 2:3], wait_edges=[e_exp])
        b.op("vector", "tensor_scalar", msk, lt_t, sm[:, 0:1], None, ALU.is_ge)
        b.op("vector", "scalar_tensor_tensor", msk2, msk, -1.0e9, lt_t,
             ALU.mult, ALU.add)
        b.op("vector", "reduce_max", sm[:, 4:5], msk2, AX.X)
        b.op("vector", "tensor_scalar", msk, lt_t, sm[:, 4:5], None, ALU.is_ge)
        b.op("vector", "tensor_mul", expv, expv, msk)
        e = b.op("vector", "tensor_scalar", cwt[:, ti * 16:(ti + 1) * 16], expv,
                 sm[:, 3:4], None, ALU.mult, inc=True)
        e_cwt.append(e)
    e_cwf = [e_cwt[-1]] if e_cwt else []

    e_up_last = None
    e_dn_last = None
    e_mh_last = None
    for e_i in range(n_experts if do_experts else 0):
        we1 = [e for e in (e_up_last,) if e]
        e_w1 = wload(b, w1buf, W["l0_w1"][e_i * 1024:(e_i + 1) * 1024, :], wait_edges=we1)
        e_w2 = wload(b, w2buf, W["l0_w2"][e_i * 512:(e_i + 1) * 512, :],
                     wait_edges=[e for e in (e_dn_last,) if e])
        e_cwrep = None
        for ti in range(16):
            bk = b.rr()
            lhs = cwt[:, ti * 16 + e_i: ti * 16 + e_i + 1].broadcast_to((128, 128))
            b.mm_group(bk, [(b.pbank[bk][:, 0:128], lhs, idf)],
                       wait_edges=e_cwf)
            e_cwrep = b.bank_read(bk, "scalar", "copy",
                                  cwrep[:, ti * 128:(ti + 1) * 128],
                                  b.pbank[bk][:, 0:128])
        for mp in range(4):
            for cch, sl in enumerate(CH512):
                bkU = b.rr(); bkG = b.rr()
                e_up_last = b.mm_group(bkU, [
                    (b.pbank[bkU][:, :], w1buf[:, k * 1024 + mp * 128:k * 1024 + mp * 128 + 128],
                     xn[k][:, sl]) for k in range(DT)], wait_edges=[e_w1])
                e_up_last = b.mm_group(bkG, [
                    (b.pbank[bkG][:, :], w1buf[:, k * 1024 + 512 + mp * 128:k * 1024 + 512 + mp * 128 + 128],
                     xn[k][:, sl]) for k in range(DT)], wait_edges=[e_w1])
                e_gs = b.bank_read(bkG, "scalar", "activation", gs, b.pbank[bkG][:, :],
                                   AF.Silu)
                b.bank_read(bkU, "vector", "scalar_tensor_tensor", hbuf,
                            b.pbank[bkU][:, :], 1.0, gs, ALU.mult, ALU.mult,
                            wait_edges=[e_gs, e_cwrep] +
                                       ([e_dn_last] if e_dn_last else []))
                e_mh_last = b.op("vector", "tensor_mul", mh[:, mp * T + sl.start: mp * T + sl.stop],
                                 hbuf, cwrep[:, sl], inc=True)
        for do in range(8):
            for cch, sl in enumerate(CH512):
                bk = b.rr()
                e_dn_last = b.mm_group(bk, [
                    (b.pbank[bk][:, :], w2buf[:, hk * 1024 + do * 128:hk * 1024 + do * 128 + 128],
                     mh[:, hk * T + sl.start: hk * T + sl.stop]) for hk in range(4)],
                    wait_edges=[e_w2, e_mh_last])
                b.bank_read(bk, "vector", "scalar_tensor_tensor", sg[do][:, sl],
                            b.pbank[bk][:, :], 1.0, sg[do][:, sl], ALU.mult, ALU.add)
    for half in range(2 if do_shared else 0):
        e_hs_last = None
        for mpl in range(4):
            mp = half * 4 + mpl
            e_wu = wload(b, wsu, W["l0_ws1"][:, mp * 128:(mp + 1) * 128],
                         wait_edges=[e for e in (e_up_last,) if e])
            e_wg2 = wload(b, wsg, W["l0_ws1"][:, 1024 + mp * 128:1024 + (mp + 1) * 128],
                          wait_edges=[e for e in (e_up_last,) if e])
            for cch, sl in enumerate(CH512):
                bkU = b.rr(); bkG = b.rr()
                b.mm_group(bkU, [(b.pbank[bkU][:, :], wsu[:, k * 128:(k + 1) * 128],
                                  xn[k][:, sl]) for k in range(DT)], wait_edges=[e_wu])
                e_up_last = b.mm_group(bkG, [(b.pbank[bkG][:, :], wsg[:, k * 128:(k + 1) * 128],
                                              xn[k][:, sl]) for k in range(DT)],
                                       wait_edges=[e_wg2])
                e_gs = b.bank_read(bkG, "scalar", "activation", gs,
                                   b.pbank[bkG][:, :], AF.Silu)
                e_hs_last = b.bank_read(bkU, "vector", "tensor_copy", hbuf,
                                        b.pbank[bkU][:, :])
                e_hs_last = b.op("vector", "tensor_mul",
                                 mh[:, mpl * T + sl.start: mpl * T + sl.stop],
                                 hbuf, gs, wait_edges=[e_gs], inc=True)
        e_ws2 = wload(b, w2buf, W["l0_ws2"][half * 512:(half + 1) * 512, :],
                      wait_edges=[e for e in (e_dn_last,) if e])
        for do in range(8):
            for cch, sl in enumerate(CH512):
                bk = b.rr()
                e_dn_last = b.mm_group(bk, [
                    (b.pbank[bk][:, :], w2buf[:, hk * 1024 + do * 128:hk * 1024 + do * 128 + 128],
                     mh[:, hk * T + sl.start: hk * T + sl.stop]) for hk in range(4)],
                    wait_edges=[e_ws2, e_hs_last])
                b.bank_read(bk, "vector", "scalar_tensor_tensor", sg[do][:, sl],
                            b.pbank[bk][:, :], 1.0, sg[do][:, sl], ALU.mult, ALU.add)
    e_z = b.fence(engines=("vector",))
    edges = [b.op("scalar", "activation", sg[d][:, :], sg[d][:, :], AF.Silu,
                  wait_edges=e_z, inc=True) for d in range(DT)]
    return edges



SWAP16 = list(range(16, 32)) + list(range(0, 16))


def emit_l0_attn(b, g, W, xn, xT, sg, pl, in_edges):
    o_all = pl.bf(8 * T)
    mark = pl.p
    qk = pl.bf(2 * T)
    v_tok = pl.bf(16 * 128)
    P = pl.bf(T)
    PT = pl.bf(16 * 128)
    qraw = pl.bf(512)
    shf = pl.bf(512)
    t1 = pl.bf(512)
    t2 = pl.bf(512)
    sums = pl.f32(16)
    diag = pl.bf(128)
    wqb = pl.bf(8 * 128)
    wkb = pl.bf(8 * 128)
    wvb = pl.bf(8 * 128)
    cosT = pl.bf(T)
    sinT = pl.bf(T)
    masks = pl.bf(4 * 512)
    e_ct = [b.dma(cosT, W["ct_l0_cos"], queue="sync"),
            b.dma(sinT, W["ct_l0_sin"], queue="sync"),
            b.dma(masks, W["ct_l0_mask"], queue="sync")]
    idb = g["ident_bf"]
    e_q_mm = None
    e_v_evac = None
    e_pv_last = None
    e_tr_last = None
    e_sc_last = None
    e_sum = None
    e_qread = None
    for pair in range(8):
        wed = [e for e in (e_q_mm,) if e]
        e_wq = wload(b, wqb, W["l0_wq"][:, pair * 128:(pair + 1) * 128], wait_edges=wed)
        e_wk = wload(b, wkb, W["l0_wkv"][:, pair * 128:(pair + 1) * 128], wait_edges=wed)
        e_wv = wload(b, wvb, W["l0_wkv"][:, 1024 + pair * 128:1024 + (pair + 1) * 128],
                     wait_edges=wed)
        for ti in range(16):
            bk = b.rr(7)
            b.mm_group(bk, [(b.pbank[bk][:, 0:128], xn[k][:, ti * 128:(ti + 1) * 128],
                             wvb[:, k * 128:(k + 1) * 128]) for k in range(DT)],
                       wait_edges=[e_wv] + list(in_edges) +
                                  ([e_pv_last] if (e_pv_last and ti == 0) else []))
            e_v_evac = b.bank_read(bk, "vector", "tensor_copy",
                                   v_tok[:, ti * 128:(ti + 1) * 128], b.pbank[bk][:, 0:128])
        for wbuf, e_w, dst_off in ((wqb, e_wq, 0), (wkb, e_wk, T)):
            for cch, sl in enumerate(CH512):
                bk = b.rr(7)
                e_q_mm = b.mm_group(bk, [(b.pbank[bk][:, :], wbuf[:, k * 128:(k + 1) * 128],
                                          xn[k][:, sl]) for k in range(DT)],
                                    wait_edges=[e_w])
                e_r = b.bank_read(bk, "scalar", "copy", qraw, b.pbank[bk][:, :],
                                  wait_edges=[e_qread] if e_qread else [])
                b.op("vector", "stream_shuffle", shf, qraw, SWAP16, wait_edges=[e_r])
                e_qread = b.op("vector", "tensor_mul", t1, qraw, cosT[:, sl], inc=True)
                b.op("vector", "tensor_mul", t2, shf, sinT[:, sl])
                dsl = slice(dst_off + sl.start, dst_off + sl.stop)
                b.op("vector", "tensor_add", qk[:, dsl], t1, t2, inc=True,
                     wait_edges=[e_sc_last] if e_sc_last else [])
        e_rope = b.fence(engines=("vector",))
        if pair == 0 and globals().get("DBG_L1"):
            QKD = b.nc.dram_tensor("qk_dbg", [128, 2 * T], BF16,
                                   kind="ExternalOutput").ap()
            b.dma(QKD, qk, queue="sync", wait_edges=e_rope)
        for hh in range(2):
            h = 2 * pair + hh
            r0 = hh * 64
            qrows = qk[r0:r0 + 64, 0:T]
            krows = qk[r0:r0 + 64, T:2 * T]
            pvbk = 7
            for ti in range(16):
                cdiag = (ti * 128) // 512
                e_exp = None
                for c in range(cdiag + 1):
                    bk = b.rr(7)
                    mms = [(b.pbank[bk][:, :], qrows[:, ti * 128:(ti + 1) * 128],
                            krows[:, c * 512:(c + 1) * 512])]
                    if c == cdiag:
                        v = ti % 4
                        mms.append((b.pbank[bk][:, :], idb,
                                    masks[:, v * 512:(v + 1) * 512]))
                    e_sc_last = b.mm_group(bk, mms,
                                           wait_edges=list(e_rope) if (ti == 0 and c == 0) else ())
                    ew = [e for e in (e_tr_last, e_sum) if e]
                    e_exp = b.bank_read(bk, "scalar", "activation",
                                        P[:, c * 512:(c + 1) * 512], b.pbank[bk][:, :],
                                        AF.Exp, bias=-20.0, scale=0.125,
                                        accum_out=sums[:, c:c + 1], wait_edges=ew)
                e_sum = b.op("vector", "reduce_sum", sums[:, 8:9],
                             sums[:, 0:cdiag + 1], AX.X, wait_edges=[e_exp])
                b.op("vector", "reciprocal", sums[:, 9:10], sums[:, 8:9])
                e_diag = b.op("vector", "tensor_scalar", diag, idb, sums[:, 9:10],
                              None, ALU.mult, inc=True,
                              wait_edges=[e_tr_last] if e_tr_last else [])
                ntj = ti + 1
                for grp in range(0, ntj, 4):
                    bk = b.rr(7)
                    nj = min(4, ntj - grp)
                    for j in range(nj):
                        b.mm_group(bk, [(b.pbank[bk][:, j * 128:(j + 1) * 128],
                                         P[:, (grp + j) * 128:(grp + j + 1) * 128], diag)],
                                   wait_edges=[e_diag] if j == 0 else ())
                    e_tr_last = b.bank_read(bk, "vector", "tensor_copy",
                                            PT[:, grp * 128:grp * 128 + nj * 128],
                                            b.pbank[bk][:, 0:nj * 128])
                b.mm_group(pvbk, [(b.pbank[pvbk][0:64, (ti % 4) * 128:(ti % 4) * 128 + 128],
                                   v_tok[:, tjt * 128 + r0:tjt * 128 + r0 + 64],
                                   PT[:, tjt * 128:(tjt + 1) * 128])
                                  for tjt in range(ntj)],
                           wait_edges=[e_tr_last, e_v_evac])
                if ti % 4 == 3:
                    blk = ti // 4
                    e_pv_last = b.bank_read(
                        pvbk, "vector", "tensor_copy",
                        o_all[r0:r0 + 64, pair * T + blk * 512: pair * T + (blk + 1) * 512],
                        b.pbank[pvbk][0:64, :])
    pl.p = mark
    e_all = b.fence()
    wob = pl.bf(8 * 1024)
    upd = pl.f32(512)
    e_wo = wload(b, wob, W["l0_wout"], wait_edges=e_all)
    e_upd = None
    for do in range(DT):
        for cch, sl in enumerate(CH512):
            bk = b.rr(7)
            b.mm_group(bk, [(b.pbank[bk][:, :], wob[:, kp * 1024 + do * 128:kp * 1024 + do * 128 + 128],
                             o_all[:, kp * T + sl.start: kp * T + sl.stop])
                            for kp in range(8)], wait_edges=[e_wo])
            b.bank_read(bk, "vector", "tensor_mul", upd, b.pbank[bk][:, :],
                        sg[do][:, sl], wait_edges=list(in_edges))
            e_upd = b.op("vector", "tensor_add", xT[do][:, sl], xT[do][:, sl], upd,
                         inc=True)
    return [e_upd]



def build_program(layers=(0,), dump=()):
    nc = bass.Bass("TRN2", target_bir_lowering=False, debug=False)
    with contextlib.ExitStack() as es:
        b = MK(nc, es)
        X = nc.dram_tensor("x_in", [D, T], F32, kind="ExternalInput").ap()
        OUT = nc.dram_tensor("x_out", [D, T], F32, kind="ExternalOutput").ap()
        W = {}
        def win(name, shape):
            W[name] = nc.dram_tensor(name, list(shape), F32, kind="ExternalInput").ap()
        if 0 in layers:
            win("l0_wq", (D, D)); win("l0_wkv", (D, 2 * D)); win("l0_wout", (D, D))
            win("l0_wg_t", (D, 16)); win("l0_w1", (16 * 1024, 1024))
            win("l0_w2", (16 * 512, 1024)); win("l0_ws1", (D, 2048)); win("l0_ws2", (D, D))
        if 3 in layers:
            win("l3_win_t", (D, 4096)); win("l3_wxp_t", (2048, 96))
            win("l3_wdt_t", (64, 2048)); win("l3_wout_t", (2048, D))
            win("l3_alog", (2048, 16))
            win("l3_lnw", (128, 8)); win("l3_lnb", (128, 8))
            win("l3_convw", (128, 64)); win("l3_convb", (128, 16))
            win("l3_bdt", (128, 16)); win("l3_dsk", (128, 16))
            win("l3f_u1", (D, 256)); win("l3f_v1", (256, 4096))
            win("l3f_u2", (4096, 256)); win("l3f_v2", (256, D))
            win("l3f_b1", (128, 32)); win("l3f_b2", (128, 8))
        if 2 in layers:
            win("l2_wq_t", (D, D)); win("l2_wk_t", (D, D)); win("l2_wv_t", (D, D))
            win("l2_wo_t", (D, D)); win("l2_fm_t", (128, 512))
            win("l2_fmb", (128, 4)); win("l2_rms", (128, 128))
            win("l2f_wqkv", (2 * D, 1536)); win("l2f_wo", (2 * 512, D))
            win("l2f_w1a", (2 * D, HID_PAD)); win("l2f_w1g", (2 * D, HID_PAD))
            win("l2f_w2", (2 * HID_PAD, D))
            win("l2f_qsks", (128, 4)); win("l2f_lng", (128, 16))
            win("l2f_lnb", (128, 16)); win("l2f_b1a", (128, 44))
            win("l2f_b1g", (128, 44)); win("l2f_b2", (128, 16))
        if 1 in layers:
            win("l1_wq_t", (D, D)); win("l1_wk_t", (D, D)); win("l1_wv_t", (D, D))
            win("l1_wo_t", (D, D)); win("l1_sqk", (128, 8))
            win("l1_wg_t", (D, 4096)); win("l1_wu_t", (D, 4096)); win("l1_wd_t", (4096, D))
            for qn, shp in (("scr_l1_qg", (D, 4096)), ("scr_l1_qu", (D, 4096)),
                            ("scr_l1_qd", (4096, D))):
                W[qn] = nc.dram_tensor(qn, list(shp), BF16, kind="Internal").ap()
        carr = host_consts()
        BIGC = ("l0_cos", "l0_sin", "l1_cos", "l1_sin", "l0_mask", "sel16",
                "l2_band", "l2_cosq", "l2_sinq", "l2_cosk", "l2_sink")
        g = {}
        e_c = []
        for name, arr in carr.items():
            ap = b.const_input(name, arr)
            if name in BIGC:
                W["ct_" + name] = ap
                continue
            dt = {np.dtype(np.float32): F32, np.dtype(bfdt): BF16}[arr.dtype]
            t = b.sb("c_" + name, list(arr.shape), dt)
            g[name] = t.ap() if hasattr(t, "ap") else t
            e_c.append(b.dma(g[name], ap, queue="sync"))
        for i, v in enumerate(FCONSTS):
            nc.const_aps.register(F32, float(v), g["fpool"][:, i:i + 1])
        if 3 in layers:
            for nm, cols in (("l3_lnw", 8), ("l3_lnb", 8), ("l3_convw", 64),
                             ("l3_convb", 16), ("l3_bdt", 16), ("l3_dsk", 16),
                             ("l3f_b1", 32), ("l3f_b2", 8)):
                tt = b.sb("c_" + nm, [128, cols], F32)
                ap_ = tt.ap() if hasattr(tt, "ap") else tt
                b.dma(ap_, W[nm], queue="sync")
                W[nm + "_sb"] = ap_
        if 2 in layers:
            for nm, cols in (("l2_fmb", 4), ("l2_rms", 128), ("l2f_qsks", 4),
                             ("l2f_lng", 16), ("l2f_lnb", 16), ("l2f_b1a", 44),
                             ("l2f_b1g", 44), ("l2f_b2", 16)):
                tt = b.sb("c_" + nm, [128, cols], F32)
                ap_ = tt.ap() if hasattr(tt, "ap") else tt
                b.dma(ap_, W[nm], queue="sync")
                W[nm + "_sb"] = ap_
        if 1 in layers:
            sqk_sb = b.sb("c_l1_sqk", [128, 8], F32)
            b.dma(sqk_sb.ap() if hasattr(sqk_sb, "ap") else sqk_sb, W["l1_sqk"],
                  queue="sync")
            W["l1_sqk_sb"] = sqk_sb.ap() if hasattr(sqk_sb, "ap") else sqk_sb
        xTt = [b.sb(f"xt{d}", [128, T], F32) for d in range(DT)]
        xT = [t.ap() for t in xTt]
        arena = b.sb("arena", [128, ARENA_COLS], BF16).ap()
        e_x = [b.dma(xT[d], X[d * 128:(d + 1) * 128, :], queue="sync")
               for d in range(DT)]
        xn = [arena[:, d * T:(d + 1) * T] for d in range(DT)]
        sg = [arena[:, (DT + d) * T:(DT + d + 1) * T] for d in range(DT)]
        edges = b.phase_barrier(extra_edges=e_x + e_c)
        dbg_out = {}
        for li in layers:
            base = 2 * DT * T
            if li == 0:
                b.mark("l0.ln")
                e_xn = emit_layernorm(b, g, xT, xn, Plan(arena, base), in_edges=edges)
                b.phase_barrier()
                if any(x in STAGES for x in ("ffn", "router", "experts", "shared")):
                    b.mark("l0.ffn")
                    e_sg = emit_l0_ffn(b, g, W, xn, sg, Plan(arena, base), e_xn)
                    b.phase_barrier()
                else:
                    e_sg = []
                if "attn" in STAGES:
                    b.mark("l0.attn")
                    edges = emit_l0_attn(b, g, W, xn, xT, sg, Plan(arena, base),
                                         e_xn + e_sg)
                edges = edges + b.phase_barrier()
            elif li == 3:
                b.mark("l3")
                edges = emit_l3(b, g, W, xT, Plan(arena, 0), edges)
                edges = edges + b.phase_barrier()
            elif li == 2:
                b.mark("l2.ln")
                e_xn = emit_layernorm(b, g, xT, xn, Plan(arena, base), in_edges=edges)
                b.phase_barrier()
                sgr = arena[:, DT * T:2 * DT * T]
                b.mark("l2.ffn")
                e_sg = emit_l2_ffn(b, g, W, xn, sgr, sg, Plan(arena, base), e_xn)
                b.phase_barrier()
                b.mark("l2.attn")
                edges = emit_l2_attn(b, g, W, xn, xT, sg, Plan(arena, base),
                                     e_xn + e_sg)
                edges = edges + b.phase_barrier()
            elif li == 1:
                b.mark("l1.ln")
                e_xn = emit_layernorm(b, g, xT, xn, Plan(arena, base), in_edges=edges)
                b.phase_barrier()
                b.mark("l1.ffn")
                e_sg = emit_l1_ffn(b, g, W, xn, sg, Plan(arena, base), e_xn)
                b.phase_barrier()
                b.mark("l1.attn")
                edges = emit_l1_attn(b, g, W, xn, xT, sg, Plan(arena, base),
                                     e_xn + e_sg)
                edges = edges + b.phase_barrier()
            else:
                raise NotImplementedError(li)
            b.mark(f"l{li}.end")
            if "xn" in dump and li == max(layers):
                XN = nc.dram_tensor("xn_out", [D, T], BF16, kind="ExternalOutput").ap()
                for d in range(DT):
                    b.dma(XN[d * 128:(d + 1) * 128, :], xn[d], queue="sync",
                          wait_edges=edges)
            if "sg" in dump and li == max(layers):
                SG = nc.dram_tensor("sg_out", [D, T], BF16, kind="ExternalOutput").ap()
                for d in range(DT):
                    b.dma(SG[d * 128:(d + 1) * 128, :], sg[d], queue="sync",
                          wait_edges=edges)
        e_out = [b.dma(OUT[d * 128:(d + 1) * 128, :], xT[d], queue="sync",
                       wait_edges=edges) for d in range(DT)]
        b.op("sync", "nop", wait_edges=e_out)
        b.finalize()
    global LAST_MARKS
    LAST_MARKS = list(getattr(b, "marks", []))
    return nc, b.consts


def host_weights_l1(inputs):
    w = {}
    for k, nm in (("L1_attn_Wq", "l1_wq_t"), ("L1_attn_Wk", "l1_wk_t"),
                  ("L1_attn_Wv", "l1_wv_t"), ("L1_attn_Wo", "l1_wo_t")):
        w[nm] = np.ascontiguousarray(np.asarray(inputs[k]).T, np.float32)
    sq = (np.asarray(inputs["L1_attn_s_qk"], np.float32) * 32.0).reshape(16, 64)
    col = np.zeros((128, 8), np.float32)
    for p in range(8):
        col[0:64, p] = sq[2 * p]
        col[64:128, p] = sq[2 * p + 1]
    w["l1_sqk"] = col
    w["l1_wg_t"] = np.ascontiguousarray(np.asarray(inputs["L1_ffn_w_gate"]).T, np.float32)
    w["l1_wu_t"] = np.ascontiguousarray(np.asarray(inputs["L1_ffn_w_up"]).T, np.float32)
    w["l1_wd_t"] = np.ascontiguousarray(np.asarray(inputs["L1_ffn_w_down"]).T, np.float32)
    return w


def host_weights_l2(inputs):
    w = {}
    for k, nm in (("L2_attn_Wq", "l2_wq_t"), ("L2_attn_Wk", "l2_wk_t"),
                  ("L2_attn_Wv", "l2_wv_t"), ("L2_attn_Wo", "l2_wo_t")):
        w[nm] = np.ascontiguousarray(np.asarray(inputs[k]).T, np.float32)
    w["l2_fm_t"] = np.ascontiguousarray(np.concatenate(
        [np.asarray(inputs[k]).T for k in ("L2_attn_fmq_w1", "L2_attn_fmq_w2",
                                           "L2_attn_fmk_w1", "L2_attn_fmk_w2")],
        axis=1), np.float32)
    fmb = np.stack([np.asarray(inputs[k]) for k in
                    ("L2_attn_fmq_b1", "L2_attn_fmq_b2",
                     "L2_attn_fmk_b1", "L2_attn_fmk_b2")], axis=1)
    w["l2_fmb"] = np.ascontiguousarray(fmb, np.float32)
    w["l2_rms"] = np.ascontiguousarray(
        np.tile(np.asarray(inputs["L2_attn_rms_w"], np.float32)[None, :], (128, 1)))
    wqkv = np.asarray(inputs["L2_ffn_wqkv"], np.float32)
    w["l2f_wqkv"] = np.ascontiguousarray(wqkv.reshape(2 * 1024, 1536))
    wo = np.asarray(inputs["L2_ffn_wo"], np.float32)
    w["l2f_wo"] = np.ascontiguousarray(wo.reshape(2 * 512, 1024))
    w1 = np.asarray(inputs["L2_ffn_w1"], np.float32)
    HID = 2730
    w1a = np.zeros((2, 1024, HID_PAD), np.float32)
    w1g = np.zeros((2, 1024, HID_PAD), np.float32)
    w1a[:, :, :HID] = w1[:, :, :HID]
    w1g[:, :, :HID] = w1[:, :, HID:]
    w["l2f_w1a"] = np.ascontiguousarray(w1a.reshape(2 * 1024, HID_PAD))
    w["l2f_w1g"] = np.ascontiguousarray(w1g.reshape(2 * 1024, HID_PAD))
    w2 = np.asarray(inputs["L2_ffn_w2"], np.float32)
    w2p = np.zeros((2, HID_PAD, 1024), np.float32)
    w2p[:, :HID] = w2
    w["l2f_w2"] = np.ascontiguousarray(w2p.reshape(2 * HID_PAD, 1024))
    qs = np.asarray(inputs["L2_ffn_q_scale"], np.float32)
    ks = np.asarray(inputs["L2_ffn_k_scale"], np.float32)
    qsks = np.zeros((128, 4), np.float32)
    for l in range(2):
        qsks[:, 2 * l] = np.tile(qs[l], 2)
        qsks[:, 2 * l + 1] = np.tile(ks[l], 2)
    w["l2f_qsks"] = qsks
    lng = np.asarray(inputs["L2_ffn_ln_g"], np.float32).reshape(2, 8, 128)
    lnb = np.asarray(inputs["L2_ffn_ln_b"], np.float32).reshape(2, 8, 128)
    w["l2f_lng"] = np.ascontiguousarray(
        lng.transpose(2, 0, 1).reshape(128, 16)[:, [l * 8 + d for l in range(2) for d in range(8)]])
    w["l2f_lnb"] = np.ascontiguousarray(
        lnb.transpose(2, 0, 1).reshape(128, 16)[:, [l * 8 + d for l in range(2) for d in range(8)]])
    b1 = np.asarray(inputs["L2_ffn_b1"], np.float32)
    b1a = np.zeros((2, HID_PAD), np.float32)
    b1g = np.zeros((2, HID_PAD), np.float32)
    b1a[:, :HID] = b1[:, :HID]
    b1g[:, :HID] = b1[:, HID:]
    w["l2f_b1a"] = np.ascontiguousarray(
        b1a.reshape(2, 22, 128).transpose(2, 0, 1).reshape(128, 44))
    w["l2f_b1g"] = np.ascontiguousarray(
        b1g.reshape(2, 22, 128).transpose(2, 0, 1).reshape(128, 44))
    b2 = np.asarray(inputs["L2_ffn_b2"], np.float32)
    w["l2f_b2"] = np.ascontiguousarray(
        b2.reshape(2, 8, 128).transpose(2, 0, 1).reshape(128, 16))
    return w


def host_weights_l3(inputs):
    w = {}
    w["l3_win_t"] = np.ascontiguousarray(np.asarray(inputs["L3_attn_W_in"]).T, np.float32)
    w["l3_wxp_t"] = np.ascontiguousarray(np.asarray(inputs["L3_attn_W_xproj"]).T, np.float32)
    w["l3_wdt_t"] = np.ascontiguousarray(np.asarray(inputs["L3_attn_W_dt"]).T, np.float32)
    w["l3_wout_t"] = np.ascontiguousarray(np.asarray(inputs["L3_attn_W_out"]).T, np.float32)
    w["l3_alog"] = np.ascontiguousarray(np.asarray(inputs["L3_attn_A_log"], np.float32))
    def colsh(v, nt):
        return np.ascontiguousarray(np.asarray(v, np.float32).reshape(nt, 128).T)
    w["l3_lnw"] = colsh(inputs["L3_attn_ln_w"], 8)
    w["l3_lnb"] = colsh(inputs["L3_attn_ln_b"], 8)
    cw = np.asarray(inputs["L3_attn_conv_w"], np.float32).reshape(16, 128, 4)
    w["l3_convw"] = np.ascontiguousarray(cw.transpose(1, 0, 2).reshape(128, 64))
    w["l3_convb"] = colsh(inputs["L3_attn_conv_b"], 16)
    w["l3_bdt"] = colsh(inputs["L3_attn_b_dt"], 16)
    w["l3_dsk"] = colsh(inputs["L3_attn_Dskip"], 16)
    w["l3f_u1"] = np.ascontiguousarray(np.asarray(inputs["L3_ffn_U1"], np.float32))
    w["l3f_v1"] = np.ascontiguousarray(np.asarray(inputs["L3_ffn_V1"], np.float32))
    w["l3f_u2"] = np.ascontiguousarray(np.asarray(inputs["L3_ffn_U2"], np.float32))
    w["l3f_v2"] = np.ascontiguousarray(np.asarray(inputs["L3_ffn_V2"], np.float32))
    w["l3f_b1"] = colsh(inputs["L3_ffn_b1"], 32)
    w["l3f_b2"] = colsh(inputs["L3_ffn_b2"], 8)
    return w


def host_weights_l0(inputs):
    w = {}
    w["l0_wq"] = np.ascontiguousarray(inputs["L0_attn_w_q"], np.float32)
    w["l0_wkv"] = np.ascontiguousarray(inputs["L0_attn_w_kv"], np.float32)
    w["l0_wout"] = np.ascontiguousarray(inputs["L0_attn_w_out"], np.float32)
    w["l0_wg_t"] = np.ascontiguousarray(np.asarray(inputs["L0_ffn_Wg"]).T, np.float32)
    w["l0_w1"] = np.ascontiguousarray(np.asarray(inputs["L0_ffn_W1"]).reshape(16 * 1024, 1024), np.float32)
    w["l0_w2"] = np.ascontiguousarray(np.asarray(inputs["L0_ffn_W2"]).reshape(16 * 512, 1024), np.float32)
    w["l0_ws1"] = np.ascontiguousarray(inputs["L0_ffn_Ws1"], np.float32)
    w["l0_ws2"] = np.ascontiguousarray(inputs["L0_ffn_Ws2"], np.float32)
    return w



_CACHE = {}


def kernel(**inputs):
    import numpy as _np
    layers = tuple(LAYERS)
    if layers not in _CACHE:
        _CACHE[layers] = build_program(layers=layers)
    nc, consts = _CACHE[layers]
    w = {}
    if 0 in layers:
        w.update(host_weights_l0(inputs))
    if 1 in layers:
        w.update(host_weights_l1(inputs))
    if 2 in layers:
        w.update(host_weights_l2(inputs))
    if 3 in layers:
        w.update(host_weights_l3(inputs))
    x = _np.asarray(inputs["L0_attn_x"], _np.float32)
    in_maps = []
    for c in range(8):
        m = dict(consts)
        m.update(w)
        m["x_in"] = _np.ascontiguousarray(x[c].T)
        in_maps.append(m)
    res = run_bass_kernel_spmd(nc, in_maps, core_ids=list(range(8)))
    out = _np.stack([_np.asarray(r["x_out"]).T for r in res.results], axis=0)
    return out



SWAPPAIR = [j ^ 1 for j in range(32)]


def emit_l1_attn(b, g, W, xn, xT, sg, pl, in_edges):
    o_all = pl.bf(8 * T)
    mark = pl.p
    qk = pl.bf(2 * T)
    v_tok = pl.bf(16 * 128)
    P = pl.bf(T)
    PT = pl.bf(16 * 128)
    qoff = (pl.p + 15) & ~15
    qraw = pl.bf(512)
    shf = pl.bf(512)
    t1 = pl.bf(512)
    t2 = qraw
    sq = pl.f32(512)
    P2 = pl.ar[:, qoff:qoff + 2048]
    n2 = P
    rn_rep = pl.bf(512)
    sums = pl.f32(16)
    diag = pl.bf(128)
    diag2 = pl.bf(128)
    wqb = pl.bf(8 * 128)
    wkb = pl.bf(8 * 128)
    wvb = pl.bf(8 * 128)
    cosT = pl.bf(T)
    sinT = pl.bf(T)
    e_ct = [b.dma(cosT, W["ct_l1_cos"], queue="sync"),
            b.dma(sinT, W["ct_l1_sin"], queue="sync")]
    idb = g["ident_bf"]
    e64 = g["e64"]
    e64t = g["e64_t"]
    e_q_mm = None
    e_pv_last = None
    e_tr_last = None
    e_sc_last = None
    e_sum = None
    e_qread = None
    sqk = W["l1_sqk_sb"]
    for pair in range(8):
        wed = [e for e in (e_q_mm,) if e]
        e_wq = wload(b, wqb, W["l1_wq_t"][:, pair * 128:(pair + 1) * 128], wait_edges=wed)
        e_wk = wload(b, wkb, W["l1_wk_t"][:, pair * 128:(pair + 1) * 128], wait_edges=wed)
        e_wv = wload(b, wvb, W["l1_wv_t"][:, pair * 128:(pair + 1) * 128], wait_edges=wed)
        for ti in range(16):
            bk = b.rr(7)
            b.mm_group(bk, [(b.pbank[bk][:, 0:128], xn[k][:, ti * 128:(ti + 1) * 128],
                             wvb[:, k * 128:(k + 1) * 128]) for k in range(DT)],
                       wait_edges=[e_wv] + list(in_edges))
            b.bank_read(bk, "vector", "tensor_copy",
                        v_tok[:, ti * 128:(ti + 1) * 128], b.pbank[bk][:, 0:128])
        for wbuf, e_w, dst_off in ((wqb, e_wq, 0), (wkb, e_wk, T)):
            for cch, sl in enumerate(CH512):
                bk = b.rr(7)
                e_q_mm = b.mm_group(bk, [(b.pbank[bk][:, :], wbuf[:, k * 128:(k + 1) * 128],
                                          xn[k][:, sl]) for k in range(DT)],
                                    wait_edges=[e_w])
                e_r = b.bank_read(bk, "scalar", "copy", qraw, b.pbank[bk][:, :],
                                  wait_edges=[e_qread] if e_qread else [])
                b.op("vector", "stream_shuffle", shf, qraw, SWAPPAIR, wait_edges=[e_r])
                e_qread = b.op("vector", "tensor_mul", t1, qraw, cosT[:, sl], inc=True)
                b.op("vector", "tensor_mul", t2, shf, sinT[:, sl])
                dsl = slice(dst_off + sl.start, dst_off + sl.stop)
                b.op("vector", "tensor_add", qk[:, dsl], t1, t2, inc=True,
                     wait_edges=[e_sc_last] if e_sc_last else [])
                e_sq = b.op("scalar", "activation", sq, qk[:, dsl], AF.Square, inc=True)
                bk2 = b.rr(7)
                b.mm_group(bk2, [(b.pbank[bk2][0:2, :], e64t, sq)], wait_edges=[e_sq])
                b.bank_read(bk2, "vector", "tensor_copy", n2[0:2, sl],
                            b.pbank[bk2][0:2, :])
            e_n2 = b.op("scalar", "activation", n2[0:2, 0:T], n2[0:2, 0:T], AF.Sqrt,
                        inc=True)
            e_rn = b.op("vector", "tensor_scalar", n2[0:2, 0:T], n2[0:2, 0:T], 1e-12,
                        None, ALU.max, wait_edges=[e_n2], inc=True)
            for cch, sl in enumerate(CH512):
                e_rc = b.op("vector", "reciprocal", sq[0:2, :], n2[0:2, sl],
                            wait_edges=[e_rn], inc=True)
                bk2 = b.rr(7)
                b.mm_group(bk2, [(b.pbank[bk2][:, :], e64, sq[0:2, :])],
                           wait_edges=[e_rc])
                e_rep = b.bank_read(bk2, "vector", "tensor_copy", rn_rep,
                                    b.pbank[bk2][:, :])
                dsl = slice(dst_off + sl.start, dst_off + sl.stop)
                e_sc_last = b.op("vector", "scalar_tensor_tensor", qk[:, dsl],
                                 qk[:, dsl], sqk[:, pair:pair + 1], rn_rep,
                                 ALU.mult, ALU.mult, wait_edges=[e_rep], inc=True)
        e_rope = b.fence(engines=("vector",))
        if pair == 0 and globals().get("DBG_L1"):
            QKD = b.nc.dram_tensor("qk_dbg", [128, 2 * T], BF16,
                                   kind="ExternalOutput").ap()
            b.dma(QKD, qk, queue="sync", wait_edges=e_rope)
        for hh in range(2):
            h = 2 * pair + hh
            r0 = hh * 64
            qrows = qk[r0:r0 + 64, 0:T]
            krows = qk[r0:r0 + 64, T:2 * T]
            pvbk = 7
            prev = None
            for ti in range(17):
                if ti < 16:
                    par = ti % 2
                    Pb = P if par == 0 else P2
                    dg = diag if par == 0 else diag2
                    sac, ssm, src = (0, 8, 9) if par == 0 else (4, 14, 15)
                    smx, sbi = (10, 11) if par == 0 else (12, 13)
                    for c in range(4):
                        b.mm_group(c, [(b.pbank[c][:, :],
                                        qrows[:, ti * 128:(ti + 1) * 128],
                                        krows[:, c * 512:(c + 1) * 512])],
                                   wait_edges=(list(e_rope) if (ti == 0 and c == 0) else ()))
                    b.op("vector", "reduce_max", sums[:, smx:smx + 1],
                         b.pbig[:, 0:2048], AX.X,
                         wait_edges=[b.last_op["tensor"]])
                    e_nm = b.op("vector", "tensor_scalar", sums[:, sbi:sbi + 1],
                                sums[:, smx:smx + 1], -8.0, None, ALU.mult,
                                inc=True)
                    e_exp = None
                    for c in range(4):
                        e_exp = b.bank_read(c, "scalar", "activation",
                                            Pb[:, c * 512:(c + 1) * 512],
                                            b.pbank[c][:, :],
                                            AF.Exp, bias=sums[:, sbi:sbi + 1],
                                            scale=8.0,
                                            accum_out=sums[:, sac + c:sac + c + 1],
                                            wait_edges=[e_nm])
                    b.op("vector", "reduce_sum", sums[:, ssm:ssm + 1],
                         sums[:, sac:sac + 4], AX.X, wait_edges=[e_exp])
                    b.op("vector", "reciprocal", sums[:, src:src + 1],
                         sums[:, ssm:ssm + 1])
                    e_diag = b.op("vector", "tensor_scalar", dg, idb,
                                  sums[:, src:src + 1], None, ALU.mult, inc=True)
                    cur = (Pb, dg, e_diag, ti)
                else:
                    cur = None
                if prev is not None:
                    Pp, dgp, e_dgp, tj = prev
                    for grp in range(0, 16, 4):
                        bk = 4 + (grp // 4) % 2
                        for j in range(4):
                            b.mm_group(bk, [(b.pbank[bk][:, j * 128:(j + 1) * 128],
                                             Pp[:, (grp + j) * 128:(grp + j + 1) * 128],
                                             dgp)],
                                       wait_edges=[e_dgp] if j == 0 else ())
                        e_tr_last = b.bank_read(bk, "vector", "tensor_copy",
                                                PT[:, grp * 128:(grp + 4) * 128],
                                                b.pbank[bk][:, :])
                    b.mm_group(pvbk, [(b.pbank[pvbk][0:64, (tj % 4) * 128:(tj % 4) * 128 + 128],
                                       v_tok[:, tjt * 128 + r0:tjt * 128 + r0 + 64],
                                       PT[:, tjt * 128:(tjt + 1) * 128])
                                      for tjt in range(16)],
                               wait_edges=[e_tr_last])
                    if tj % 4 == 3:
                        blk = tj // 4
                        e_pv_last = b.bank_read(
                            pvbk, "vector", "tensor_copy",
                            o_all[r0:r0 + 64, pair * T + blk * 512: pair * T + (blk + 1) * 512],
                            b.pbank[pvbk][0:64, :])
                prev = cur
            if pair == 0 and hh == 0 and globals().get("DBG_L1"):
                PD = b.nc.dram_tensor("p_dbg", [128, T], BF16,
                                      kind="ExternalOutput").ap()
                b.dma(PD, P, queue="sync", wait_edges=[e_pv_last])
    if globals().get("DBG_L1"):
        OD = b.nc.dram_tensor("o_dbg", [128, 8 * T], BF16,
                              kind="ExternalOutput").ap()
        b.dma(OD, o_all, queue="sync", wait_edges=b.fence())
    pl.p = mark
    e_all = b.fence()
    wob = pl.bf(8 * 1024)
    upd = pl.f32(512)
    e_wo = wload(b, wob, W["l1_wo_t"], wait_edges=e_all)
    e_upd = None
    for do in range(DT):
        for cch, sl in enumerate(CH512):
            bk = b.rr(7)
            b.mm_group(bk, [(b.pbank[bk][:, :], wob[:, kp * 1024 + do * 128:kp * 1024 + do * 128 + 128],
                             o_all[:, kp * T + sl.start: kp * T + sl.stop])
                            for kp in range(8)], wait_edges=[e_wo])
            b.bank_read(bk, "vector", "tensor_mul", upd, b.pbank[bk][:, :],
                        sg[do][:, sl], wait_edges=list(in_edges))
            e_upd = b.op("vector", "tensor_add", xT[do][:, sl], xT[do][:, sl], upd,
                         inc=True)
    return [e_upd]



def emit_token_absmax(b, g, tiles_slices, pl_bank, out_col, tmp_cols, idf):
    ncomb = 0
    e_red = None
    for i4 in range(0, len(tiles_slices), 4):
        grp = tiles_slices[i4:i4 + 4]
        bk = b.rr(6)
        for j, ap in enumerate(grp):
            b.op("tensor", "transpose", b.pbank_bf[bk][:, j * 128:(j + 1) * 128],
                 ap, idf)
        e_red = b.bank_read(bk, "vector", "tensor_reduce",
                            tmp_cols[:, ncomb:ncomb + 1],
                            b.pbank_bf[bk][:, 0:len(grp) * 128], AX.X, ALU.max,
                            apply_absolute_value=True)
        ncomb += 1
    b.op("vector", "tensor_copy", out_col, tmp_cols[:, 0:1], wait_edges=[e_red])
    for i in range(1, ncomb):
        b.op("vector", "tensor_max", out_col, out_col, tmp_cols[:, i:i + 1])
    return b.op("vector", "nop", inc=True)


def emit_l1_quant_weights(b, g, W, pl, swrec):
    wf = pl.f32(1024)
    qv = pl.f32(1024)
    qb = pl.bf(1024)
    acc = pl.f32(40)
    ones_f = g["ones_f32_128x1"]
    onesr_f = g["ones_f32_1x128"]
    edges = []
    for mi, wname, qname in ((0, "l1_wg_t", "scr_l1_qg"), (1, "l1_wu_t", "scr_l1_qu"),
                             (2, "l1_wd_t", "scr_l1_qd")):
        src = W[wname]
        dst = W[qname]
        R, C = src.shape
        tiles = [(ri, cj) for ri in range(R // 128) for cj in range(C // 1024)]
        e_red = None
        for i, (ri, cj) in enumerate(tiles):
            e_l = b.dma(wf, src[ri * 128:(ri + 1) * 128, cj * 1024:(cj + 1) * 1024],
                        queue="sync")
            e_red = b.op("vector", "tensor_reduce", acc[:, i:i + 1], wf, AX.X,
                         ALU.add, apply_absolute_value=True, wait_edges=[e_l],
                         inc=True)
        e_s = b.op("vector", "reduce_sum", acc[:, 36:37], acc[:, 0:len(tiles)], AX.X,
                   wait_edges=[e_red], inc=True)
        bk = b.rr()
        b.mm_group(bk, [(b.pbank[bk][0:1, 0:1], ones_f, acc[:, 36:37])],
                   wait_edges=[e_s])
        e_t = b.bank_read(bk, "vector", "tensor_copy", acc[0:1, 38:39],
                          b.pbank[bk][0:1, 0:1])
        b.op("vector", "tensor_scalar", acc[0:1, 38:39], acc[0:1, 38:39],
             1.0 / (R * C), None, ALU.mult, wait_edges=[e_t])
        e_m = b.op("vector", "tensor_scalar", acc[0:1, 38:39], acc[0:1, 38:39],
                   1e-5, None, ALU.max, inc=True)
        bk = b.rr()
        b.mm_group(bk, [(b.pbank[bk][:, 0:1], onesr_f, acc[0:1, 38:39])],
                   wait_edges=[e_m])
        e_sw = b.bank_read(bk, "vector", "tensor_copy", swrec[:, mi:mi + 1],
                           b.pbank[bk][:, 0:1])
        edges.append(e_sw)
        b.op("vector", "reciprocal", acc[:, 39:40], swrec[:, mi:mi + 1],
             wait_edges=[e_sw])
        e_rq = b.op("vector", "nop", inc=True)
        for ri, cj in tiles:
            dsl = (slice(ri * 128, (ri + 1) * 128), slice(cj * 1024, (cj + 1) * 1024))
            e_l = b.dma(wf, src[dsl[0], dsl[1]], queue="sync")
            e_c = b.op("scalar", "activation", qv, wf, AF.Copy, scale=acc[:, 39:40],
                       wait_edges=[e_l, e_rq], inc=True)
            b.op("vector", "tensor_scalar", qv, qv, MAGIC, -MAGIC, ALU.add, ALU.add,
                 wait_edges=[e_c])
            e_q = b.op("vector", "tensor_scalar", qb, qv, 1.0, -1.0, ALU.min,
                       ALU.max, inc=True)
            e_st = b.dma(dst[dsl[0], dsl[1]], qb, queue="sync", wait_edges=[e_q])
            edges.append(e_st)
    return swrec, edges


def emit_l1_ffn(b, g, W, xn, sg, pl, xn_edges):
    swrec_region = pl.f32(4)
    plq = Plan(pl.ar, pl.p)
    swrec, e_qw = emit_l1_quant_weights(b, g, W, plq, swrec_region)
    b.mark("l1.ffn.mm")
    idf = g["ident_f32"]
    idb = g["ident_bf"]
    gu = pl.bf(32 * 512)
    xq = pl.bf(8 * 512)
    qwb = pl.bf(8 * 512)
    qdb = pl.bf(32 * 128)
    strep = pl.bf(512)
    rstrep = pl.bf(4 * 512)
    cols = pl.f32(32)
    tsc = pl.f32(512)
    tsc2 = pl.f32(512)
    e_dn_mm = None
    e_g_mm = None
    e_xq_mm = None
    for cch, sl in enumerate(CH512):
        rst_c = rstrep[:, cch * 512:(cch + 1) * 512]
        for tsub in range(4):
            t0 = sl.start + tsub * 128
            slices = [xn[k][:, t0:t0 + 128] for k in range(DT)]
            e_amax = emit_token_absmax(b, g, slices, None, cols[:, tsub:tsub + 1],
                                       cols[:, 8:16], idb)
        b.op("vector", "nop", wait_edges=[e_amax])
        for tsub in range(4):
            c = cols[:, tsub:tsub + 1]
            b.op("vector", "tensor_scalar", c, c, 1e-5, None, ALU.max)
            b.op("vector", "reciprocal", cols[:, 4 + tsub:5 + tsub], c)
            b.op("vector", "tensor_scalar", cols[:, 4 + tsub:5 + tsub],
                 cols[:, 4 + tsub:5 + tsub], 127.0, None, ALU.mult)
            b.op("vector", "tensor_scalar", c, c, 1.0 / 127.0, None, ALU.mult)
        e_cols = b.op("vector", "nop", inc=True)
        bk = b.rr(6)
        for tsub in range(4):
            b.mm_group(bk, [(b.pbank[bk][:, tsub * 128:(tsub + 1) * 128],
                             cols[:, 4 + tsub:5 + tsub].broadcast_to((128, 128)),
                             idf)], wait_edges=[e_cols] if tsub == 0 else ())
        e_strep = b.bank_read(bk, "vector", "tensor_copy", strep, b.pbank[bk][:, :])
        bk = b.rr(6)
        for tsub in range(4):
            b.mm_group(bk, [(b.pbank[bk][:, tsub * 128:(tsub + 1) * 128],
                             cols[:, tsub:tsub + 1].broadcast_to((128, 128)),
                             idf)], wait_edges=[e_cols] if tsub == 0 else ())
        e_rstrep = b.bank_read(bk, "vector", "tensor_copy", rst_c, b.pbank[bk][:, :])
        e_xq = None
        for k in range(DT):
            b.op("vector", "tensor_mul", tsc, xn[k][:, sl], strep,
                 wait_edges=[e_strep])
            b.op("vector", "tensor_scalar", tsc, tsc, MAGIC, -MAGIC, ALU.add, ALU.add)
            e_xq = b.op("vector", "tensor_scalar", xq[:, k * 512:(k + 1) * 512], tsc,
                        127.0, -128.0, ALU.min, ALU.max, inc=True)
        for mb in range(8):
            for which, qsrc in enumerate(("scr_l1_qg", "scr_l1_qu")):
                e_w = wload(b, qwb, W[qsrc][:, mb * 512:(mb + 1) * 512],
                            wait_edges=[e for e in (e_g_mm,) if e] + list(e_qw))
                for mt in range(4):
                    bk = b.rr(6)
                    e_g_mm = b.mm_group(bk, [
                        (b.pbank[bk][:, :],
                         qwb[:, k * 512 + mt * 128: k * 512 + mt * 128 + 128],
                         xq[:, k * 512:(k + 1) * 512]) for k in range(DT)],
                        wait_edges=[e_w, e_xq])
                    m_i = mb * 4 + mt
                    msl = slice(m_i * 512, (m_i + 1) * 512)
                    if which == 0:
                        b.bank_read(bk, "vector", "tensor_scalar", tsc,
                                    b.pbank[bk][:, :], swrec[:, 0:1], None, ALU.mult)
                        e_t = b.op("vector", "tensor_mul", tsc, tsc, rst_c,
                                   wait_edges=[e_rstrep], inc=True)
                        b.op("scalar", "activation", gu[:, msl], tsc, AF.Silu,
                             wait_edges=[e_t], inc=True)
                    else:
                        b.bank_read(bk, "vector", "tensor_scalar", tsc2,
                                    b.pbank[bk][:, :], swrec[:, 1:2], None, ALU.mult)
                        b.op("vector", "tensor_mul", tsc2, tsc2, rst_c)
                        b.op("vector", "tensor_mul", gu[:, msl], gu[:, msl], tsc2,
                             wait_edges=[b.last_op["scalar"]], inc=True)
        for tsub in range(4):
            t0 = tsub * 128
            slices = [gu[:, m_i * 512 + t0: m_i * 512 + t0 + 128]
                      for m_i in range(32)]
            e_amax = emit_token_absmax(b, g, slices, None, cols[:, tsub:tsub + 1],
                                       cols[:, 8:16], idb)
        for tsub in range(4):
            c = cols[:, tsub:tsub + 1]
            b.op("vector", "tensor_scalar", c, c, 1e-5, None, ALU.max,
                 wait_edges=[e_amax] if tsub == 0 else ())
            b.op("vector", "reciprocal", cols[:, 4 + tsub:5 + tsub], c)
            b.op("vector", "tensor_scalar", cols[:, 4 + tsub:5 + tsub],
                 cols[:, 4 + tsub:5 + tsub], 127.0, None, ALU.mult)
            b.op("vector", "tensor_scalar", c, c, 1.0 / 127.0, None, ALU.mult)
        e_cols = b.op("vector", "nop", inc=True)
        bk = b.rr(6)
        for tsub in range(4):
            b.mm_group(bk, [(b.pbank[bk][:, tsub * 128:(tsub + 1) * 128],
                             cols[:, 4 + tsub:5 + tsub].broadcast_to((128, 128)),
                             idf)], wait_edges=[e_cols] if tsub == 0 else ())
        e_strep = b.bank_read(bk, "vector", "tensor_copy", strep, b.pbank[bk][:, :])
        bk = b.rr(6)
        for tsub in range(4):
            b.mm_group(bk, [(b.pbank[bk][:, tsub * 128:(tsub + 1) * 128],
                             cols[:, tsub:tsub + 1].broadcast_to((128, 128)),
                             idf)], wait_edges=[e_cols] if tsub == 0 else ())
        e_rstrep = b.bank_read(bk, "vector", "tensor_copy", rst_c, b.pbank[bk][:, :])
        e_guq = None
        for m_i in range(32):
            msl = slice(m_i * 512, (m_i + 1) * 512)
            b.op("vector", "tensor_mul", tsc, gu[:, msl], strep,
                 wait_edges=[e_strep])
            b.op("vector", "tensor_scalar", tsc, tsc, MAGIC, -MAGIC, ALU.add, ALU.add)
            e_guq = b.op("vector", "tensor_scalar", gu[:, msl], tsc, 127.0, -128.0,
                         ALU.min, ALU.max, inc=True)
        for do in range(DT):
            e_wd = wload(b, qdb, W["scr_l1_qd"][:, do * 128:(do + 1) * 128],
                         wait_edges=[e for e in (e_dn_mm,) if e] + list(e_qw))
            bk = b.rr(6)
            e_dn_mm = b.mm_group(bk, [
                (b.pbank[bk][:, :], qdb[:, k * 128:(k + 1) * 128],
                 gu[:, k * 512:(k + 1) * 512]) for k in range(32)],
                wait_edges=[e_wd, e_guq])
            b.bank_read(bk, "vector", "tensor_scalar", tsc, b.pbank[bk][:, :],
                        swrec[:, 2:3], None, ALU.mult)
            e_t = b.op("vector", "tensor_mul", tsc, tsc, rst_c,
                       wait_edges=[e_rstrep], inc=True)
            b.op("scalar", "activation", sg[do][:, sl], tsc, AF.Silu,
                 wait_edges=[e_t], inc=True)
    return b.fence(engines=("scalar",))



def emit_l2_attn(b, g, W, xn, xT, sg, pl, in_edges):
    o_all = pl.bf(8 * T)
    mark = pl.p
    qf = pl.bf(T)
    kf = pl.bf(T)
    ktok = pl.bf(T)
    vtok = pl.bf(T)
    otok = pl.bf(T)
    Ssb = pl.bf(128)
    qraw = pl.bf(512)
    t1 = pl.f32(512)
    atm = pl.bf(128)
    sqs = pl.f32(128)
    cstat = pl.f32(8)
    wqb = pl.bf(8 * 128)
    wkb = pl.bf(8 * 128)
    wvb = pl.bf(8 * 128)
    fmw = pl.bf(4 * 128)
    fmb = W["l2_fmb_sb"]
    rmsrep = W["l2_rms_sb"]
    idb = g["ident_bf"]
    triu = g["l2_triu"]
    e_fm = wload(b, fmw, W["l2_fm_t"])
    e_q_mm = None
    e_o_tr = None
    SCALE = 128.0 ** -0.5
    for h in range(8):
        wed = [e for e in (e_q_mm,) if e]
        e_wq = wload(b, wqb, W["l2_wq_t"][:, h * 128:(h + 1) * 128], wait_edges=wed)
        e_wk = wload(b, wkb, W["l2_wk_t"][:, h * 128:(h + 1) * 128], wait_edges=wed)
        e_wv = wload(b, wvb, W["l2_wv_t"][:, h * 128:(h + 1) * 128], wait_edges=wed)
        for ti in range(16):
            bk = b.rr(6)
            b.mm_group(bk, [(b.pbank[bk][:, 0:128], xn[k][:, ti * 128:(ti + 1) * 128],
                             wvb[:, k * 128:(k + 1) * 128]) for k in range(DT)],
                       wait_edges=[e_wv] + list(in_edges))
            b.bank_read(bk, "vector", "tensor_copy",
                        vtok[:, ti * 128:(ti + 1) * 128], b.pbank[bk][:, 0:128])
        for which, (wbuf, e_w, dstbuf) in enumerate(
                ((wqb, e_wq, qf), (wkb, e_wk, kf))):
            for cch, sl in enumerate(CH512):
                bk = b.rr(6)
                e_q_mm = b.mm_group(bk, [(b.pbank[bk][:, :], wbuf[:, k * 128:(k + 1) * 128],
                                          xn[k][:, sl]) for k in range(DT)],
                                    wait_edges=[e_w])
                e_r = b.bank_read(bk, "scalar", "copy", qraw, b.pbank[bk][:, :])
                bkA = b.rr(6); bkB = b.rr(6)
                fo = which * 2
                b.mm_group(bkA, [(b.pbank[bkA][:, :], fmw[:, fo * 128:(fo + 1) * 128],
                                  qraw)], wait_edges=[e_fm, e_r])
                b.mm_group(bkB, [(b.pbank[bkB][:, :], fmw[:, (fo + 1) * 128:(fo + 2) * 128],
                                  qraw)], wait_edges=[e_fm, e_r])
                sc2 = SCALE if which == 0 else 1.0
                b.bank_read(bkA, "vector", "tensor_scalar", t1, b.pbank[bkA][:, :],
                            fmb[:, 2 * which:2 * which + 1], sc2, ALU.add, ALU.mult)
                b.bank_read(bkB, "vector", "scalar_tensor_tensor", dstbuf[:, sl],
                            b.pbank[bkB][:, :], fmb[:, 2 * which + 1:2 * which + 2],
                            t1, ALU.add, ALU.mult)
        e_kf = b.fence(engines=("vector",))
        for grp in range(0, 16, 4):
            bk = b.rr(6)
            for j in range(4):
                b.op("tensor", "transpose",
                     b.pbank_bf[bk][:, j * 128:(j + 1) * 128],
                     kf[:, (grp + j) * 128:(grp + j + 1) * 128], idb,
                     wait_edges=(e_kf if (grp == 0 and j == 0) else ()))
            b.bank_read(bk, "vector", "tensor_copy",
                        ktok[:, grp * 128:(grp + 4) * 128],
                        b.pbank_bf[bk][:, 0:512])
        e_kt = b.fence(engines=("vector",))
        b.op("vector", "memset", Ssb, 0.0)
        e_S = b.op("vector", "nop", inc=True)
        prev_rec2 = None
        for n in range(16):
            cs = slice(n * 128, (n + 1) * 128)
            bk = b.rr(5)
            b.mm_group(bk, [(b.pbank[bk][:, 0:128], kf[:, cs], qf[:, cs])],
                       wait_edges=e_kt if n == 0 else ())
            e_at = b.bank_read(bk, "vector", "tensor_mul", atm, b.pbank[bk][:, 0:128],
                               triu)
            mms = []
            if n > 0:
                mms.append((b.pbank[7][:, 0:128], qf[:, cs], Ssb))
            mms.append((b.pbank[7][:, 0:128], atm, vtok[:, cs]))
            b.mm_group(7, mms, wait_edges=[e_at, e_S])
            def bld(e, st=(n == 0), sp=(n == 15), l=ktok[:, cs], r=vtok[:, cs]):
                return e.matmul(b.pbank[6][:, 0:128], l, r, start=st, stop=sp)
            rec = b._mkrec("tensor", bld)
            rec.consumed = True
            b._dep_read(rec, ktok[:, cs])
            b._dep_read(rec, vtok[:, cs])
            if n == 0:
                b._dep_write(rec, b.pbank[6][:, 0:128])
            if prev_rec2 is not None:
                rec.waits.add(prev_rec2)
            rec2 = b.op("scalar", "copy", Ssb, b.pbank[6][:, 0:128], inc=True,
                        wait_edges=[rec])
            prev_rec2 = rec2
            e_S = rec2
            e_sq = b.op("scalar", "activation", sqs, b.pbank[7][:, 0:128], AF.Square,
                        accum_out=cstat[:, 0:1], wait_edges=[b.last_op["tensor"]],
                        inc=True)
            b.op("vector", "tensor_scalar", cstat[:, 1:2], cstat[:, 0:1],
                 1.0 / 128.0, 1e-5, ALU.mult, ALU.add, wait_edges=[e_sq])
            e_sr = b.op("scalar", "activation", cstat[:, 1:2], cstat[:, 1:2],
                        AF.Sqrt, inc=True)
            e_rc = b.op("vector", "reciprocal", cstat[:, 2:3], cstat[:, 1:2],
                        wait_edges=[e_sr], inc=True)
            b.op("vector", "scalar_tensor_tensor", otok[:, cs], b.pbank[7][:, 0:128],
                 cstat[:, 2:3], rmsrep, ALU.mult, ALU.mult, wait_edges=[e_rc],
                 inc=True)
        e_on = b.fence(engines=("vector",))
        for grp in range(0, 16, 4):
            bk = b.rr(5)
            for j in range(4):
                b.op("tensor", "transpose",
                     b.pbank_bf[bk][:, j * 128:(j + 1) * 128],
                     otok[:, (grp + j) * 128:(grp + j + 1) * 128], idb,
                     wait_edges=(e_on if (grp == 0 and j == 0) else ()))
            e_o_tr = b.bank_read(bk, "vector", "tensor_copy",
                                 o_all[:, h * T + grp * 128: h * T + (grp + 4) * 128],
                                 b.pbank_bf[bk][:, 0:512])
    pl.p = mark
    e_all = b.fence()
    wob = pl.bf(8 * 1024)
    upd = pl.f32(512)
    e_wo = wload(b, wob, W["l2_wo_t"], wait_edges=e_all)
    e_upd = None
    for do in range(DT):
        for cch, sl in enumerate(CH512):
            bk = b.rr(7)
            b.mm_group(bk, [(b.pbank[bk][:, :], wob[:, kp * 1024 + do * 128:kp * 1024 + do * 128 + 128],
                             o_all[:, kp * T + sl.start: kp * T + sl.stop])
                            for kp in range(8)], wait_edges=[e_wo])
            b.bank_read(bk, "vector", "tensor_mul", upd, b.pbank[bk][:, :],
                        sg[do][:, sl], wait_edges=list(in_edges))
            e_upd = b.op("vector", "tensor_add", xT[do][:, sl], xT[do][:, sl], upd,
                         inc=True)
    return [e_upd]



HID_PAD = 2816
NH = HID_PAD // 128


def emit_l2_ffn(b, g, W, xn, sgr, sg, pl, xn_edges):
    y = pl.bf(8 * T)
    yt = [y[:, d * T:(d + 1) * T] for d in range(8)]
    shc0 = pl.bf(4 * 512)
    mark0 = pl.p
    idb = g["ident_bf"]
    rot64 = g["l2_rot64"]
    e64 = g["e64"]
    e64t = g["e64_t"]
    onesr_f = g["ones_f32_1x128"]
    ones_f = g["ones_f32_128x1"]
    qsks = W["l2f_qsks_sb"]
    lng = W["l2f_lng_sb"]
    lnb = W["l2f_lnb_sb"]
    b1a = W["l2f_b1a_sb"]
    b1g = W["l2f_b1g_sb"]
    b2c = W["l2f_b2_sb"]
    e_y = []
    for d in range(8):
        e_y.append(b.op("vector", "tensor_copy", yt[d], xn[d][:, :],
                        wait_edges=list(xn_edges) if d == 0 else [], inc=True))
    e_upd = e_y

    def build_shift_c0(waits):
        ee = []
        for j in range(4):
            b.op("vector", "memset", shc0[:, j * 512:j * 512 + 1], 0.0,
                 wait_edges=waits if j == 0 else [])
            ee.append(b.op("vector", "tensor_copy",
                           shc0[:, j * 512 + 1:(j + 1) * 512],
                           yt[4 + j][:, 0:511], inc=True))
        return ee

    def hs(d, lo, hi):
        if d < 4:
            return yt[d][:, lo:hi]
        if lo == 0:
            assert hi <= 512
            return shc0[:, (d - 4) * 512:(d - 4) * 512 + (hi - lo)]
        return yt[d][:, lo - 1:hi - 1]

    for l in range(2):
        b.mark(f"l2f.attn{l}")
        pl.p = mark0
        qk = pl.bf(2 * T)
        v_tok = pl.bf(16 * 128)
        P = pl.bf(2 * 512)
        PT = pl.bf(8 * 128)
        o_all = sgr[:, 0:8192]
        qnb = pl.bf(512)
        qraw = pl.f32(512)
        t1 = pl.f32(512)
        sq = pl.f32(512)
        n2 = pl.bf(T)
        sums = pl.f32(16)
        diag = pl.bf(128)
        wqb = pl.bf(8 * 128)
        wkb = pl.bf(8 * 128)
        wvb = pl.bf(8 * 128)
        trg = sgr[:, 8192:8192 + 4096]
        band = sgr[:, 12288:12288 + 1408]
        e_band = b.dma(band, W["ct_l2_band"], queue="sync")
        e_sh = build_shift_c0(e_upd)
        e_q_mm = None
        e_pv_last = None
        e_tr_last = None
        e_sum = None
        e_sc_last = None
        wsrc = W["l2f_wqkv"]
        for pair in range(4):
            wed = [e for e in (e_q_mm,) if e]
            e_wq = wload(b, wqb, wsrc[l * 1024:(l + 1) * 1024, pair * 128:(pair + 1) * 128],
                         wait_edges=wed)
            e_wk = wload(b, wkb, wsrc[l * 1024:(l + 1) * 1024, 512 + pair * 128:512 + (pair + 1) * 128],
                         wait_edges=wed)
            e_wv = wload(b, wvb, wsrc[l * 1024:(l + 1) * 1024, 1024 + pair * 128:1024 + (pair + 1) * 128],
                         wait_edges=wed)
            for ti in range(16):
                bk = b.rr(7)
                b.mm_group(bk, [(b.pbank[bk][:, 0:128],
                                 hs(k, ti * 128, (ti + 1) * 128),
                                 wvb[:, k * 128:(k + 1) * 128]) for k in range(DT)],
                           wait_edges=[e_wv] + e_sh)
                b.bank_read(bk, "vector", "tensor_copy",
                            v_tok[:, ti * 128:(ti + 1) * 128], b.pbank[bk][:, 0:128])
            for which, (wbuf, e_w, dst_off) in enumerate(
                    ((wqb, e_wq, 0), (wkb, e_wk, T))):
                for cch, sl in enumerate(CH512):
                    bk = b.rr(7)
                    e_q_mm = b.mm_group(bk, [(b.pbank[bk][:, :],
                                              wbuf[:, k * 128:(k + 1) * 128],
                                              hs(k, sl.start, sl.stop))
                                             for k in range(DT)],
                                        wait_edges=[e_w] + e_sh)
                    dsl = slice(dst_off + sl.start, dst_off + sl.stop)
                    e_r = b.bank_read(bk, "vector", "tensor_copy", qk[:, dsl],
                                      b.pbank[bk][:, :],
                                      wait_edges=[e_sc_last] if e_sc_last else [])
                    e_sq = b.op("scalar", "activation", sq, qk[:, dsl], AF.Square,
                                wait_edges=[e_r], inc=True)
                    bk2 = b.rr(7)
                    b.mm_group(bk2, [(b.pbank[bk2][0:2, :], e64t, sq)],
                               wait_edges=[e_sq])
                    b.bank_read(bk2, "vector", "tensor_copy", n2[0:2, sl],
                                b.pbank[bk2][0:2, :])
                e_n2 = b.op("scalar", "activation", n2[0:2, 0:T], n2[0:2, 0:T],
                            AF.Sqrt, inc=True)
                e_rn = b.op("vector", "tensor_scalar", n2[0:2, 0:T], n2[0:2, 0:T],
                            1e-12, None, ALU.max, wait_edges=[e_n2], inc=True)
                e_tq = [b.dma(trg[:, 0:T], W["ct_l2_cos" + ("q" if which == 0 else "k")],
                              queue="sync"),
                        b.dma(trg[:, T:2 * T], W["ct_l2_sin" + ("q" if which == 0 else "k")],
                              queue="sync")]
                for cch, sl in enumerate(CH512):
                    dsl = slice(dst_off + sl.start, dst_off + sl.stop)
                    e_rc = b.op("vector", "reciprocal", sq[0:2, :], n2[0:2, sl],
                                wait_edges=[e_rn], inc=True)
                    bk2 = b.rr(7)
                    b.mm_group(bk2, [(b.pbank[bk2][:, :], e64, sq[0:2, :])],
                               wait_edges=[e_rc])
                    e_rep = b.bank_read(bk2, "vector", "tensor_copy", qraw,
                                        b.pbank[bk2][:, :])
                    e_qn = b.op("vector", "scalar_tensor_tensor", qnb, qk[:, dsl],
                                qsks[:, 2 * l + which:2 * l + which + 1], qraw,
                                ALU.mult, ALU.mult, wait_edges=[e_rep], inc=True)
                    bk3 = b.rr(7)
                    b.mm_group(bk3, [(b.pbank[bk3][:, :], rot64, qnb)],
                               wait_edges=[e_qn])
                    b.op("vector", "tensor_mul", t1, qnb, trg[:, sl.start:sl.stop],
                         wait_edges=e_tq)
                    e_rt = b.bank_read(bk3, "vector", "scalar_tensor_tensor", qraw,
                                       b.pbank[bk3][:, :], 1.0,
                                       trg[:, T + sl.start:T + sl.stop],
                                       ALU.mult, ALU.mult)
                    e_sc_last = b.op("vector", "tensor_add", qk[:, dsl], t1, qraw,
                                     wait_edges=[e_rt], inc=True)
            e_rope = b.fence(engines=("vector",))
            for hh in range(2):
                h = 2 * pair + hh
                r0 = hh * 64
                qrows = qk[r0:r0 + 64, 0:T]
                krows = qk[r0:r0 + 64, T:2 * T]
                pvbk = 7
                for ti in range(16):
                    w = ti // 4
                    v_ = ti % 4
                    nch = 1 if w == 0 else 2
                    e_exp = None
                    for ci in range(nch):
                        kc = w - 1 + ci if w > 0 else 0
                        moff = (3 - v_) * 128 + (512 if (w == 0 or ci == 1) else 0)
                        bk = ci
                        b.mm_group(bk, [(b.pbank[bk][:, :],
                                         qrows[:, ti * 128:(ti + 1) * 128],
                                         krows[:, kc * 512:(kc + 1) * 512]),
                                        (b.pbank[bk][:, :], idb,
                                         band[:, moff:moff + 512])],
                                   wait_edges=(list(e_rope) + [e_band])
                                   if (ti == 0 and ci == 0) else ())
                    e_mx = b.op("vector", "reduce_max", sums[:, 10:11],
                                b.pbig[:, 0:nch * 512], AX.X,
                                wait_edges=[b.last_op["tensor"]])
                    e_nm = b.op("vector", "tensor_scalar", sums[:, 11:12],
                                sums[:, 10:11], -1.0, None, ALU.mult, inc=True)
                    for ci in range(nch):
                        ew = [e_nm] + [e for e in (e_tr_last, e_sum) if e]
                        e_exp = b.bank_read(ci, "scalar", "activation",
                                            P[:, ci * 512:(ci + 1) * 512],
                                            b.pbank[ci][:, :], AF.Exp,
                                            bias=sums[:, 11:12], scale=1.0,
                                            accum_out=sums[:, ci:ci + 1],
                                            wait_edges=ew)
                    e_sum = b.op("vector", "reduce_sum", sums[:, 8:9],
                                 sums[:, 0:nch], AX.X, wait_edges=[e_exp])
                    b.op("vector", "reciprocal", sums[:, 9:10], sums[:, 8:9])
                    e_diag = b.op("vector", "tensor_scalar", diag, idb,
                                  sums[:, 9:10], None, ALU.mult, inc=True,
                                  wait_edges=[e_tr_last] if e_tr_last else [])
                    ntj = nch * 4
                    for grp in range(0, ntj, 4):
                        bk = 4 + (grp // 4) % 2
                        for j in range(4):
                            b.mm_group(bk, [(b.pbank[bk][:, j * 128:(j + 1) * 128],
                                             P[:, (grp + j) * 128:(grp + j + 1) * 128],
                                             diag)],
                                       wait_edges=[e_diag] if j == 0 else ())
                        e_tr_last = b.bank_read(bk, "vector", "tensor_copy",
                                                PT[:, grp * 128:(grp + 4) * 128],
                                                b.pbank[bk][:, :])
                    kbase = 4 * (w - 1) if w > 0 else 0
                    b.mm_group(pvbk, [
                        (b.pbank[pvbk][0:64, v_ * 128:v_ * 128 + 128],
                         v_tok[:, (kbase + tjt) * 128 + r0:(kbase + tjt) * 128 + r0 + 64],
                         PT[:, tjt * 128:(tjt + 1) * 128]) for tjt in range(ntj)],
                        wait_edges=[e_tr_last])
                    if v_ == 3:
                        blk = ti // 4
                        e_pv_last = b.bank_read(
                            pvbk, "vector", "tensor_copy",
                            o_all[r0:r0 + 64, pair * T + blk * 512:pair * T + (blk + 1) * 512],
                            b.pbank[pvbk][0:64, :])
        e_allo = b.fence()
        wo2 = sgr[:, 8192:8192 + 4096]
        tevu = pl.f32(512)
        e_wo = wload(b, wo2, W["l2f_wo"][l * 512:(l + 1) * 512, :], wait_edges=e_allo)
        e_upd = []
        for do in range(DT):
            for cch, sl in enumerate(CH512):
                bk = b.rr(7)
                b.mm_group(bk, [(b.pbank[bk][:, :],
                                 wo2[:, kp * 1024 + do * 128:kp * 1024 + do * 128 + 128],
                                 o_all[:, kp * T + sl.start:kp * T + sl.stop])
                                for kp in range(4)], wait_edges=[e_wo])
                b.bank_read(bk, "vector", "tensor_copy", tevu, b.pbank[bk][:, :])
                e_u = b.op("vector", "tensor_add", yt[do][:, sl], yt[do][:, sl],
                           tevu, inc=True)
                e_upd.append(e_u)
        b.mark(f"l2f.mlp{l}")
        pl.p = mark0
        xn2 = sgr[:, 0:4096]
        hbuf = sgr[:, 4096:4096 + 5632]
        w1b = sgr[:, 9728:9728 + 1024]
        w2b = sgr[:, 10752:10752 + 1408]
        rows = pl.f32(2 * 512)
        rep = pl.f32(2 * 512)
        sqw = pl.f32(512)
        tev = pl.f32(512)
        tev2 = pl.f32(512)
        e_sh2 = build_shift_c0(e_upd)
        A = rows[0:1, 0:512]
        B = rows[0:1, 512:1024]
        tmp2 = rep[0:1, 0:512]
        e_w1_mm = None
        e_w2_mm = None
        for cch, sl in enumerate(CH512):
            b.mm_group(0, [(b.pbank[0][0:1, :], g["ones_bf_128x1"],
                            hs(d, sl.start, sl.stop)) for d in range(8)],
                       wait_edges=(e_sh2 + e_upd) if cch == 0 else ())
            b.bank_read(0, "vector", "tensor_copy", A, b.pbank[0][0:1, :])
            last_mm = None
            for d in range(8):
                e_s2 = b.op("scalar", "activation", sqw, hs(d, sl.start, sl.stop),
                            AF.Square, inc=True)
                last_mm = b.mm_step(b.pbank[1][0:1, :], ones_f, sqw,
                                    start=(d == 0), stop=(d == 7),
                                    wait_edges=[e_s2])
            e_b = b.op("vector", "tensor_copy", B, b.pbank[1][0:1, :],
                       wait_edges=[last_mm], inc=True)
            b.op("vector", "tensor_scalar", A, A, -1.0 / D, None, ALU.mult)
            b.op("vector", "tensor_scalar", B, B, 1.0 / D, None, ALU.mult)
            b.op("vector", "tensor_mul", tmp2, A, A)
            b.op("vector", "tensor_sub", B, B, tmp2)
            e_var = b.op("vector", "tensor_scalar", B, B, 1e-5, None, ALU.add,
                         inc=True)
            e_sqr = b.op("scalar", "activation", B, B, AF.Sqrt, wait_edges=[e_var],
                         inc=True)
            b.op("vector", "reciprocal", B, B, wait_edges=[e_sqr])
            e_stats = b.op("vector", "tensor_mul", A, A, B, inc=True)
            bk = b.rr(6)
            b.mm_group(bk, [(b.pbank[bk][:, :], onesr_f, B)], wait_edges=[e_stats])
            e_r1 = b.bank_read(bk, "vector", "tensor_copy", rep[:, 0:512],
                               b.pbank[bk][:, :])
            bk = b.rr(6)
            b.mm_group(bk, [(b.pbank[bk][:, :], onesr_f, A)], wait_edges=[e_stats])
            e_r2 = b.bank_read(bk, "vector", "tensor_copy", rep[:, 512:1024],
                               b.pbank[bk][:, :])
            e_xn2 = None
            for d in range(8):
                b.op("vector", "tensor_mul", tev, hs(d, sl.start, sl.stop),
                     rep[:, 0:512], wait_edges=[e_r1, e_r2])
                b.op("vector", "tensor_add", tev, tev, rep[:, 512:1024])
                e_xn2 = b.op("vector", "tensor_scalar", xn2[:, d * 512:(d + 1) * 512],
                             tev, lng[:, l * 8 + d:l * 8 + d + 1],
                             lnb[:, l * 8 + d:l * 8 + d + 1], ALU.mult, ALU.add,
                             inc=True)
            for half in range(2):
                hk0 = half * 11
                for i in range(11):
                    hi = hk0 + i
                    e_wa = wload(b, w1b[:, 0:8 * 128],
                                 W["l2f_w1a"][l * 1024:(l + 1) * 1024,
                                              hi * 128:(hi + 1) * 128],
                                 wait_edges=[e for e in (e_w1_mm,) if e])
                    bkA = b.rr(6)
                    b.mm_group(bkA, [(b.pbank[bkA][:, :],
                                      w1b[:, k * 128:(k + 1) * 128],
                                      xn2[:, k * 512:(k + 1) * 512])
                                     for k in range(8)],
                               wait_edges=[e_wa, e_xn2])
                    e_ta = b.bank_read(bkA, "vector", "tensor_scalar", tev,
                                       b.pbank[bkA][:, :],
                                       b1a[:, l * 22 + hi:l * 22 + hi + 1], None,
                                       ALU.add)
                    e_wg2 = wload(b, w2b[:, 0:8 * 128] if False else w1b[:, 0:8 * 128],
                                  W["l2f_w1g"][l * 1024:(l + 1) * 1024,
                                               hi * 128:(hi + 1) * 128],
                                  wait_edges=[e_ta])
                    bkG = b.rr(6)
                    e_w1_mm = b.mm_group(bkG, [(b.pbank[bkG][:, :],
                                                w1b[:, k * 128:(k + 1) * 128],
                                                xn2[:, k * 512:(k + 1) * 512])
                                               for k in range(8)],
                                         wait_edges=[e_wg2, e_xn2])
                    e_gg = b.bank_read(bkG, "scalar", "activation", tev2,
                                       b.pbank[bkG][:, :], AF.Gelu,
                                       bias=b1g[:, l * 22 + hi:l * 22 + hi + 1])
                    b.op("vector", "tensor_mul", hbuf[:, i * 512:(i + 1) * 512],
                         tev, tev2, wait_edges=[e_ta, e_gg], inc=True)
                e_h = b.fence(engines=("vector",))
                for do in range(DT):
                    e_wd = wload(b, w2b,
                                 W["l2f_w2"][l * HID_PAD + hk0 * 128:
                                             l * HID_PAD + (hk0 + 11) * 128,
                                             do * 128:(do + 1) * 128],
                                 wait_edges=[e for e in (e_w2_mm,) if e])
                    bk = b.rr(6)
                    e_w2_mm = b.mm_group(bk, [
                        (b.pbank[bk][:, :], w2b[:, k * 128:(k + 1) * 128],
                         hbuf[:, k * 512:(k + 1) * 512]) for k in range(11)],
                        wait_edges=[e_wd] + e_h)
                    if half == 0:
                        b.bank_read(bk, "vector", "tensor_copy", tev,
                                    b.pbank[bk][:, :])
                    else:
                        b.bank_read(bk, "vector", "tensor_scalar", tev,
                                    b.pbank[bk][:, :],
                                    b2c[:, l * 8 + do:l * 8 + do + 1], None,
                                    ALU.add)
                    e_u = b.op("vector", "tensor_add", yt[do][:, sl], yt[do][:, sl],
                               tev, inc=True)
            e_upd = [e_u]
    e_fin = b.fence(engines=("vector",))
    edges = [b.op("scalar", "activation", sg[d][:, :], y[:, d * T:(d + 1) * T],
                  AF.Silu, wait_edges=e_fin, inc=True) for d in range(DT)]
    return edges



def emit_l3(b, g, W, xT, pl, in_edges):
    CL = 256
    NCH = T // CL
    NB = 4
    idb = g["ident_bf"]
    onesb = g["ones_bf_128x1"]
    ones_f = g["ones_f32_128x1"]
    onesr_f = g["ones_f32_1x128"]
    sel16_buf = pl.bf(2048)
    sel16 = sel16_buf[0:16, :]
    e_sel = b.dma(sel16, W["ct_sel16"], queue="sync")
    xnc = pl.bf(8 * CL)
    xln = pl.bf(8 * CL)
    sgc = pl.bf(8 * CL)
    Psb = pl.bf(2 * CL)
    Qsb = pl.bf(2 * CL)
    xcb = pl.bf(16 * (CL + 8))
    zs = pl.bf(16 * CL)
    xconv = pl.bf(16 * CL)
    ygated = pl.bf(16 * CL)
    prd = pl.bf(CL)
    hst = prd
    prB = pl.bf(CL)
    hst2 = prB
    prC = pl.bf(CL)
    dt = pl.f32(CL)
    sp1 = pl.f32(CL)
    sp2 = pl.f32(CL)
    dtu = pl.f32(CL)
    dA = pl.bf(NB * CL)
    dBu = pl.bf(NB * CL)
    hsc = pl.bf(NB * CL)
    tmpb = pl.f32(NB * CL)
    pt12 = pl.f32(2 * CL)
    pt1 = pt12[:, 0:CL]
    pt2 = pt12[:, CL:2 * CL]
    Brep = pl.bf(16 * CL)
    Crep = pl.bf(16 * CL)
    yac = pl.f32(CL)
    state = pl.f32(256)
    rows = pl.f32(2 * CL)
    rep = pl.f32(2 * CL)
    sqw = pl.f32(CL)
    tev = pl.f32(CL)
    asb = pl.f32(256)
    wu1 = pl.bf(8 * 256)
    wv1 = pl.bf(2 * 4096)
    wu2r = [pl.bf(256), pl.bf(256)]
    wv2 = pl.bf(2 * 1024)
    winb = pl.bf(8 * 512)
    wxp = pl.bf(16 * 96)
    wdt = pl.bf(2048)
    wob = pl.bf(16 * 128)
    lnw = W["l3_lnw_sb"]
    lnbc = W["l3_lnb_sb"]
    cwc = W["l3_convw_sb"]
    cbc = W["l3_convb_sb"]
    bdtc = W["l3_bdt_sb"]
    dskc = W["l3_dsk_sb"]
    f1b = W["l3f_b1_sb"]
    f2b = W["l3f_b2_sb"]
    e_al = b.dma(asb.rearrange("p (a n) -> p a n", a=16),
                 W["l3_alog"].rearrange("(a p) n -> p a n", p=128),
                 queue="sync")
    e_ax = b.op("scalar", "activation", asb, asb, AF.Exp, wait_edges=[e_al],
                inc=True)
    e_A = b.op("vector", "tensor_scalar", asb, asb, -1.0, None, ALU.mult,
               wait_edges=[e_ax], inc=True)
    b.op("vector", "memset", state, 0.0)
    e_st0 = b.op("vector", "nop", inc=True)
    e_wu1 = wload(b, wu1, W["l3f_u1"])
    e_wv1 = wload(b, wv1, W["l3f_v1"])

    e_wv2 = wload(b, wv2, W["l3f_v2"])
    e_wxp = wload(b, wxp, W["l3_wxp_t"])
    e_wdt = b.dma(wdt[0:64, :], W["l3_wdt_t"], queue="gpsimd")
    A_ = rows[0:1, 0:CL]
    B_ = rows[0:1, CL:2 * CL]
    tmp2 = rep[0:1, 0:CL]
    e_upd = list(in_edges)
    e_win_mm = None
    e_wo_mm = None
    chunks = [slice(c * CL, (c + 1) * CL) for c in range(NCH)]
    for cch, sl in enumerate(chunks):
        def ln_chunk(src_tiles, dst, gcol, bcol, waits):
            ones1 = onesb if src_tiles(0).dtype == BF16 else ones_f
            b.mm_group(0, [(b.pbank[0][0:1, 0:CL], ones1, src_tiles(d))
                           for d in range(8)], wait_edges=waits)
            b.bank_read(0, "vector", "tensor_copy", A_, b.pbank[0][0:1, 0:CL])
            last_mm = None
            for d in range(8):
                e_s2 = b.op("scalar", "activation", sqw, src_tiles(d), AF.Square,
                            wait_edges=waits if d == 0 else [], inc=True)
                last_mm = b.mm_step(b.pbank[1][0:1, 0:CL], ones_f, sqw,
                                    start=(d == 0), stop=(d == 7),
                                    wait_edges=[e_s2])
            b.op("vector", "tensor_copy", B_, b.pbank[1][0:1, 0:CL],
                 wait_edges=[last_mm])
            b.op("vector", "tensor_scalar", A_, A_, -1.0 / D, None, ALU.mult)
            b.op("vector", "tensor_scalar", B_, B_, 1.0 / D, None, ALU.mult)
            b.op("vector", "tensor_mul", tmp2, A_, A_)
            b.op("vector", "tensor_sub", B_, B_, tmp2)
            e_var = b.op("vector", "tensor_scalar", B_, B_, 1e-5, None, ALU.add,
                         inc=True)
            e_sqr = b.op("scalar", "activation", B_, B_, AF.Sqrt,
                         wait_edges=[e_var], inc=True)
            b.op("vector", "reciprocal", B_, B_, wait_edges=[e_sqr])
            e_stats = b.op("vector", "tensor_mul", A_, A_, B_, inc=True)
            bk = b.rr(6)
            b.mm_group(bk, [(b.pbank[bk][:, 0:CL], onesr_f, B_)],
                       wait_edges=[e_stats])
            e_r1 = b.bank_read(bk, "vector", "tensor_copy", rep[:, 0:CL],
                               b.pbank[bk][:, 0:CL])
            bk = b.rr(6)
            b.mm_group(bk, [(b.pbank[bk][:, 0:CL], onesr_f, A_)],
                       wait_edges=[e_stats])
            e_r2 = b.bank_read(bk, "vector", "tensor_copy", rep[:, CL:2 * CL],
                               b.pbank[bk][:, 0:CL])
            ee = None
            for d in range(8):
                b.op("vector", "tensor_mul", tev, src_tiles(d), rep[:, 0:CL],
                     wait_edges=[e_r1, e_r2])
                if gcol is None:
                    ee = b.op("vector", "tensor_add", dst(d), tev,
                              rep[:, CL:2 * CL], inc=True)
                else:
                    b.op("vector", "tensor_add", tev, tev, rep[:, CL:2 * CL])
                    ee = b.op("vector", "tensor_scalar", dst(d), tev,
                              gcol(d), bcol(d), ALU.mult, ALU.add, inc=True)
            return ee
        e_xnc = ln_chunk(lambda d: xT[d][:, sl],
                         lambda d: xnc[:, d * CL:(d + 1) * CL], None, None,
                         e_upd)
        for m in range(2):
            bk = b.rr(6)
            b.mm_group(bk, [(b.pbank[bk][:, 0:CL],
                             wu1[:, k * 256 + m * 128:k * 256 + m * 128 + 128],
                             xnc[:, k * CL:(k + 1) * CL]) for k in range(8)],
                       wait_edges=[e_wu1, e_xnc])
            b.bank_read(bk, "vector", "tensor_copy", Psb[:, m * CL:(m + 1) * CL],
                        b.pbank[bk][:, 0:CL])
        e_P = b.fence(engines=("vector",))
        last_q = [None, None]
        prevF = None
        for mt in range(33):
            if mt < 32:
                bkH = b.rr(6)
                b.mm_group(bkH, [(b.pbank[bkH][:, 0:CL],
                                  wv1[:, k * 4096 + mt * 128:k * 4096 + mt * 128 + 128],
                                  Psb[:, k * CL:(k + 1) * CL]) for k in range(2)],
                           wait_edges=[e_wv1] + e_P)
                hb = hst if mt % 2 == 0 else hst2
                e_g = b.bank_read(bkH, "scalar", "activation", hb,
                                  b.pbank[bkH][:, 0:CL], AF.Gelu_apprx_tanh,
                                  bias=f1b[:, mt:mt + 1])
                wbuf = wu2r[mt % 2]
                e_wu2 = wload(b, wbuf, W["l3f_u2"][mt * 128:(mt + 1) * 128, :],
                              wait_edges=[last_q[1]] if last_q[1] is not None else [])
                curF = (hb, wbuf, e_g, e_wu2, mt)
            else:
                curF = None
            if prevF is not None:
                hbp, wbp, e_gp, e_w2p, mtp = prevF
                for m in range(2):
                    last_q[m] = b.mm_step(
                        b.pbank[6 + m][:, 0:CL],
                        wbp[:, m * 128:(m + 1) * 128],
                        hbp, start=(mtp == 0), stop=(mtp == 31),
                        wait_edges=[e_w2p, e_gp])
            prevF = curF
        for m in range(2):
            last_q[m].consumed = True
            b.op("vector", "tensor_copy", Qsb[:, m * CL:(m + 1) * CL],
                 b.pbank[6 + m][:, 0:CL], wait_edges=[last_q[m]], inc=True)
        e_Q = b.fence(engines=("vector",))
        for do in range(DT):
            bk = b.rr(6)
            b.mm_group(bk, [(b.pbank[bk][:, 0:CL],
                             wv2[:, k * 1024 + do * 128:k * 1024 + do * 128 + 128],
                             Qsb[:, k * CL:(k + 1) * CL]) for k in range(2)],
                       wait_edges=[e_wv2] + e_Q)
            e_t = b.bank_read(bk, "vector", "tensor_scalar", tev,
                              b.pbank[bk][:, 0:CL], f2b[:, do:do + 1], None,
                              ALU.add)
            b.op("scalar", "activation", sgc[:, do * CL:(do + 1) * CL], tev,
                 AF.Silu, wait_edges=[e_t], inc=True)
        e_sgc = b.fence(engines=("scalar",))
        e_xln = ln_chunk(lambda d: xnc[:, d * CL:(d + 1) * CL],
                         lambda d: xln[:, d * CL:(d + 1) * CL],
                         lambda d: lnw[:, d:d + 1], lambda d: lnbc[:, d:d + 1],
                         [e_xnc])
        e_carry = []
        for dti in range(16):
            c0 = dti * (CL + 8)
            if cch == 0:
                e_c0 = b.op("vector", "memset", xcb[:, c0:c0 + 3], 0.0, inc=True)
            else:
                e_c0 = b.op("vector", "tensor_copy", xcb[:, c0:c0 + 3],
                            xcb[:, c0 + CL:c0 + CL + 3], inc=True)
            e_carry.append(e_c0)
        for slab in range(8):
            e_w = wload(b, winb, W["l3_win_t"][:, slab * 512:(slab + 1) * 512],
                        wait_edges=[e for e in (e_win_mm,) if e])
            for mt in range(4):
                gi = slab * 4 + mt
                bk = b.rr(6)
                e_win_mm = b.mm_group(bk, [
                    (b.pbank[bk][:, 0:CL],
                     winb[:, k * 512 + mt * 128:k * 512 + mt * 128 + 128],
                     xln[:, k * CL:(k + 1) * CL]) for k in range(8)],
                    wait_edges=[e_w, e_xln])
                if gi < 16:
                    b.bank_read(bk, "vector", "tensor_copy",
                                xcb[:, gi * (CL + 8) + 3:gi * (CL + 8) + 3 + CL],
                                b.pbank[bk][:, 0:CL],
                                wait_edges=[e_carry[gi]])
                else:
                    t_ = gi - 16
                    b.bank_read(bk, "scalar", "activation",
                                zs[:, t_ * CL:(t_ + 1) * CL],
                                b.pbank[bk][:, 0:CL], AF.Silu)
        e_xz = b.fence()
        e_conv = []
        for dti in range(16):
            c0 = dti * (CL + 8)
            b.op("vector", "tensor_scalar", tev, xcb[:, c0:c0 + CL],
                 cwc[:, dti * 4:dti * 4 + 1], None, ALU.mult,
                 wait_edges=e_xz if dti == 0 else [])
            for kk in range(1, 4):
                b.op("vector", "scalar_tensor_tensor", tev,
                     xcb[:, c0 + kk:c0 + kk + CL],
                     cwc[:, dti * 4 + kk:dti * 4 + kk + 1], tev,
                     ALU.mult, ALU.add)
            e_c = b.op("scalar", "activation", xconv[:, dti * CL:(dti + 1) * CL],
                       tev, AF.Silu, bias=cbc[:, dti:dti + 1], inc=True)
            e_conv.append(e_c)
        bkd = b.rr(6)
        b.mm_group(bkd, [(b.pbank[bkd][0:64, 0:CL], wxp[:, k * 96:k * 96 + 64],
                          xconv[:, k * CL:(k + 1) * CL]) for k in range(16)],
                   wait_edges=[e_wxp, e_conv[-1]])
        b.bank_read(bkd, "vector", "tensor_copy", prd[0:64, :],
                    b.pbank[bkd][0:64, 0:CL])
        bkB = b.rr(6)
        b.mm_group(bkB, [(b.pbank[bkB][0:16, 0:CL], wxp[:, k * 96 + 64:k * 96 + 80],
                          xconv[:, k * CL:(k + 1) * CL]) for k in range(16)],
                   wait_edges=[e_wxp])
        b.bank_read(bkB, "vector", "tensor_copy", prB[0:16, :],
                    b.pbank[bkB][0:16, 0:CL])
        bkC = b.rr(6)
        b.mm_group(bkC, [(b.pbank[bkC][0:16, 0:CL], wxp[:, k * 96 + 80:k * 96 + 96],
                          xconv[:, k * CL:(k + 1) * CL]) for k in range(16)],
                   wait_edges=[e_wxp])
        b.bank_read(bkC, "vector", "tensor_copy", prC[0:16, :],
                    b.pbank[bkC][0:16, 0:CL])
        e_pr = b.fence(engines=("vector",))
        e_brep = None
        for n in range(16):
            bk = b.rr(6)
            b.mm_group(bk, [(b.pbank[bk][:, 0:CL], sel16[:, n * 128:(n + 1) * 128],
                             prB[0:16, :])], wait_edges=e_pr + [e_sel])
            b.bank_read(bk, "scalar", "copy", Brep[:, n * CL:(n + 1) * CL],
                        b.pbank[bk][:, 0:CL])
            bk = b.rr(6)
            b.mm_group(bk, [(b.pbank[bk][:, 0:CL], sel16[:, n * 128:(n + 1) * 128],
                             prC[0:16, :])], wait_edges=e_pr)
            e_brep = b.bank_read(bk, "scalar", "copy", Crep[:, n * CL:(n + 1) * CL],
                                 b.pbank[bk][:, 0:CL])
        e_yg = None
        for dti in range(16):
            bk = b.rr(6)
            b.mm_group(bk, [(b.pbank[bk][:, 0:CL], wdt[0:64, dti * 128:(dti + 1) * 128],
                             prd[0:64, :])], wait_edges=[e_wdt] + e_pr)
            e_t0 = b.bank_read(bk, "vector", "tensor_scalar", sp1,
                               b.pbank[bk][:, 0:CL], bdtc[:, dti:dti + 1], None,
                               ALU.add)
            e_a1 = b.op("scalar", "activation", sp2, sp1, AF.Abs,
                        wait_edges=[e_t0], inc=True)
            b.op("scalar", "activation", sp2, sp2, AF.Exp, scale=-1.0)
            e_a2 = b.op("scalar", "activation", sp2, sp2, AF.Ln, bias=1.0,
                        inc=True)
            b.op("vector", "tensor_relu", dt, sp1, wait_edges=[e_a2])
            e_dt = b.op("vector", "tensor_add", dt, dt, sp2, inc=True)
            e_dtu = b.op("gpsimd", "tensor_mul", dtu, dt,
                         xconv[:, dti * CL:(dti + 1) * CL],
                         wait_edges=[e_dt, e_conv[dti]], inc=True)
            for nb in range(16 // NB):
                for j in range(NB):
                    n = nb * NB + j
                    b.op("scalar", "activation", dA[:, j * CL:(j + 1) * CL],
                         dt, AF.Exp, scale=asb[:, dti * 16 + n:dti * 16 + n + 1],
                         wait_edges=[e_dt, e_A], inc=True)
                dtu3 = dtu.rearrange("p (o t) -> p o t", o=1).broadcast_to(
                    (128, NB, CL))
                dBu3 = dBu.rearrange("p (o t) -> p o t", o=NB)
                Brep3 = Brep[:, nb * NB * CL:(nb + 1) * NB * CL].rearrange(
                    "p (o t) -> p o t", o=NB)
                e_dbu = b.op("vector", "tensor_tensor", dBu3, dtu3, Brep3,
                             ALU.mult, wait_edges=[e_dtu, e_brep], inc=True)
                e_scan = []
                for j in range(NB):
                    n = nb * NB + j
                    e_s = b.op("vector", "tensor_tensor_scan",
                               hsc[:, j * CL:(j + 1) * CL],
                               dA[:, j * CL:(j + 1) * CL],
                               dBu[:, j * CL:(j + 1) * CL],
                               state[:, dti * 16 + n:dti * 16 + n + 1],
                               ALU.mult, ALU.add,
                               wait_edges=[e_dbu, e_st0], inc=True)
                    e_scan.append(e_s)
                hsc3 = hsc.rearrange("p (o t) -> p o t", o=NB)
                c0 = dti * 16 + nb * NB
                b.op("vector", "tensor_copy",
                     state[:, c0:c0 + NB].rearrange("p (o t) -> p o t", o=NB),
                     hsc3[:, :, CL - 1:CL], wait_edges=e_scan)
                tmp3o = tmpb.rearrange("p (o t) -> p o t", o=NB)
                Crep3 = Crep[:, nb * NB * CL:(nb + 1) * NB * CL].rearrange(
                    "p (o t) -> p o t", o=NB)
                e_tm = b.op("gpsimd", "tensor_tensor", tmp3o, hsc3, Crep3,
                            ALU.mult, wait_edges=e_scan, inc=True)
                e_p12 = b.op("gpsimd", "tensor_add", pt12, tmpb[:, 0:2 * CL],
                             tmpb[:, 2 * CL:4 * CL], wait_edges=[e_tm], inc=True)
                if nb == 0:
                    e_y = b.op("vector", "tensor_add", yac, pt1, pt2,
                               wait_edges=[e_p12], inc=True)
                else:
                    b.op("vector", "tensor_add", pt1, pt1, pt2,
                         wait_edges=[e_p12])
                    e_y = b.op("vector", "tensor_add", yac, yac, pt1, inc=True)
            b.op("vector", "scalar_tensor_tensor", yac,
                 xconv[:, dti * CL:(dti + 1) * CL], dskc[:, dti:dti + 1], yac,
                 ALU.mult, ALU.add, wait_edges=[e_y])
            e_yg = b.op("vector", "tensor_mul", ygated[:, dti * CL:(dti + 1) * CL],
                        yac, zs[:, dti * CL:(dti + 1) * CL], inc=True)
        e_fin = b.fence(engines=("vector",))
        for do in range(DT):
            e_wo = wload(b, wob, W["l3_wout_t"][:, do * 128:(do + 1) * 128],
                         wait_edges=[e for e in (e_wo_mm,) if e])
            bk = b.rr(6)
            e_wo_mm = b.mm_group(bk, [
                (b.pbank[bk][:, 0:CL], wob[:, k * 128:(k + 1) * 128],
                 ygated[:, k * CL:(k + 1) * CL]) for k in range(16)],
                wait_edges=[e_wo] + e_fin)
            b.bank_read(bk, "vector", "tensor_add", tev, b.pbank[bk][:, 0:CL],
                        xnc[:, do * CL:(do + 1) * CL])
            b.op("vector", "tensor_mul", tev, tev, sgc[:, do * CL:(do + 1) * CL],
                 wait_edges=e_sgc)
            e_u = b.op("vector", "tensor_add", xT[do][:, sl], xT[do][:, sl], tev,
                       inc=True)
        e_upd = [e_u]
    return e_upd


# version 09 (final)
# speedup vs baseline: 1.0600x; 1.0141x over previous
import contextlib
import numpy as np
import ml_dtypes
import concourse.bass as bass
import concourse.mybir as mybir
from concourse.bass_utils import run_bass_kernel_spmd

F32 = mybir.dt.float32
BF16 = mybir.dt.bfloat16
AF = mybir.ActivationFunctionType
ALU = mybir.AluOpType
AX = mybir.AxisListType
bfdt = ml_dtypes.bfloat16

T = 2048
D = 1024
NT = T // 512
DT = D // 128
MAGIC = 12582912.0
ARENA_COLS = 71040

LAYERS = [0, 1, 2, 3]
STAGES = ('ffn', 'attn')
N_EXPERTS = 16

CH512 = [slice(c * 512, (c + 1) * 512) for c in range(NT)]


class Sem:
    def __init__(self, h):
        self.h = h
        self.n = 0

    def inc(self, k=1):
        self.n += k
        return self.n


class OpRec:
    __slots__ = ("engine", "build", "waits", "uid", "val", "consumed", "is_dma",
                 "dma_sem", "dst_ap", "needs_clear", "epoch", "clear_dep")

    def __init__(self, engine, build, uid, is_dma=False):
        self.engine = engine
        self.build = build
        self.waits = set()
        self.uid = uid
        self.val = None
        self.consumed = False
        self.is_dma = is_dma


def _ap_range(ap):
    try:
        t = ap.tensor
    except AttributeError:
        return None
    import concourse.mybir as mb
    esz = mb.dt.size(ap.dtype)
    dims = [list(x) for x in ap.ap]
    off = ap.offset
    if not isinstance(off, int):
        return None
    if not dims:
        return (t.name, 0, 1, off * esz, (off + 1) * esz)
    pstep, pcount = dims[0]
    if pstep <= 0:
        pstep = 1
    p0 = off // pstep if pstep > 1 else 0
    col0 = off - p0 * pstep
    span = 1
    for st, ct in dims[1:]:
        span += abs(st) * (ct - 1)
    return (t.name, p0, p0 + pcount, col0 * esz, (col0 + span) * esz)


class MK:
    ENGINES = ("sync", "scalar", "vector", "tensor", "gpsimd")
    ESEM = {"vector": "dve", "scalar": "act", "tensor": "te", "gpsimd": "pool",
            "sync": "sp"}

    def __init__(self, nc, es):
        self.nc = nc
        self.es = es
        self.recs = []
        self.consts = {}
        self.semh = {}
        for name in ("te", "dve", "act", "pool", "sp"):
            self.semh[name] = es.enter_context(nc.semaphore("sem_" + name))
        self.NSYNCSEM = 8
        for i in range(self.NSYNCSEM):
            self.semh[f"dmah{i}"] = es.enter_context(nc.semaphore(f"sem_dmah{i}"))
        self.NSWSEM = 64
        for i in range(self.NSWSEM):
            self.semh[f"dmas{i}"] = es.enter_context(nc.semaphore(f"sem_dmas{i}"))
        self._dmah_rr = 0
        self._dmah_prev = [None] * self.NSYNCSEM
        self._dmah_count = [0] * self.NSYNCSEM
        self._dmas_rr = 0
        self._dmas_prev = [None] * 64
        self._dmas_count = [0] * 64
        self._epoch = 0
        self._ep_bar = {}
        self.psum = es.enter_context(nc.psum_tensor("pb", [128, 4096], F32))
        _pb = self.psum.ap() if hasattr(self.psum, "ap") else self.psum
        self.pbank = [_pb[:, i * 512:(i + 1) * 512] for i in range(8)]
        self.pbig = _pb
        _pbb = _pb.bitcast(BF16)
        self.pbank_bf = [_pbb[:, i * 1024:(i + 1) * 1024] for i in range(8)]
        self._rr = 0
        self._uid = 0
        self.segs = {}
        self.track_dram = False
        self.last_op = {e: None for e in self.ENGINES}

    def sb(self, name, shape, dtype):
        return self.es.enter_context(self.nc.sbuf_tensor(name, shape, dtype))

    def rr(self, mod=8):
        i = self._rr % mod
        self._rr += 1
        return i

    def const_input(self, name, arr):
        arr = np.ascontiguousarray(arr)
        self.consts[name] = arr
        dt = {np.dtype(np.float32): F32, np.dtype(bfdt): BF16}[arr.dtype]
        return self.nc.dram_tensor(name, list(arr.shape), dt, kind="ExternalInput").ap()

    def _dep_read(self, rec, ap):
        r = _ap_range(ap)
        if r is None:
            return
        name, p0, p1, b0, b1 = r
        if name not in self.segs:
            if not name.startswith(("arena", "xt", "pb", "c_", "scr_")):
                return
            self.segs[name] = []
        for seg in self.segs[name]:
            if seg[0] < p1 and p0 < seg[1] and seg[2] < b1 and b0 < seg[3]:
                if seg[4] is not None and seg[4] is not rec:
                    rec.waits.add(seg[4])
                seg[5].append(rec)

    def _dep_write(self, rec, ap, writer=None):
        r = _ap_range(ap)
        if r is None:
            return
        name, p0, p1, b0, b1 = r
        if name not in self.segs:
            if not name.startswith(("arena", "xt", "pb", "c_", "scr_")):
                return
            self.segs[name] = []
        wrec = writer if writer is not None else rec
        keep = []
        for seg in self.segs[name]:
            if seg[0] < p1 and p0 < seg[1] and seg[2] < b1 and b0 < seg[3]:
                if seg[4] is not None and seg[4] is not rec and seg[4] is not wrec:
                    rec.waits.add(seg[4])
                for rd in seg[5]:
                    if rd is not rec and rd is not wrec:
                        rec.waits.add(rd)
                if not (p0 <= seg[0] and seg[1] <= p1 and b0 <= seg[2]
                        and seg[3] <= b1):
                    keep.append(seg)
            else:
                keep.append(seg)
        keep.append([p0, p1, b0, b1, wrec, []])
        self.segs[name] = keep

    def _mkrec(self, engine, build, is_dma=False):
        self._uid += 1
        rec = OpRec(engine, build, self._uid, is_dma)
        rec.dst_ap = None
        rec.needs_clear = False
        rec.clear_dep = None
        rec.epoch = self._epoch
        bar = self._ep_bar.get(self._epoch - 1)
        if bar is not None and engine in bar:
            rec.waits.add(bar[engine])
        self.recs.append(rec)
        self.last_op[engine] = rec
        return rec

    def _dep_war(self, rec, ap):
        r = _ap_range(ap)
        if r is None or r[0] not in self.segs:
            return
        name, p0, p1, b0, b1 = r
        for seg in self.segs[name]:
            if seg[0] < p1 and p0 < seg[1] and seg[2] < b1 and b0 < seg[3]:
                if seg[4] is not None and seg[4] is not rec:
                    rec.waits.add(seg[4])
                for rd in seg[5]:
                    if rd is not rec:
                        rec.waits.add(rd)

    def dma(self, out_ap, in_ap, queue="sync", wait_edges=()):
        def build(e, o=out_ap, i=in_ap):
            return e.dma_start(out=o, in_=i)
        rec = self._mkrec(queue, build, is_dma=True)
        rec.dst_ap = out_ap
        if queue == "sync":
            si = self._dmah_rr % self.NSYNCSEM
            self._dmah_rr += 1
            rec.dma_sem = f"dmah{si}"
            self._dmah_count[si] += 16
            rec.val = self._dmah_count[si]
            prev = self._dmah_prev[si]
            if prev is not None:
                rec.waits.add(prev)
            self._dmah_prev[si] = rec
        else:
            si = self._dmas_rr % self.NSWSEM
            self._dmas_rr += 1
            rec.dma_sem = f"dmas{si}"
            self._dmas_count[si] += 16
            rec.val = self._dmas_count[si]
            prev = self._dmas_prev[si]
            if prev is not None:
                rec.waits.add(prev)
                self._dep_war(rec, prev.dst_ap)
            self._dmas_prev[si] = rec
        for w in wait_edges:
            if w is not None:
                rec.waits.add(w)
        self._dep_read(rec, in_ap)
        self._dep_write(rec, out_ap)
        rec.consumed = True
        return rec

    def op(self, engine, method, *args, wait_edges=(), inc=False, **kwargs):
        def build(e, m=method, a=args, k=kwargs):
            return getattr(e, m)(*a, **k)
        rec = self._mkrec(engine, build)
        for w in wait_edges:
            if w is not None:
                rec.waits.add(w)
        aps = [a for a in args if isinstance(a, bass.AP)]
        for a in aps[1:]:
            self._dep_read(rec, a)
        for kw, v in kwargs.items():
            if isinstance(v, bass.AP) and kw != "accum_out":
                self._dep_read(rec, v)
        if method != "nop" and aps:
            self._dep_write(rec, aps[0])
        ao = kwargs.get("accum_out")
        if isinstance(ao, bass.AP):
            self._dep_write(rec, ao)
        return rec

    def fence(self, engines=("tensor", "vector", "scalar", "gpsimd")):
        return [self.last_op[e] for e in engines if self.last_op[e] is not None]

    def mark(self, name):
        if not hasattr(self, "marks"):
            self.marks = []
            self._marksb = self.sb("marksb", [1, 4], F32)
        self.marks.append(name)
        self.op("gpsimd", "memset", self._marksb[0:1, 0:4], 0.0)

    def phase_barrier(self, extra_edges=()):
        edges = self.fence() + list(extra_edges)
        edges += [r for r in self._dmah_prev if r is not None]
        edges += [r for r in self._dmas_prev if r is not None]
        recs = []
        bar = {}
        for eng in ("tensor", "vector", "scalar", "gpsimd", "sync"):
            r = self.op(eng, "nop", wait_edges=edges)
            recs.append(r)
            bar[eng] = r
        self._ep_bar[self._epoch] = bar
        self._epoch += 1
        return recs

    def mm_group(self, bank_i, mms, wait_edges=()):
        n = len(mms)
        recs = []
        for j, (out_ap, lhsT, rhs) in enumerate(mms):
            start, stop = (j == 0), (j == n - 1)
            def build(e, o=out_ap, l=lhsT, r=rhs, st=start, sp=stop):
                return e.matmul(o, l, r, start=st, stop=sp)
            rec = self._mkrec("tensor", build)
            if j == 0:
                for w in wait_edges:
                    if w is not None:
                        rec.waits.add(w)
            self._dep_read(rec, lhsT)
            self._dep_read(rec, rhs)
            recs.append(rec)
        for out_ap in {id(m[0]): m[0] for m in mms}.values():
            self._dep_write(recs[0], out_ap, writer=recs[-1])
        return recs[-1]

    def mm_step(self, out_ap, lhsT, rhs, start, stop, wait_edges=()):
        def bld(e, o=out_ap, l=lhsT, r=rhs, st=start, sp=stop):
            return e.matmul(o, l, r, start=st, stop=sp)
        rec = self._mkrec("tensor", bld)
        for w in wait_edges:
            if w is not None:
                rec.waits.add(w)
        self._dep_read(rec, lhsT)
        self._dep_read(rec, rhs)
        self._dep_write(rec, out_ap)
        return rec

    def bank_te(self, bank_i, method, *args, wait_edges=(), **kwargs):
        return self.op("tensor", method, *args, wait_edges=wait_edges, **kwargs)

    def bank_read(self, bank_i, engine, method, *args, wait_edges=(), **kwargs):
        return self.op(engine, method, *args, wait_edges=wait_edges, **kwargs)

    def finalize(self):
        for rec in self.recs:
            for w in rec.waits:
                w.consumed = True
        for bar in self._ep_bar.values():
            for r in bar.values():
                r.consumed = True
        counters = {}
        semname = {}
        for rec in self.recs:
            if rec.is_dma:
                semname[rec.uid] = rec.dma_sem
                rec.consumed = True
                continue
            sname = self.ESEM[rec.engine]
            semname[rec.uid] = sname
            if rec.consumed:
                counters[sname] = counters.get(sname, 0) + 1
                rec.val = counters[sname]
        hw = {e: {} for e in self.ENGINES}
        plan = []
        for rec in self.recs:
            needs = {}
            for w in rec.waits:
                if w.val is None:
                    continue
                sname = semname[w.uid]
                if w.engine == rec.engine and not w.is_dma and not rec.is_dma:
                    pass
                cur = needs.get(sname, 0)
                if w.val > cur:
                    needs[sname] = w.val
            waits = []
            for sname, val in needs.items():
                if sname.startswith("dma"):
                    waits.append((self.semh[sname], val))
                    continue
                if hw[rec.engine].get(sname, 0) >= val:
                    continue
                hw[rec.engine][sname] = val
                waits.append((self.semh[sname], val))
            plan.append((rec, waits))
        block = self.es.enter_context(self.nc.Block())
        for name in self.ENGINES:
            def runner(eng, _n=name):
                for rec, waits in plan:
                    if rec.engine != _n:
                        continue
                    if rec.is_dma:
                        for sh, val in waits:
                            eng.wait_ge(sh, val)
                        inst = rec.build(eng)
                    else:
                        for sh, val in waits[1:]:
                            eng.wait_ge(sh, val)
                        inst = rec.build(eng)
                        if waits:
                            inst._wait_ge(waits[0][0], waits[0][1])
                    if rec.val is not None:
                        sname = semname[rec.uid]
                        inst.then_inc(self.semh[sname], 16 if rec.is_dma else 1)
            getattr(block, name)(runner)
        self.final_counts = counters


class Plan:

    def __init__(self, ar_ap, base=0):
        self.ar = ar_ap
        self.p = base

    def bf(self, ncols):
        off = (self.p + 15) & ~15
        self.p = off + ncols
        assert self.p <= ARENA_COLS, f"arena overflow: {self.p} > {ARENA_COLS}"
        return self.ar[:, off:off + ncols]

    def f32(self, ncols):
        off = (self.p + 15) & ~15
        self.p = off + 2 * ncols
        assert self.p <= ARENA_COLS, f"arena overflow: {self.p} > {ARENA_COLS}"
        return self.ar[:, off:off + 2 * ncols].bitcast(F32)



FCONSTS = [-20.0, 0.125, 1e-5, 1.0, 0.5, -1.0, 2.0, -0.5, 127.0, -128.0,
           12582912.0, -12582912.0, -3.0e38, 1e-12, 1e-5 ** 0.5]


def host_consts():
    c = {}
    c["fpool"] = np.tile(np.array(FCONSTS, np.float32)[None, :], (128, 1))
    c["ident_f32"] = np.eye(128, dtype=np.float32)
    c["ident_bf"] = np.eye(128, dtype=bfdt)
    c["ones_f32_128x1"] = np.ones((128, 1), dtype=np.float32)
    c["ones_bf_128x1"] = np.ones((128, 1), dtype=bfdt)
    c["ones_f32_1x128"] = np.ones((1, 128), dtype=np.float32)
    ROT, BASE = 32, 10000.0
    inv = 1.0 / BASE ** (np.arange(0, ROT, 2, dtype=np.float64) / ROT)
    fr = np.arange(T, dtype=np.float64)[:, None] * inv[None, :]
    cos, sin = np.cos(fr), np.sin(fr)
    COS = np.ones((128, T), dtype=np.float64)
    SIN = np.zeros((128, T), dtype=np.float64)
    for base in (0, 64):
        COS[base:base + 16] = cos.T
        COS[base + 16:base + 32] = cos.T
        SIN[base:base + 16] = -sin.T
        SIN[base + 16:base + 32] = sin.T
    c["l0_cos"] = COS.astype(bfdt)
    c["l0_sin"] = SIN.astype(bfdt)
    inv1 = 1.0 / 10000.0 ** (np.arange(0, 64, 2, dtype=np.float64) / 64)
    fr1 = np.arange(T, dtype=np.float64)[:, None] * inv1[None, :]
    c1, s1_ = np.cos(fr1), np.sin(fr1)
    COS1 = np.zeros((128, T), dtype=np.float64)
    SIN1 = np.zeros((128, T), dtype=np.float64)
    for r in range(128):
        p = (r % 64) // 2
        COS1[r] = c1[:, p]
        SIN1[r] = -s1_[:, p] if (r % 2 == 0) else s1_[:, p]
    c["l1_cos"] = COS1.astype(bfdt)
    c["l1_sin"] = SIN1.astype(bfdt)
    e64 = np.zeros((2, 128), dtype=np.float32)
    e64[0, 0:64] = 1.0
    e64[1, 64:128] = 1.0
    c["e64"] = e64
    c["e64_t"] = np.ascontiguousarray(e64.T)
    tri = np.triu(np.ones((128, 128), np.float32))
    c["l2_triu"] = tri.astype(bfdt)
    R64 = np.zeros((128, 128), np.float32)
    for blk in (0, 64):
        for i in range(32):
            R64[blk + 32 + i, blk + i] = -1.0
            R64[blk + i, blk + 32 + i] = 1.0
    c["l2_rot64"] = R64.astype(bfdt)
    band = np.full((128, 1408), -1e9, np.float32)
    for i in range(128):
        band[i, i + 384:i + 897] = 0.0
    c["l2_band"] = band.astype(bfdt)
    DH, WIN = 64, 512
    half_idx = np.arange(0, DH, 2, dtype=np.float64)
    invf = 1.0 / 10000.0 ** (half_idx / DH)
    tpos2 = np.arange(T, dtype=np.float64)
    fr2 = tpos2[:, None] * invf[None, :]
    fr2 = np.concatenate([fr2, fr2], axis=-1)
    base2 = (half_idx + 0.4 * DH) / (1.4 * DH)
    power = (tpos2 - T // 2) / (WIN // 2)
    xsc = base2[None, :] ** power[:, None]
    xsc = np.concatenate([xsc, xsc], axis=-1)
    cos2, sin2 = np.cos(fr2), np.sin(fr2)
    def tile128(a):
        return np.concatenate([a.T, a.T], axis=0)
    c["l2_cosq"] = (tile128(cos2 * xsc) * 8.0).astype(bfdt)
    c["l2_sinq"] = (tile128(sin2 * xsc) * 8.0).astype(bfdt)
    c["l2_cosk"] = tile128(cos2 / xsc).astype(bfdt)
    c["l2_sink"] = tile128(sin2 / xsc).astype(bfdt)
    sel = np.zeros((16, 2048), np.float32)
    for n in range(16):
        sel[n, n * 128:(n + 1) * 128] = 1.0
    c["sel16"] = sel.astype(bfdt)
    mv = np.zeros((4, 128, 512), dtype=np.float32)
    for v in range(4):
        qi = v * 128 + np.arange(128)[:, None]
        mv[v] = np.where(np.arange(512)[None, :] <= qi, 0.0, -1e9)
    c["l0_mask"] = np.ascontiguousarray(mv.transpose(1, 0, 2).reshape(128, 4 * 512)).astype(bfdt)
    return c



def emit_layernorm(b, g, xT, xn_tiles, pl, in_edges=()):
    ones_f = g["ones_f32_128x1"]
    onesr_f = g["ones_f32_1x128"]
    rows = pl.f32(2 * T)
    sqw = pl.f32(512 * DT)
    rep = pl.f32(2 * T)
    tmp = pl.f32(T)
    A = rows[0:1, 0:T]
    B = rows[0:1, T:2 * T]
    tmp2 = rep[0:1, 0:T]
    e_sq_mm = None
    for cch, sl in enumerate(CH512):
        ie = list(in_edges) if cch == 0 else []
        b.mm_group(0, [(b.pbank[0][0:1, :], ones_f[:, :], xT[d][:, sl])
                       for d in range(DT)], wait_edges=ie)
        e_list = [b.op("scalar", "activation", sqw[:, d * 512:(d + 1) * 512],
                       xT[d][:, sl], AF.Square,
                       wait_edges=ie + ([e_sq_mm] if e_sq_mm else []), inc=True)
                  for d in range(DT)]
        e_sq_mm = b.mm_group(1, [(b.pbank[1][0:1, :], ones_f[:, :],
                                  sqw[:, d * 512:(d + 1) * 512]) for d in range(DT)],
                             wait_edges=[e_list[-1]])
        b.bank_read(0, "vector", "tensor_copy", A[:, sl], b.pbank[0][0:1, :])
        e_ab = b.bank_read(1, "vector", "tensor_copy", B[:, sl], b.pbank[1][0:1, :])
    b.op("vector", "tensor_scalar", A, A, -1.0 / D, None, ALU.mult,
         wait_edges=[e_ab])
    b.op("vector", "tensor_scalar", B, B, 1.0 / D, None, ALU.mult)
    b.op("vector", "tensor_mul", tmp2, A, A)
    b.op("vector", "tensor_sub", B, B, tmp2)
    e_var = b.op("vector", "tensor_scalar", B, B, 1e-5, None, ALU.add, inc=True)
    e_sqr = b.op("scalar", "activation", B, B, AF.Sqrt,
                 wait_edges=[e_var], inc=True)
    b.op("vector", "reciprocal", B, B, wait_edges=[e_sqr])
    e_stats = b.op("vector", "tensor_mul", A, A, B, inc=True)
    rstd, nmr = B, A
    rstd_rep, nmr_rep = rep[:, 0:T], rep[:, T:2 * T]
    e_rep = None
    for cch, sl in enumerate(CH512):
        b.mm_group(2, [(b.pbank[2][:, :], onesr_f[:, :], rstd[:, sl])],
                   wait_edges=[e_stats])
        b.bank_read(2, "scalar", "copy", rstd_rep[:, sl], b.pbank[2][:, :])
        b.mm_group(3, [(b.pbank[3][:, :], onesr_f[:, :], nmr[:, sl])],
                   wait_edges=[e_stats])
        e_rep = b.bank_read(3, "scalar", "copy", nmr_rep[:, sl], b.pbank[3][:, :])
    edges = []
    for d in range(DT):
        b.op("vector", "tensor_mul", tmp[:, :], xT[d][:, :], rstd_rep[:, :],
             wait_edges=[e_rep] if d == 0 else [])
        e = b.op("vector", "tensor_add", xn_tiles[d][:, :], tmp[:, :],
                 nmr_rep[:, :], inc=True)
        edges.append(e)
    return edges


def wload(b, dst_ap2d, src_ap2d, queue="gpsimd", wait_edges=()):
    rows, cols = src_ap2d.shape
    a = rows // 128
    if a == 1:
        return b.dma(dst_ap2d, src_ap2d, queue=queue, wait_edges=wait_edges)
    src3 = src_ap2d.rearrange("(a p) n -> p a n", p=128)
    dst3 = dst_ap2d.rearrange("p (a n) -> p a n", a=a)
    return b.dma(dst3, src3, queue=queue, wait_edges=wait_edges)



def emit_l0_ffn(b, g, W, xn, sg, pl, xn_edges):
    onesr_f = g["ones_f32_1x128"]
    idf = g["ident_f32"]
    wg = pl.bf(DT * 16)
    lg = pl.f32(T)
    lt = pl.f32(16 * 16)
    cwt = pl.f32(16 * 16)
    cwf = lg
    sm = pl.f32(16)
    expv = pl.f32(16)
    msk = pl.f32(16)
    msk2 = pl.f32(16)
    ninf = pl.f32(16)
    w1buf = pl.bf(8 * 1024)
    w2buf = pl.bf(4 * 1024)
    mh = pl.bf(4 * T)
    gs = pl.bf(512)
    hbuf = pl.bf(512)
    cwrep = pl.bf(T)
    wsu = pl.bf(8 * 128)
    wsg = pl.bf(8 * 128)

    do_router = "router" in STAGES or "ffn" in STAGES
    do_experts = "experts" in STAGES or "ffn" in STAGES
    do_shared = "shared" in STAGES or "ffn" in STAGES
    n_experts = N_EXPERTS
    e_wg = wload(b, wg, W["l0_wg_t"])
    b.op("vector", "memset", ninf, -3.0e38)
    for d in range(DT):
        b.op("vector", "memset", sg[d][:, :], 0.0)
    e_zero = b.op("vector", "nop", inc=True)

    LV = globals().get("RLEVEL", 9)
    for cch, sl in enumerate(CH512) if (do_router and LV >= 2) else []:
        bk = b.rr()
        b.mm_group(bk, [(b.pbank[bk][0:16, :], wg[:, k * 16:(k + 1) * 16],
                         xn[k][:, sl]) for k in range(DT)],
                   wait_edges=[e_wg] + list(xn_edges))
        b.bank_read(bk, "scalar", "copy", lg[0:16, sl], b.pbank[bk][0:16, :])
    for ti in range(16) if (do_router and LV >= 3) else []:
        bk = b.rr()
        e = b.bank_te(bk, "transpose", b.pbank[bk][:, 0:16],
                      lg[0:16, ti * 128:(ti + 1) * 128], idf[0:16, 0:16])
        b.bank_read(bk, "vector", "tensor_copy", lt[:, ti * 16:(ti + 1) * 16],
                    b.pbank[bk][:, 0:16])
    e_cwt = []
    for ti in range(16) if (do_router and LV >= 4) else []:
        lt_t = lt[:, ti * 16:(ti + 1) * 16]
        b.op("vector", "reduce_max", sm[:, 0:1], lt_t, AX.X)
        e_nm = b.op("vector", "tensor_scalar", sm[:, 1:2], sm[:, 0:1], -1.0, None,
                    ALU.mult, inc=True)
        e_exp = b.op("scalar", "activation", expv, lt_t, AF.Exp, bias=sm[:, 1:2],
                     accum_out=sm[:, 2:3], wait_edges=[e_nm], inc=True)
        b.op("vector", "reciprocal", sm[:, 3:4], sm[:, 2:3], wait_edges=[e_exp])
        b.op("vector", "tensor_scalar", msk, lt_t, sm[:, 0:1], None, ALU.is_ge)
        b.op("vector", "scalar_tensor_tensor", msk2, msk, -1.0e9, lt_t,
             ALU.mult, ALU.add)
        b.op("vector", "reduce_max", sm[:, 4:5], msk2, AX.X)
        b.op("vector", "tensor_scalar", msk, lt_t, sm[:, 4:5], None, ALU.is_ge)
        b.op("vector", "tensor_mul", expv, expv, msk)
        e = b.op("vector", "tensor_scalar", cwt[:, ti * 16:(ti + 1) * 16], expv,
                 sm[:, 3:4], None, ALU.mult, inc=True)
        e_cwt.append(e)
    e_cwf = [e_cwt[-1]] if e_cwt else []

    e_up_last = None
    e_dn_last = None
    e_mh_last = None
    for e_i in range(n_experts if do_experts else 0):
        we1 = [e for e in (e_up_last,) if e]
        e_w1 = wload(b, w1buf, W["l0_w1"][e_i * 1024:(e_i + 1) * 1024, :], wait_edges=we1)
        e_w2 = wload(b, w2buf, W["l0_w2"][e_i * 512:(e_i + 1) * 512, :],
                     wait_edges=[e for e in (e_dn_last,) if e])
        e_cwrep = None
        for ti in range(16):
            bk = b.rr()
            lhs = cwt[:, ti * 16 + e_i: ti * 16 + e_i + 1].broadcast_to((128, 128))
            b.mm_group(bk, [(b.pbank[bk][:, 0:128], lhs, idf)],
                       wait_edges=e_cwf)
            e_cwrep = b.bank_read(bk, "scalar", "copy",
                                  cwrep[:, ti * 128:(ti + 1) * 128],
                                  b.pbank[bk][:, 0:128])
        for mp in range(4):
            for cch, sl in enumerate(CH512):
                bkU = b.rr(); bkG = b.rr()
                e_up_last = b.mm_group(bkU, [
                    (b.pbank[bkU][:, :], w1buf[:, k * 1024 + mp * 128:k * 1024 + mp * 128 + 128],
                     xn[k][:, sl]) for k in range(DT)], wait_edges=[e_w1])
                e_up_last = b.mm_group(bkG, [
                    (b.pbank[bkG][:, :], w1buf[:, k * 1024 + 512 + mp * 128:k * 1024 + 512 + mp * 128 + 128],
                     xn[k][:, sl]) for k in range(DT)], wait_edges=[e_w1])
                e_gs = b.bank_read(bkG, "scalar", "activation", gs, b.pbank[bkG][:, :],
                                   AF.Silu)
                b.bank_read(bkU, "vector", "scalar_tensor_tensor", hbuf,
                            b.pbank[bkU][:, :], 1.0, gs, ALU.mult, ALU.mult,
                            wait_edges=[e_gs, e_cwrep] +
                                       ([e_dn_last] if e_dn_last else []))
                e_mh_last = b.op("vector", "tensor_mul", mh[:, mp * T + sl.start: mp * T + sl.stop],
                                 hbuf, cwrep[:, sl], inc=True)
        for do in range(8):
            for cch, sl in enumerate(CH512):
                bk = b.rr()
                e_dn_last = b.mm_group(bk, [
                    (b.pbank[bk][:, :], w2buf[:, hk * 1024 + do * 128:hk * 1024 + do * 128 + 128],
                     mh[:, hk * T + sl.start: hk * T + sl.stop]) for hk in range(4)],
                    wait_edges=[e_w2, e_mh_last])
                b.bank_read(bk, "vector", "scalar_tensor_tensor", sg[do][:, sl],
                            b.pbank[bk][:, :], 1.0, sg[do][:, sl], ALU.mult, ALU.add)
    for half in range(2 if do_shared else 0):
        e_hs_last = None
        for mpl in range(4):
            mp = half * 4 + mpl
            e_wu = wload(b, wsu, W["l0_ws1"][:, mp * 128:(mp + 1) * 128],
                         wait_edges=[e for e in (e_up_last,) if e])
            e_wg2 = wload(b, wsg, W["l0_ws1"][:, 1024 + mp * 128:1024 + (mp + 1) * 128],
                          wait_edges=[e for e in (e_up_last,) if e])
            for cch, sl in enumerate(CH512):
                bkU = b.rr(); bkG = b.rr()
                b.mm_group(bkU, [(b.pbank[bkU][:, :], wsu[:, k * 128:(k + 1) * 128],
                                  xn[k][:, sl]) for k in range(DT)], wait_edges=[e_wu])
                e_up_last = b.mm_group(bkG, [(b.pbank[bkG][:, :], wsg[:, k * 128:(k + 1) * 128],
                                              xn[k][:, sl]) for k in range(DT)],
                                       wait_edges=[e_wg2])
                e_gs = b.bank_read(bkG, "scalar", "activation", gs,
                                   b.pbank[bkG][:, :], AF.Silu)
                e_hs_last = b.bank_read(bkU, "vector", "tensor_copy", hbuf,
                                        b.pbank[bkU][:, :])
                e_hs_last = b.op("vector", "tensor_mul",
                                 mh[:, mpl * T + sl.start: mpl * T + sl.stop],
                                 hbuf, gs, wait_edges=[e_gs], inc=True)
        e_ws2 = wload(b, w2buf, W["l0_ws2"][half * 512:(half + 1) * 512, :],
                      wait_edges=[e for e in (e_dn_last,) if e])
        for do in range(8):
            for cch, sl in enumerate(CH512):
                bk = b.rr()
                e_dn_last = b.mm_group(bk, [
                    (b.pbank[bk][:, :], w2buf[:, hk * 1024 + do * 128:hk * 1024 + do * 128 + 128],
                     mh[:, hk * T + sl.start: hk * T + sl.stop]) for hk in range(4)],
                    wait_edges=[e_ws2, e_hs_last])
                b.bank_read(bk, "vector", "scalar_tensor_tensor", sg[do][:, sl],
                            b.pbank[bk][:, :], 1.0, sg[do][:, sl], ALU.mult, ALU.add)
    e_z = b.fence(engines=("vector",))
    edges = [b.op("scalar", "activation", sg[d][:, :], sg[d][:, :], AF.Silu,
                  wait_edges=e_z, inc=True) for d in range(DT)]
    return edges



SWAP16 = list(range(16, 32)) + list(range(0, 16))


def emit_l0_attn(b, g, W, xn, xT, sg, pl, in_edges):
    o_all = pl.bf(8 * T)
    mark = pl.p
    qk = pl.bf(2 * T)
    v_tok = pl.bf(16 * 128)
    P = pl.bf(T)
    PT = pl.bf(16 * 128)
    qraw = pl.bf(512)
    shf = pl.bf(512)
    t1 = pl.bf(512)
    t2 = pl.bf(512)
    sums = pl.f32(16)
    diag = pl.bf(128)
    wqb = pl.bf(8 * 128)
    wkb = pl.bf(8 * 128)
    wvb = pl.bf(8 * 128)
    cosT = pl.bf(T)
    sinT = pl.bf(T)
    masks = pl.bf(4 * 512)
    e_ct = [b.dma(cosT, W["ct_l0_cos"], queue="sync"),
            b.dma(sinT, W["ct_l0_sin"], queue="sync"),
            b.dma(masks, W["ct_l0_mask"], queue="sync")]
    idb = g["ident_bf"]
    e_q_mm = None
    e_v_evac = None
    e_pv_last = None
    e_tr_last = None
    e_sc_last = None
    e_sum = None
    e_qread = None
    for pair in range(8):
        wed = [e for e in (e_q_mm,) if e]
        e_wq = wload(b, wqb, W["l0_wq"][:, pair * 128:(pair + 1) * 128], wait_edges=wed)
        e_wk = wload(b, wkb, W["l0_wkv"][:, pair * 128:(pair + 1) * 128], wait_edges=wed)
        e_wv = wload(b, wvb, W["l0_wkv"][:, 1024 + pair * 128:1024 + (pair + 1) * 128],
                     wait_edges=wed)
        for ti in range(16):
            bk = b.rr(7)
            b.mm_group(bk, [(b.pbank[bk][:, 0:128], xn[k][:, ti * 128:(ti + 1) * 128],
                             wvb[:, k * 128:(k + 1) * 128]) for k in range(DT)],
                       wait_edges=[e_wv] + list(in_edges) +
                                  ([e_pv_last] if (e_pv_last and ti == 0) else []))
            e_v_evac = b.bank_read(bk, "vector", "tensor_copy",
                                   v_tok[:, ti * 128:(ti + 1) * 128], b.pbank[bk][:, 0:128])
        for wbuf, e_w, dst_off in ((wqb, e_wq, 0), (wkb, e_wk, T)):
            for cch, sl in enumerate(CH512):
                bk = b.rr(7)
                e_q_mm = b.mm_group(bk, [(b.pbank[bk][:, :], wbuf[:, k * 128:(k + 1) * 128],
                                          xn[k][:, sl]) for k in range(DT)],
                                    wait_edges=[e_w])
                e_r = b.bank_read(bk, "scalar", "copy", qraw, b.pbank[bk][:, :],
                                  wait_edges=[e_qread] if e_qread else [])
                b.op("vector", "stream_shuffle", shf, qraw, SWAP16, wait_edges=[e_r])
                e_qread = b.op("vector", "tensor_mul", t1, qraw, cosT[:, sl], inc=True)
                b.op("vector", "tensor_mul", t2, shf, sinT[:, sl])
                dsl = slice(dst_off + sl.start, dst_off + sl.stop)
                b.op("vector", "tensor_add", qk[:, dsl], t1, t2, inc=True,
                     wait_edges=[e_sc_last] if e_sc_last else [])
        e_rope = b.fence(engines=("vector",))
        if pair == 0 and globals().get("DBG_L1"):
            QKD = b.nc.dram_tensor("qk_dbg", [128, 2 * T], BF16,
                                   kind="ExternalOutput").ap()
            b.dma(QKD, qk, queue="sync", wait_edges=e_rope)
        for hh in range(2):
            h = 2 * pair + hh
            r0 = hh * 64
            qrows = qk[r0:r0 + 64, 0:T]
            krows = qk[r0:r0 + 64, T:2 * T]
            pvbk = 7
            for ti in range(16):
                cdiag = (ti * 128) // 512
                e_exp = None
                for c in range(cdiag + 1):
                    bk = b.rr(7)
                    mms = [(b.pbank[bk][:, :], qrows[:, ti * 128:(ti + 1) * 128],
                            krows[:, c * 512:(c + 1) * 512])]
                    if c == cdiag:
                        v = ti % 4
                        mms.append((b.pbank[bk][:, :], idb,
                                    masks[:, v * 512:(v + 1) * 512]))
                    e_sc_last = b.mm_group(bk, mms,
                                           wait_edges=list(e_rope) if (ti == 0 and c == 0) else ())
                    ew = [e for e in (e_tr_last, e_sum) if e]
                    e_exp = b.bank_read(bk, "scalar", "activation",
                                        P[:, c * 512:(c + 1) * 512], b.pbank[bk][:, :],
                                        AF.Exp, bias=-20.0, scale=0.125,
                                        accum_out=sums[:, c:c + 1], wait_edges=ew)
                e_sum = b.op("vector", "reduce_sum", sums[:, 8:9],
                             sums[:, 0:cdiag + 1], AX.X, wait_edges=[e_exp])
                b.op("vector", "reciprocal", sums[:, 9:10], sums[:, 8:9])
                e_diag = b.op("vector", "tensor_scalar", diag, idb, sums[:, 9:10],
                              None, ALU.mult, inc=True,
                              wait_edges=[e_tr_last] if e_tr_last else [])
                ntj = ti + 1
                for grp in range(0, ntj, 4):
                    bk = b.rr(7)
                    nj = min(4, ntj - grp)
                    for j in range(nj):
                        b.mm_group(bk, [(b.pbank[bk][:, j * 128:(j + 1) * 128],
                                         P[:, (grp + j) * 128:(grp + j + 1) * 128], diag)],
                                   wait_edges=[e_diag] if j == 0 else ())
                    e_tr_last = b.bank_read(bk, "vector", "tensor_copy",
                                            PT[:, grp * 128:grp * 128 + nj * 128],
                                            b.pbank[bk][:, 0:nj * 128])
                b.mm_group(pvbk, [(b.pbank[pvbk][0:64, (ti % 4) * 128:(ti % 4) * 128 + 128],
                                   v_tok[:, tjt * 128 + r0:tjt * 128 + r0 + 64],
                                   PT[:, tjt * 128:(tjt + 1) * 128])
                                  for tjt in range(ntj)],
                           wait_edges=[e_tr_last, e_v_evac])
                if ti % 4 == 3:
                    blk = ti // 4
                    e_pv_last = b.bank_read(
                        pvbk, "vector", "tensor_copy",
                        o_all[r0:r0 + 64, pair * T + blk * 512: pair * T + (blk + 1) * 512],
                        b.pbank[pvbk][0:64, :])
    pl.p = mark
    e_all = b.fence()
    wob = pl.bf(8 * 1024)
    upd = pl.f32(512)
    e_wo = wload(b, wob, W["l0_wout"], wait_edges=e_all)
    e_upd = None
    for do in range(DT):
        for cch, sl in enumerate(CH512):
            bk = b.rr(7)
            b.mm_group(bk, [(b.pbank[bk][:, :], wob[:, kp * 1024 + do * 128:kp * 1024 + do * 128 + 128],
                             o_all[:, kp * T + sl.start: kp * T + sl.stop])
                            for kp in range(8)], wait_edges=[e_wo])
            b.bank_read(bk, "vector", "tensor_mul", upd, b.pbank[bk][:, :],
                        sg[do][:, sl], wait_edges=list(in_edges))
            e_upd = b.op("vector", "tensor_add", xT[do][:, sl], xT[do][:, sl], upd,
                         inc=True)
    return [e_upd]



def build_program(layers=(0,), dump=()):
    nc = bass.Bass("TRN2", target_bir_lowering=False, debug=False)
    with contextlib.ExitStack() as es:
        b = MK(nc, es)
        X = nc.dram_tensor("x_in", [D, T], F32, kind="ExternalInput").ap()
        OUT = nc.dram_tensor("x_out", [D, T], F32, kind="ExternalOutput").ap()
        W = {}
        def win(name, shape):
            W[name] = nc.dram_tensor(name, list(shape), F32, kind="ExternalInput").ap()
        if 0 in layers:
            win("l0_wq", (D, D)); win("l0_wkv", (D, 2 * D)); win("l0_wout", (D, D))
            win("l0_wg_t", (D, 16)); win("l0_w1", (16 * 1024, 1024))
            win("l0_w2", (16 * 512, 1024)); win("l0_ws1", (D, 2048)); win("l0_ws2", (D, D))
        if 3 in layers:
            win("l3_win_t", (D, 4096)); win("l3_wxp_t", (2048, 96))
            win("l3_wdt_t", (64, 2048)); win("l3_wout_t", (2048, D))
            win("l3_alog", (2048, 16))
            win("l3_lnw", (128, 8)); win("l3_lnb", (128, 8))
            win("l3_convw", (128, 64)); win("l3_convb", (128, 16))
            win("l3_bdt", (128, 16)); win("l3_dsk", (128, 16))
            win("l3f_u1", (D, 256)); win("l3f_v1", (256, 4096))
            win("l3f_u2", (4096, 256)); win("l3f_v2", (256, D))
            win("l3f_b1", (128, 32)); win("l3f_b2", (128, 8))
        if 2 in layers:
            win("l2_wq_t", (D, D)); win("l2_wk_t", (D, D)); win("l2_wv_t", (D, D))
            win("l2_wo_t", (D, D)); win("l2_fm_t", (128, 512))
            win("l2_fmb", (128, 4)); win("l2_rms", (128, 128))
            win("l2f_wqkv", (2 * D, 1536)); win("l2f_wo", (2 * 512, D))
            win("l2f_w1a", (2 * D, HID_PAD)); win("l2f_w1g", (2 * D, HID_PAD))
            win("l2f_w2", (2 * HID_PAD, D))
            win("l2f_qsks", (128, 4)); win("l2f_lng", (128, 16))
            win("l2f_lnb", (128, 16)); win("l2f_b1a", (128, 44))
            win("l2f_b1g", (128, 44)); win("l2f_b2", (128, 16))
        if 1 in layers:
            win("l1_wq_t", (D, D)); win("l1_wk_t", (D, D)); win("l1_wv_t", (D, D))
            win("l1_wo_t", (D, D)); win("l1_sqk", (128, 8))
            win("l1_wg_t", (D, 4096)); win("l1_wu_t", (D, 4096)); win("l1_wd_t", (4096, D))
            for qn, shp in (("scr_l1_qg", (D, 4096)), ("scr_l1_qu", (D, 4096)),
                            ("scr_l1_qd", (4096, D))):
                W[qn] = nc.dram_tensor(qn, list(shp), BF16, kind="Internal").ap()
        carr = host_consts()
        BIGC = ("l0_cos", "l0_sin", "l1_cos", "l1_sin", "l0_mask", "sel16",
                "l2_band", "l2_cosq", "l2_sinq", "l2_cosk", "l2_sink")
        g = {}
        e_c = []
        for name, arr in carr.items():
            ap = b.const_input(name, arr)
            if name in BIGC:
                W["ct_" + name] = ap
                continue
            dt = {np.dtype(np.float32): F32, np.dtype(bfdt): BF16}[arr.dtype]
            t = b.sb("c_" + name, list(arr.shape), dt)
            g[name] = t.ap() if hasattr(t, "ap") else t
            e_c.append(b.dma(g[name], ap, queue="sync"))
        for i, v in enumerate(FCONSTS):
            nc.const_aps.register(F32, float(v), g["fpool"][:, i:i + 1])
        if 3 in layers:
            for nm, cols in (("l3_lnw", 8), ("l3_lnb", 8), ("l3_convw", 64),
                             ("l3_convb", 16), ("l3_bdt", 16), ("l3_dsk", 16),
                             ("l3f_b1", 32), ("l3f_b2", 8)):
                tt = b.sb("c_" + nm, [128, cols], F32)
                ap_ = tt.ap() if hasattr(tt, "ap") else tt
                b.dma(ap_, W[nm], queue="sync")
                W[nm + "_sb"] = ap_
        if 2 in layers:
            for nm, cols in (("l2_fmb", 4), ("l2_rms", 128), ("l2f_qsks", 4),
                             ("l2f_lng", 16), ("l2f_lnb", 16), ("l2f_b1a", 44),
                             ("l2f_b1g", 44), ("l2f_b2", 16)):
                tt = b.sb("c_" + nm, [128, cols], F32)
                ap_ = tt.ap() if hasattr(tt, "ap") else tt
                b.dma(ap_, W[nm], queue="sync")
                W[nm + "_sb"] = ap_
        if 1 in layers:
            sqk_sb = b.sb("c_l1_sqk", [128, 8], F32)
            b.dma(sqk_sb.ap() if hasattr(sqk_sb, "ap") else sqk_sb, W["l1_sqk"],
                  queue="sync")
            W["l1_sqk_sb"] = sqk_sb.ap() if hasattr(sqk_sb, "ap") else sqk_sb
        xTt = [b.sb(f"xt{d}", [128, T], F32) for d in range(DT)]
        xT = [t.ap() for t in xTt]
        arena = b.sb("arena", [128, ARENA_COLS], BF16).ap()
        e_x = [b.dma(xT[d], X[d * 128:(d + 1) * 128, :], queue="sync")
               for d in range(DT)]
        xn = [arena[:, d * T:(d + 1) * T] for d in range(DT)]
        sg = [arena[:, (DT + d) * T:(DT + d + 1) * T] for d in range(DT)]
        edges = b.phase_barrier(extra_edges=e_x + e_c)
        dbg_out = {}
        for li in layers:
            base = 2 * DT * T
            if li == 0:
                b.mark("l0.ln")
                e_xn = emit_layernorm(b, g, xT, xn, Plan(arena, base), in_edges=edges)
                b.phase_barrier()
                if any(x in STAGES for x in ("ffn", "router", "experts", "shared")):
                    b.mark("l0.ffn")
                    e_sg = emit_l0_ffn(b, g, W, xn, sg, Plan(arena, base), e_xn)
                    b.phase_barrier()
                else:
                    e_sg = []
                if "attn" in STAGES:
                    b.mark("l0.attn")
                    edges = emit_l0_attn(b, g, W, xn, xT, sg, Plan(arena, base),
                                         e_xn + e_sg)
                edges = edges + b.phase_barrier()
            elif li == 3:
                b.mark("l3")
                edges = emit_l3(b, g, W, xT, Plan(arena, 0), edges)
                edges = edges + b.phase_barrier()
            elif li == 2:
                b.mark("l2.ln")
                e_xn = emit_layernorm(b, g, xT, xn, Plan(arena, base), in_edges=edges)
                b.phase_barrier()
                sgr = arena[:, DT * T:2 * DT * T]
                b.mark("l2.ffn")
                e_sg = emit_l2_ffn(b, g, W, xn, sgr, sg, Plan(arena, base), e_xn)
                b.phase_barrier()
                b.mark("l2.attn")
                edges = emit_l2_attn(b, g, W, xn, xT, sg, Plan(arena, base),
                                     e_xn + e_sg)
                edges = edges + b.phase_barrier()
            elif li == 1:
                b.mark("l1.ln")
                e_xn = emit_layernorm(b, g, xT, xn, Plan(arena, base), in_edges=edges)
                b.phase_barrier()
                b.mark("l1.ffn")
                e_sg = emit_l1_ffn(b, g, W, xn, sg, Plan(arena, base), e_xn)
                b.phase_barrier()
                b.mark("l1.attn")
                edges = emit_l1_attn(b, g, W, xn, xT, sg, Plan(arena, base),
                                     e_xn + e_sg)
                edges = edges + b.phase_barrier()
            else:
                raise NotImplementedError(li)
            b.mark(f"l{li}.end")
            if "xn" in dump and li == max(layers):
                XN = nc.dram_tensor("xn_out", [D, T], BF16, kind="ExternalOutput").ap()
                for d in range(DT):
                    b.dma(XN[d * 128:(d + 1) * 128, :], xn[d], queue="sync",
                          wait_edges=edges)
            if "sg" in dump and li == max(layers):
                SG = nc.dram_tensor("sg_out", [D, T], BF16, kind="ExternalOutput").ap()
                for d in range(DT):
                    b.dma(SG[d * 128:(d + 1) * 128, :], sg[d], queue="sync",
                          wait_edges=edges)
        e_out = [b.dma(OUT[d * 128:(d + 1) * 128, :], xT[d], queue="sync",
                       wait_edges=edges) for d in range(DT)]
        b.op("sync", "nop", wait_edges=e_out)
        b.finalize()
    global LAST_MARKS
    LAST_MARKS = list(getattr(b, "marks", []))
    return nc, b.consts


def host_weights_l1(inputs):
    w = {}
    for k, nm in (("L1_attn_Wq", "l1_wq_t"), ("L1_attn_Wk", "l1_wk_t"),
                  ("L1_attn_Wv", "l1_wv_t"), ("L1_attn_Wo", "l1_wo_t")):
        w[nm] = np.ascontiguousarray(np.asarray(inputs[k]).T, np.float32)
    sq = (np.asarray(inputs["L1_attn_s_qk"], np.float32) * 32.0).reshape(16, 64)
    col = np.zeros((128, 8), np.float32)
    for p in range(8):
        col[0:64, p] = sq[2 * p]
        col[64:128, p] = sq[2 * p + 1]
    w["l1_sqk"] = col
    w["l1_wg_t"] = np.ascontiguousarray(np.asarray(inputs["L1_ffn_w_gate"]).T, np.float32)
    w["l1_wu_t"] = np.ascontiguousarray(np.asarray(inputs["L1_ffn_w_up"]).T, np.float32)
    w["l1_wd_t"] = np.ascontiguousarray(np.asarray(inputs["L1_ffn_w_down"]).T, np.float32)
    return w


def host_weights_l2(inputs):
    w = {}
    for k, nm in (("L2_attn_Wq", "l2_wq_t"), ("L2_attn_Wk", "l2_wk_t"),
                  ("L2_attn_Wv", "l2_wv_t"), ("L2_attn_Wo", "l2_wo_t")):
        w[nm] = np.ascontiguousarray(np.asarray(inputs[k]).T, np.float32)
    w["l2_fm_t"] = np.ascontiguousarray(np.concatenate(
        [np.asarray(inputs[k]).T for k in ("L2_attn_fmq_w1", "L2_attn_fmq_w2",
                                           "L2_attn_fmk_w1", "L2_attn_fmk_w2")],
        axis=1), np.float32)
    fmb = np.stack([np.asarray(inputs[k]) for k in
                    ("L2_attn_fmq_b1", "L2_attn_fmq_b2",
                     "L2_attn_fmk_b1", "L2_attn_fmk_b2")], axis=1)
    w["l2_fmb"] = np.ascontiguousarray(fmb, np.float32)
    w["l2_rms"] = np.ascontiguousarray(
        np.tile(np.asarray(inputs["L2_attn_rms_w"], np.float32)[None, :], (128, 1)))
    wqkv = np.asarray(inputs["L2_ffn_wqkv"], np.float32)
    w["l2f_wqkv"] = np.ascontiguousarray(wqkv.reshape(2 * 1024, 1536))
    wo = np.asarray(inputs["L2_ffn_wo"], np.float32)
    w["l2f_wo"] = np.ascontiguousarray(wo.reshape(2 * 512, 1024))
    w1 = np.asarray(inputs["L2_ffn_w1"], np.float32)
    HID = 2730
    w1a = np.zeros((2, 1024, HID_PAD), np.float32)
    w1g = np.zeros((2, 1024, HID_PAD), np.float32)
    w1a[:, :, :HID] = w1[:, :, :HID]
    w1g[:, :, :HID] = w1[:, :, HID:]
    w["l2f_w1a"] = np.ascontiguousarray(w1a.reshape(2 * 1024, HID_PAD))
    w["l2f_w1g"] = np.ascontiguousarray(w1g.reshape(2 * 1024, HID_PAD))
    w2 = np.asarray(inputs["L2_ffn_w2"], np.float32)
    w2p = np.zeros((2, HID_PAD, 1024), np.float32)
    w2p[:, :HID] = w2
    w["l2f_w2"] = np.ascontiguousarray(w2p.reshape(2 * HID_PAD, 1024))
    qs = np.asarray(inputs["L2_ffn_q_scale"], np.float32)
    ks = np.asarray(inputs["L2_ffn_k_scale"], np.float32)
    qsks = np.zeros((128, 4), np.float32)
    for l in range(2):
        qsks[:, 2 * l] = np.tile(qs[l], 2)
        qsks[:, 2 * l + 1] = np.tile(ks[l], 2)
    w["l2f_qsks"] = qsks
    lng = np.asarray(inputs["L2_ffn_ln_g"], np.float32).reshape(2, 8, 128)
    lnb = np.asarray(inputs["L2_ffn_ln_b"], np.float32).reshape(2, 8, 128)
    w["l2f_lng"] = np.ascontiguousarray(
        lng.transpose(2, 0, 1).reshape(128, 16)[:, [l * 8 + d for l in range(2) for d in range(8)]])
    w["l2f_lnb"] = np.ascontiguousarray(
        lnb.transpose(2, 0, 1).reshape(128, 16)[:, [l * 8 + d for l in range(2) for d in range(8)]])
    b1 = np.asarray(inputs["L2_ffn_b1"], np.float32)
    b1a = np.zeros((2, HID_PAD), np.float32)
    b1g = np.zeros((2, HID_PAD), np.float32)
    b1a[:, :HID] = b1[:, :HID]
    b1g[:, :HID] = b1[:, HID:]
    w["l2f_b1a"] = np.ascontiguousarray(
        b1a.reshape(2, 22, 128).transpose(2, 0, 1).reshape(128, 44))
    w["l2f_b1g"] = np.ascontiguousarray(
        b1g.reshape(2, 22, 128).transpose(2, 0, 1).reshape(128, 44))
    b2 = np.asarray(inputs["L2_ffn_b2"], np.float32)
    w["l2f_b2"] = np.ascontiguousarray(
        b2.reshape(2, 8, 128).transpose(2, 0, 1).reshape(128, 16))
    return w


def host_weights_l3(inputs):
    w = {}
    w["l3_win_t"] = np.ascontiguousarray(np.asarray(inputs["L3_attn_W_in"]).T, np.float32)
    w["l3_wxp_t"] = np.ascontiguousarray(np.asarray(inputs["L3_attn_W_xproj"]).T, np.float32)
    w["l3_wdt_t"] = np.ascontiguousarray(np.asarray(inputs["L3_attn_W_dt"]).T, np.float32)
    w["l3_wout_t"] = np.ascontiguousarray(np.asarray(inputs["L3_attn_W_out"]).T, np.float32)
    w["l3_alog"] = np.ascontiguousarray(np.asarray(inputs["L3_attn_A_log"], np.float32))
    def colsh(v, nt):
        return np.ascontiguousarray(np.asarray(v, np.float32).reshape(nt, 128).T)
    w["l3_lnw"] = colsh(inputs["L3_attn_ln_w"], 8)
    w["l3_lnb"] = colsh(inputs["L3_attn_ln_b"], 8)
    cw = np.asarray(inputs["L3_attn_conv_w"], np.float32).reshape(16, 128, 4)
    w["l3_convw"] = np.ascontiguousarray(cw.transpose(1, 0, 2).reshape(128, 64))
    w["l3_convb"] = colsh(inputs["L3_attn_conv_b"], 16)
    w["l3_bdt"] = colsh(inputs["L3_attn_b_dt"], 16)
    w["l3_dsk"] = colsh(inputs["L3_attn_Dskip"], 16)
    w["l3f_u1"] = np.ascontiguousarray(np.asarray(inputs["L3_ffn_U1"], np.float32))
    w["l3f_v1"] = np.ascontiguousarray(np.asarray(inputs["L3_ffn_V1"], np.float32))
    w["l3f_u2"] = np.ascontiguousarray(np.asarray(inputs["L3_ffn_U2"], np.float32))
    w["l3f_v2"] = np.ascontiguousarray(np.asarray(inputs["L3_ffn_V2"], np.float32))
    w["l3f_b1"] = colsh(inputs["L3_ffn_b1"], 32)
    w["l3f_b2"] = colsh(inputs["L3_ffn_b2"], 8)
    return w


def host_weights_l0(inputs):
    w = {}
    w["l0_wq"] = np.ascontiguousarray(inputs["L0_attn_w_q"], np.float32)
    w["l0_wkv"] = np.ascontiguousarray(inputs["L0_attn_w_kv"], np.float32)
    w["l0_wout"] = np.ascontiguousarray(inputs["L0_attn_w_out"], np.float32)
    w["l0_wg_t"] = np.ascontiguousarray(np.asarray(inputs["L0_ffn_Wg"]).T, np.float32)
    w["l0_w1"] = np.ascontiguousarray(np.asarray(inputs["L0_ffn_W1"]).reshape(16 * 1024, 1024), np.float32)
    w["l0_w2"] = np.ascontiguousarray(np.asarray(inputs["L0_ffn_W2"]).reshape(16 * 512, 1024), np.float32)
    w["l0_ws1"] = np.ascontiguousarray(inputs["L0_ffn_Ws1"], np.float32)
    w["l0_ws2"] = np.ascontiguousarray(inputs["L0_ffn_Ws2"], np.float32)
    return w



_CACHE = {}


def kernel(**inputs):
    import numpy as _np
    layers = tuple(LAYERS)
    if layers not in _CACHE:
        _CACHE[layers] = build_program(layers=layers)
    nc, consts = _CACHE[layers]
    w = {}
    if 0 in layers:
        w.update(host_weights_l0(inputs))
    if 1 in layers:
        w.update(host_weights_l1(inputs))
    if 2 in layers:
        w.update(host_weights_l2(inputs))
    if 3 in layers:
        w.update(host_weights_l3(inputs))
    x = _np.asarray(inputs["L0_attn_x"], _np.float32)
    in_maps = []
    for c in range(8):
        m = dict(consts)
        m.update(w)
        m["x_in"] = _np.ascontiguousarray(x[c].T)
        in_maps.append(m)
    res = run_bass_kernel_spmd(nc, in_maps, core_ids=list(range(8)))
    out = _np.stack([_np.asarray(r["x_out"]).T for r in res.results], axis=0)
    return out



SWAPPAIR = [j ^ 1 for j in range(32)]


def emit_l1_attn(b, g, W, xn, xT, sg, pl, in_edges):
    o_all = pl.bf(8 * T)
    mark = pl.p
    qk = pl.bf(2 * T)
    v_tok = pl.bf(16 * 128)
    P = pl.bf(T)
    PT = pl.bf(16 * 128)
    qoff = (pl.p + 15) & ~15
    qraw = pl.bf(512)
    shf = pl.bf(512)
    t1 = pl.bf(512)
    t2 = qraw
    sq = pl.f32(512)
    P2 = pl.ar[:, qoff:qoff + 2048]
    n2 = P
    rn_rep = pl.bf(512)
    sums = pl.f32(16)
    diag = pl.bf(128)
    diag2 = pl.bf(128)
    wqb = pl.bf(8 * 128)
    wkb = pl.bf(8 * 128)
    wvb = pl.bf(8 * 128)
    cosT = pl.bf(T)
    sinT = pl.bf(T)
    e_ct = [b.dma(cosT, W["ct_l1_cos"], queue="sync"),
            b.dma(sinT, W["ct_l1_sin"], queue="sync")]
    idb = g["ident_bf"]
    e64 = g["e64"]
    e64t = g["e64_t"]
    e_q_mm = None
    e_pv_last = None
    e_tr_last = None
    e_sc_last = None
    e_sum = None
    e_qread = None
    sqk = W["l1_sqk_sb"]
    for pair in range(8):
        wed = [e for e in (e_q_mm,) if e]
        e_wq = wload(b, wqb, W["l1_wq_t"][:, pair * 128:(pair + 1) * 128], wait_edges=wed)
        e_wk = wload(b, wkb, W["l1_wk_t"][:, pair * 128:(pair + 1) * 128], wait_edges=wed)
        e_wv = wload(b, wvb, W["l1_wv_t"][:, pair * 128:(pair + 1) * 128], wait_edges=wed)
        for ti in range(16):
            bk = b.rr(7)
            b.mm_group(bk, [(b.pbank[bk][:, 0:128], xn[k][:, ti * 128:(ti + 1) * 128],
                             wvb[:, k * 128:(k + 1) * 128]) for k in range(DT)],
                       wait_edges=[e_wv] + list(in_edges))
            b.bank_read(bk, "vector", "tensor_copy",
                        v_tok[:, ti * 128:(ti + 1) * 128], b.pbank[bk][:, 0:128])
        for wbuf, e_w, dst_off in ((wqb, e_wq, 0), (wkb, e_wk, T)):
            for cch, sl in enumerate(CH512):
                bk = b.rr(7)
                e_q_mm = b.mm_group(bk, [(b.pbank[bk][:, :], wbuf[:, k * 128:(k + 1) * 128],
                                          xn[k][:, sl]) for k in range(DT)],
                                    wait_edges=[e_w])
                e_r = b.bank_read(bk, "scalar", "copy", qraw, b.pbank[bk][:, :],
                                  wait_edges=[e_qread] if e_qread else [])
                b.op("vector", "stream_shuffle", shf, qraw, SWAPPAIR, wait_edges=[e_r])
                e_qread = b.op("vector", "tensor_mul", t1, qraw, cosT[:, sl], inc=True)
                b.op("vector", "tensor_mul", t2, shf, sinT[:, sl])
                dsl = slice(dst_off + sl.start, dst_off + sl.stop)
                b.op("vector", "tensor_add", qk[:, dsl], t1, t2, inc=True,
                     wait_edges=[e_sc_last] if e_sc_last else [])
                e_sq = b.op("scalar", "activation", sq, qk[:, dsl], AF.Square, inc=True)
                bk2 = b.rr(7)
                b.mm_group(bk2, [(b.pbank[bk2][0:2, :], e64t, sq)], wait_edges=[e_sq])
                b.bank_read(bk2, "vector", "tensor_copy", n2[0:2, sl],
                            b.pbank[bk2][0:2, :])
            e_n2 = b.op("scalar", "activation", n2[0:2, 0:T], n2[0:2, 0:T], AF.Sqrt,
                        inc=True)
            e_rn = b.op("vector", "tensor_scalar", n2[0:2, 0:T], n2[0:2, 0:T], 1e-12,
                        None, ALU.max, wait_edges=[e_n2], inc=True)
            for cch, sl in enumerate(CH512):
                e_rc = b.op("vector", "reciprocal", sq[0:2, :], n2[0:2, sl],
                            wait_edges=[e_rn], inc=True)
                bk2 = b.rr(7)
                b.mm_group(bk2, [(b.pbank[bk2][:, :], e64, sq[0:2, :])],
                           wait_edges=[e_rc])
                e_rep = b.bank_read(bk2, "vector", "tensor_copy", rn_rep,
                                    b.pbank[bk2][:, :])
                dsl = slice(dst_off + sl.start, dst_off + sl.stop)
                e_sc_last = b.op("vector", "scalar_tensor_tensor", qk[:, dsl],
                                 qk[:, dsl], sqk[:, pair:pair + 1], rn_rep,
                                 ALU.mult, ALU.mult, wait_edges=[e_rep], inc=True)
        e_rope = b.fence(engines=("vector",))
        if pair == 0 and globals().get("DBG_L1"):
            QKD = b.nc.dram_tensor("qk_dbg", [128, 2 * T], BF16,
                                   kind="ExternalOutput").ap()
            b.dma(QKD, qk, queue="sync", wait_edges=e_rope)
        for hh in range(2):
            h = 2 * pair + hh
            r0 = hh * 64
            qrows = qk[r0:r0 + 64, 0:T]
            krows = qk[r0:r0 + 64, T:2 * T]
            pvbk = 7
            prev = None
            for ti in range(17):
                if ti < 16:
                    par = ti % 2
                    Pb = P if par == 0 else P2
                    dg = diag if par == 0 else diag2
                    sac, ssm, src = (0, 8, 9) if par == 0 else (4, 14, 15)
                    smx, sbi = (10, 11) if par == 0 else (12, 13)
                    for c in range(4):
                        b.mm_group(c, [(b.pbank[c][:, :],
                                        qrows[:, ti * 128:(ti + 1) * 128],
                                        krows[:, c * 512:(c + 1) * 512])],
                                   wait_edges=(list(e_rope) if (ti == 0 and c == 0) else ()))
                    b.op("vector", "reduce_max", sums[:, smx:smx + 1],
                         b.pbig[:, 0:2048], AX.X,
                         wait_edges=[b.last_op["tensor"]])
                    e_nm = b.op("vector", "tensor_scalar", sums[:, sbi:sbi + 1],
                                sums[:, smx:smx + 1], -8.0, None, ALU.mult,
                                inc=True)
                    e_exp = None
                    for c in range(4):
                        e_exp = b.bank_read(c, "scalar", "activation",
                                            Pb[:, c * 512:(c + 1) * 512],
                                            b.pbank[c][:, :],
                                            AF.Exp, bias=sums[:, sbi:sbi + 1],
                                            scale=8.0,
                                            accum_out=sums[:, sac + c:sac + c + 1],
                                            wait_edges=[e_nm])
                    b.op("vector", "reduce_sum", sums[:, ssm:ssm + 1],
                         sums[:, sac:sac + 4], AX.X, wait_edges=[e_exp])
                    b.op("vector", "reciprocal", sums[:, src:src + 1],
                         sums[:, ssm:ssm + 1])
                    e_diag = b.op("vector", "tensor_scalar", dg, idb,
                                  sums[:, src:src + 1], None, ALU.mult, inc=True)
                    cur = (Pb, dg, e_diag, ti)
                else:
                    cur = None
                if prev is not None:
                    Pp, dgp, e_dgp, tj = prev
                    for grp in range(0, 16, 4):
                        bk = 4 + (grp // 4) % 2
                        for j in range(4):
                            b.mm_group(bk, [(b.pbank[bk][:, j * 128:(j + 1) * 128],
                                             Pp[:, (grp + j) * 128:(grp + j + 1) * 128],
                                             dgp)],
                                       wait_edges=[e_dgp] if j == 0 else ())
                        e_tr_last = b.bank_read(bk, "vector", "tensor_copy",
                                                PT[:, grp * 128:(grp + 4) * 128],
                                                b.pbank[bk][:, :])
                    b.mm_group(pvbk, [(b.pbank[pvbk][0:64, (tj % 4) * 128:(tj % 4) * 128 + 128],
                                       v_tok[:, tjt * 128 + r0:tjt * 128 + r0 + 64],
                                       PT[:, tjt * 128:(tjt + 1) * 128])
                                      for tjt in range(16)],
                               wait_edges=[e_tr_last])
                    if tj % 4 == 3:
                        blk = tj // 4
                        e_pv_last = b.bank_read(
                            pvbk, "vector", "tensor_copy",
                            o_all[r0:r0 + 64, pair * T + blk * 512: pair * T + (blk + 1) * 512],
                            b.pbank[pvbk][0:64, :])
                prev = cur
            if pair == 0 and hh == 0 and globals().get("DBG_L1"):
                PD = b.nc.dram_tensor("p_dbg", [128, T], BF16,
                                      kind="ExternalOutput").ap()
                b.dma(PD, P, queue="sync", wait_edges=[e_pv_last])
    if globals().get("DBG_L1"):
        OD = b.nc.dram_tensor("o_dbg", [128, 8 * T], BF16,
                              kind="ExternalOutput").ap()
        b.dma(OD, o_all, queue="sync", wait_edges=b.fence())
    pl.p = mark
    e_all = b.fence()
    wob = pl.bf(8 * 1024)
    upd = pl.f32(512)
    e_wo = wload(b, wob, W["l1_wo_t"], wait_edges=e_all)
    e_upd = None
    for do in range(DT):
        for cch, sl in enumerate(CH512):
            bk = b.rr(7)
            b.mm_group(bk, [(b.pbank[bk][:, :], wob[:, kp * 1024 + do * 128:kp * 1024 + do * 128 + 128],
                             o_all[:, kp * T + sl.start: kp * T + sl.stop])
                            for kp in range(8)], wait_edges=[e_wo])
            b.bank_read(bk, "vector", "tensor_mul", upd, b.pbank[bk][:, :],
                        sg[do][:, sl], wait_edges=list(in_edges))
            e_upd = b.op("vector", "tensor_add", xT[do][:, sl], xT[do][:, sl], upd,
                         inc=True)
    return [e_upd]



def emit_token_absmax(b, g, tiles_slices, pl_bank, out_col, tmp_cols, idf):
    ncomb = 0
    e_red = None
    for i4 in range(0, len(tiles_slices), 4):
        grp = tiles_slices[i4:i4 + 4]
        bk = b.rr(6)
        for j, ap in enumerate(grp):
            b.op("tensor", "transpose", b.pbank_bf[bk][:, j * 128:(j + 1) * 128],
                 ap, idf)
        e_red = b.bank_read(bk, "vector", "tensor_reduce",
                            tmp_cols[:, ncomb:ncomb + 1],
                            b.pbank_bf[bk][:, 0:len(grp) * 128], AX.X, ALU.max,
                            apply_absolute_value=True)
        ncomb += 1
    b.op("vector", "tensor_copy", out_col, tmp_cols[:, 0:1], wait_edges=[e_red])
    for i in range(1, ncomb):
        b.op("vector", "tensor_max", out_col, out_col, tmp_cols[:, i:i + 1])
    return b.op("vector", "nop", inc=True)


def emit_l1_quant_weights(b, g, W, pl, swrec):
    wf = pl.f32(1024)
    qv = pl.f32(1024)
    qb = pl.bf(1024)
    acc = pl.f32(40)
    ones_f = g["ones_f32_128x1"]
    onesr_f = g["ones_f32_1x128"]
    edges = []
    for mi, wname, qname in ((0, "l1_wg_t", "scr_l1_qg"), (1, "l1_wu_t", "scr_l1_qu"),
                             (2, "l1_wd_t", "scr_l1_qd")):
        src = W[wname]
        dst = W[qname]
        R, C = src.shape
        tiles = [(ri, cj) for ri in range(R // 128) for cj in range(C // 1024)]
        e_red = None
        for i, (ri, cj) in enumerate(tiles):
            e_l = b.dma(wf, src[ri * 128:(ri + 1) * 128, cj * 1024:(cj + 1) * 1024],
                        queue="sync")
            e_red = b.op("vector", "tensor_reduce", acc[:, i:i + 1], wf, AX.X,
                         ALU.add, apply_absolute_value=True, wait_edges=[e_l],
                         inc=True)
        e_s = b.op("vector", "reduce_sum", acc[:, 36:37], acc[:, 0:len(tiles)], AX.X,
                   wait_edges=[e_red], inc=True)
        bk = b.rr()
        b.mm_group(bk, [(b.pbank[bk][0:1, 0:1], ones_f, acc[:, 36:37])],
                   wait_edges=[e_s])
        e_t = b.bank_read(bk, "vector", "tensor_copy", acc[0:1, 38:39],
                          b.pbank[bk][0:1, 0:1])
        b.op("vector", "tensor_scalar", acc[0:1, 38:39], acc[0:1, 38:39],
             1.0 / (R * C), None, ALU.mult, wait_edges=[e_t])
        e_m = b.op("vector", "tensor_scalar", acc[0:1, 38:39], acc[0:1, 38:39],
                   1e-5, None, ALU.max, inc=True)
        bk = b.rr()
        b.mm_group(bk, [(b.pbank[bk][:, 0:1], onesr_f, acc[0:1, 38:39])],
                   wait_edges=[e_m])
        e_sw = b.bank_read(bk, "vector", "tensor_copy", swrec[:, mi:mi + 1],
                           b.pbank[bk][:, 0:1])
        edges.append(e_sw)
        b.op("vector", "reciprocal", acc[:, 39:40], swrec[:, mi:mi + 1],
             wait_edges=[e_sw])
        e_rq = b.op("vector", "nop", inc=True)
        for ri, cj in tiles:
            dsl = (slice(ri * 128, (ri + 1) * 128), slice(cj * 1024, (cj + 1) * 1024))
            e_l = b.dma(wf, src[dsl[0], dsl[1]], queue="sync")
            e_c = b.op("scalar", "activation", qv, wf, AF.Copy, scale=acc[:, 39:40],
                       wait_edges=[e_l, e_rq], inc=True)
            b.op("vector", "tensor_scalar", qv, qv, MAGIC, -MAGIC, ALU.add, ALU.add,
                 wait_edges=[e_c])
            e_q = b.op("vector", "tensor_scalar", qb, qv, 1.0, -1.0, ALU.min,
                       ALU.max, inc=True)
            e_st = b.dma(dst[dsl[0], dsl[1]], qb, queue="sync", wait_edges=[e_q])
            edges.append(e_st)
    return swrec, edges


def emit_l1_ffn(b, g, W, xn, sg, pl, xn_edges):
    swrec_region = pl.f32(4)
    plq = Plan(pl.ar, pl.p)
    swrec, e_qw = emit_l1_quant_weights(b, g, W, plq, swrec_region)
    b.mark("l1.ffn.mm")
    idf = g["ident_f32"]
    idb = g["ident_bf"]
    gu = pl.bf(32 * 512)
    xq = pl.bf(8 * 512)
    qwb = pl.bf(8 * 512)
    qdb = pl.bf(32 * 128)
    strep = pl.bf(512)
    rstrep = pl.bf(4 * 512)
    cols = pl.f32(32)
    tsc = pl.f32(512)
    tsc2 = pl.f32(512)
    e_dn_mm = None
    e_g_mm = None
    e_xq_mm = None
    for cch, sl in enumerate(CH512):
        rst_c = rstrep[:, cch * 512:(cch + 1) * 512]
        for tsub in range(4):
            t0 = sl.start + tsub * 128
            slices = [xn[k][:, t0:t0 + 128] for k in range(DT)]
            e_amax = emit_token_absmax(b, g, slices, None, cols[:, tsub:tsub + 1],
                                       cols[:, 8:16], idb)
        b.op("vector", "nop", wait_edges=[e_amax])
        for tsub in range(4):
            c = cols[:, tsub:tsub + 1]
            b.op("vector", "tensor_scalar", c, c, 1e-5, None, ALU.max)
            b.op("vector", "reciprocal", cols[:, 4 + tsub:5 + tsub], c)
            b.op("vector", "tensor_scalar", cols[:, 4 + tsub:5 + tsub],
                 cols[:, 4 + tsub:5 + tsub], 127.0, None, ALU.mult)
            b.op("vector", "tensor_scalar", c, c, 1.0 / 127.0, None, ALU.mult)
        e_cols = b.op("vector", "nop", inc=True)
        bk = b.rr(6)
        for tsub in range(4):
            b.mm_group(bk, [(b.pbank[bk][:, tsub * 128:(tsub + 1) * 128],
                             cols[:, 4 + tsub:5 + tsub].broadcast_to((128, 128)),
                             idf)], wait_edges=[e_cols] if tsub == 0 else ())
        e_strep = b.bank_read(bk, "vector", "tensor_copy", strep, b.pbank[bk][:, :])
        bk = b.rr(6)
        for tsub in range(4):
            b.mm_group(bk, [(b.pbank[bk][:, tsub * 128:(tsub + 1) * 128],
                             cols[:, tsub:tsub + 1].broadcast_to((128, 128)),
                             idf)], wait_edges=[e_cols] if tsub == 0 else ())
        e_rstrep = b.bank_read(bk, "vector", "tensor_copy", rst_c, b.pbank[bk][:, :])
        e_xq = None
        for k in range(DT):
            b.op("vector", "tensor_mul", tsc, xn[k][:, sl], strep,
                 wait_edges=[e_strep])
            b.op("vector", "tensor_scalar", tsc, tsc, MAGIC, -MAGIC, ALU.add, ALU.add)
            e_xq = b.op("vector", "tensor_scalar", xq[:, k * 512:(k + 1) * 512], tsc,
                        127.0, -128.0, ALU.min, ALU.max, inc=True)
        for mb in range(8):
            for which, qsrc in enumerate(("scr_l1_qg", "scr_l1_qu")):
                e_w = wload(b, qwb, W[qsrc][:, mb * 512:(mb + 1) * 512],
                            wait_edges=[e for e in (e_g_mm,) if e] + list(e_qw))
                for mt in range(4):
                    bk = b.rr(6)
                    e_g_mm = b.mm_group(bk, [
                        (b.pbank[bk][:, :],
                         qwb[:, k * 512 + mt * 128: k * 512 + mt * 128 + 128],
                         xq[:, k * 512:(k + 1) * 512]) for k in range(DT)],
                        wait_edges=[e_w, e_xq])
                    m_i = mb * 4 + mt
                    msl = slice(m_i * 512, (m_i + 1) * 512)
                    if which == 0:
                        b.bank_read(bk, "vector", "tensor_scalar", tsc,
                                    b.pbank[bk][:, :], swrec[:, 0:1], None, ALU.mult)
                        e_t = b.op("vector", "tensor_mul", tsc, tsc, rst_c,
                                   wait_edges=[e_rstrep], inc=True)
                        b.op("scalar", "activation", gu[:, msl], tsc, AF.Silu,
                             wait_edges=[e_t], inc=True)
                    else:
                        b.bank_read(bk, "vector", "tensor_scalar", tsc2,
                                    b.pbank[bk][:, :], swrec[:, 1:2], None, ALU.mult)
                        b.op("vector", "tensor_mul", tsc2, tsc2, rst_c)
                        b.op("vector", "tensor_mul", gu[:, msl], gu[:, msl], tsc2,
                             wait_edges=[b.last_op["scalar"]], inc=True)
        for tsub in range(4):
            t0 = tsub * 128
            slices = [gu[:, m_i * 512 + t0: m_i * 512 + t0 + 128]
                      for m_i in range(32)]
            e_amax = emit_token_absmax(b, g, slices, None, cols[:, tsub:tsub + 1],
                                       cols[:, 8:16], idb)
        for tsub in range(4):
            c = cols[:, tsub:tsub + 1]
            b.op("vector", "tensor_scalar", c, c, 1e-5, None, ALU.max,
                 wait_edges=[e_amax] if tsub == 0 else ())
            b.op("vector", "reciprocal", cols[:, 4 + tsub:5 + tsub], c)
            b.op("vector", "tensor_scalar", cols[:, 4 + tsub:5 + tsub],
                 cols[:, 4 + tsub:5 + tsub], 127.0, None, ALU.mult)
            b.op("vector", "tensor_scalar", c, c, 1.0 / 127.0, None, ALU.mult)
        e_cols = b.op("vector", "nop", inc=True)
        bk = b.rr(6)
        for tsub in range(4):
            b.mm_group(bk, [(b.pbank[bk][:, tsub * 128:(tsub + 1) * 128],
                             cols[:, 4 + tsub:5 + tsub].broadcast_to((128, 128)),
                             idf)], wait_edges=[e_cols] if tsub == 0 else ())
        e_strep = b.bank_read(bk, "vector", "tensor_copy", strep, b.pbank[bk][:, :])
        bk = b.rr(6)
        for tsub in range(4):
            b.mm_group(bk, [(b.pbank[bk][:, tsub * 128:(tsub + 1) * 128],
                             cols[:, tsub:tsub + 1].broadcast_to((128, 128)),
                             idf)], wait_edges=[e_cols] if tsub == 0 else ())
        e_rstrep = b.bank_read(bk, "vector", "tensor_copy", rst_c, b.pbank[bk][:, :])
        e_guq = None
        for m_i in range(32):
            msl = slice(m_i * 512, (m_i + 1) * 512)
            b.op("vector", "tensor_mul", tsc, gu[:, msl], strep,
                 wait_edges=[e_strep])
            b.op("vector", "tensor_scalar", tsc, tsc, MAGIC, -MAGIC, ALU.add, ALU.add)
            e_guq = b.op("vector", "tensor_scalar", gu[:, msl], tsc, 127.0, -128.0,
                         ALU.min, ALU.max, inc=True)
        for do in range(DT):
            e_wd = wload(b, qdb, W["scr_l1_qd"][:, do * 128:(do + 1) * 128],
                         wait_edges=[e for e in (e_dn_mm,) if e] + list(e_qw))
            bk = b.rr(6)
            e_dn_mm = b.mm_group(bk, [
                (b.pbank[bk][:, :], qdb[:, k * 128:(k + 1) * 128],
                 gu[:, k * 512:(k + 1) * 512]) for k in range(32)],
                wait_edges=[e_wd, e_guq])
            b.bank_read(bk, "vector", "tensor_scalar", tsc, b.pbank[bk][:, :],
                        swrec[:, 2:3], None, ALU.mult)
            e_t = b.op("vector", "tensor_mul", tsc, tsc, rst_c,
                       wait_edges=[e_rstrep], inc=True)
            b.op("scalar", "activation", sg[do][:, sl], tsc, AF.Silu,
                 wait_edges=[e_t], inc=True)
    return b.fence(engines=("scalar",))



def emit_l2_attn(b, g, W, xn, xT, sg, pl, in_edges):
    o_all = pl.bf(8 * T)
    mark = pl.p
    qf = pl.bf(T)
    kf = pl.bf(T)
    ktok = pl.bf(T)
    vtok = pl.bf(T)
    otok = pl.bf(T)
    Ssb = pl.bf(128)
    qraw = pl.bf(512)
    t1 = pl.f32(512)
    atm = pl.bf(128)
    sqs = pl.f32(128)
    cstat = pl.f32(8)
    wqb = pl.bf(8 * 128)
    wkb = pl.bf(8 * 128)
    wvb = pl.bf(8 * 128)
    fmw = pl.bf(4 * 128)
    fmb = W["l2_fmb_sb"]
    rmsrep = W["l2_rms_sb"]
    idb = g["ident_bf"]
    triu = g["l2_triu"]
    e_fm = wload(b, fmw, W["l2_fm_t"])
    e_q_mm = None
    e_o_tr = None
    SCALE = 128.0 ** -0.5
    for h in range(8):
        wed = [e for e in (e_q_mm,) if e]
        e_wq = wload(b, wqb, W["l2_wq_t"][:, h * 128:(h + 1) * 128], wait_edges=wed)
        e_wk = wload(b, wkb, W["l2_wk_t"][:, h * 128:(h + 1) * 128], wait_edges=wed)
        e_wv = wload(b, wvb, W["l2_wv_t"][:, h * 128:(h + 1) * 128], wait_edges=wed)
        for ti in range(16):
            bk = b.rr(6)
            b.mm_group(bk, [(b.pbank[bk][:, 0:128], xn[k][:, ti * 128:(ti + 1) * 128],
                             wvb[:, k * 128:(k + 1) * 128]) for k in range(DT)],
                       wait_edges=[e_wv] + list(in_edges))
            b.bank_read(bk, "vector", "tensor_copy",
                        vtok[:, ti * 128:(ti + 1) * 128], b.pbank[bk][:, 0:128])
        for which, (wbuf, e_w, dstbuf) in enumerate(
                ((wqb, e_wq, qf), (wkb, e_wk, kf))):
            for cch, sl in enumerate(CH512):
                bk = b.rr(6)
                e_q_mm = b.mm_group(bk, [(b.pbank[bk][:, :], wbuf[:, k * 128:(k + 1) * 128],
                                          xn[k][:, sl]) for k in range(DT)],
                                    wait_edges=[e_w])
                e_r = b.bank_read(bk, "scalar", "copy", qraw, b.pbank[bk][:, :])
                bkA = b.rr(6); bkB = b.rr(6)
                fo = which * 2
                b.mm_group(bkA, [(b.pbank[bkA][:, :], fmw[:, fo * 128:(fo + 1) * 128],
                                  qraw)], wait_edges=[e_fm, e_r])
                b.mm_group(bkB, [(b.pbank[bkB][:, :], fmw[:, (fo + 1) * 128:(fo + 2) * 128],
                                  qraw)], wait_edges=[e_fm, e_r])
                sc2 = SCALE if which == 0 else 1.0
                b.bank_read(bkA, "vector", "tensor_scalar", t1, b.pbank[bkA][:, :],
                            fmb[:, 2 * which:2 * which + 1], sc2, ALU.add, ALU.mult)
                b.bank_read(bkB, "vector", "scalar_tensor_tensor", dstbuf[:, sl],
                            b.pbank[bkB][:, :], fmb[:, 2 * which + 1:2 * which + 2],
                            t1, ALU.add, ALU.mult)
        e_kf = b.fence(engines=("vector",))
        for grp in range(0, 16, 4):
            bk = b.rr(6)
            for j in range(4):
                b.op("tensor", "transpose",
                     b.pbank_bf[bk][:, j * 128:(j + 1) * 128],
                     kf[:, (grp + j) * 128:(grp + j + 1) * 128], idb,
                     wait_edges=(e_kf if (grp == 0 and j == 0) else ()))
            b.bank_read(bk, "vector", "tensor_copy",
                        ktok[:, grp * 128:(grp + 4) * 128],
                        b.pbank_bf[bk][:, 0:512])
        e_kt = b.fence(engines=("vector",))
        b.op("vector", "memset", Ssb, 0.0)
        e_S = b.op("vector", "nop", inc=True)
        prev_rec2 = None
        for n in range(16):
            cs = slice(n * 128, (n + 1) * 128)
            bk = b.rr(5)
            b.mm_group(bk, [(b.pbank[bk][:, 0:128], kf[:, cs], qf[:, cs])],
                       wait_edges=e_kt if n == 0 else ())
            e_at = b.bank_read(bk, "vector", "tensor_mul", atm, b.pbank[bk][:, 0:128],
                               triu)
            mms = []
            if n > 0:
                mms.append((b.pbank[7][:, 0:128], qf[:, cs], Ssb))
            mms.append((b.pbank[7][:, 0:128], atm, vtok[:, cs]))
            b.mm_group(7, mms, wait_edges=[e_at, e_S])
            def bld(e, st=(n == 0), sp=(n == 15), l=ktok[:, cs], r=vtok[:, cs]):
                return e.matmul(b.pbank[6][:, 0:128], l, r, start=st, stop=sp)
            rec = b._mkrec("tensor", bld)
            rec.consumed = True
            b._dep_read(rec, ktok[:, cs])
            b._dep_read(rec, vtok[:, cs])
            if n == 0:
                b._dep_write(rec, b.pbank[6][:, 0:128])
            if prev_rec2 is not None:
                rec.waits.add(prev_rec2)
            rec2 = b.op("scalar", "copy", Ssb, b.pbank[6][:, 0:128], inc=True,
                        wait_edges=[rec])
            prev_rec2 = rec2
            e_S = rec2
            e_sq = b.op("scalar", "activation", sqs, b.pbank[7][:, 0:128], AF.Square,
                        accum_out=cstat[:, 0:1], wait_edges=[b.last_op["tensor"]],
                        inc=True)
            b.op("vector", "tensor_scalar", cstat[:, 1:2], cstat[:, 0:1],
                 1.0 / 128.0, 1e-5, ALU.mult, ALU.add, wait_edges=[e_sq])
            e_sr = b.op("scalar", "activation", cstat[:, 1:2], cstat[:, 1:2],
                        AF.Sqrt, inc=True)
            e_rc = b.op("vector", "reciprocal", cstat[:, 2:3], cstat[:, 1:2],
                        wait_edges=[e_sr], inc=True)
            b.op("vector", "scalar_tensor_tensor", otok[:, cs], b.pbank[7][:, 0:128],
                 cstat[:, 2:3], rmsrep, ALU.mult, ALU.mult, wait_edges=[e_rc],
                 inc=True)
        e_on = b.fence(engines=("vector",))
        for grp in range(0, 16, 4):
            bk = b.rr(5)
            for j in range(4):
                b.op("tensor", "transpose",
                     b.pbank_bf[bk][:, j * 128:(j + 1) * 128],
                     otok[:, (grp + j) * 128:(grp + j + 1) * 128], idb,
                     wait_edges=(e_on if (grp == 0 and j == 0) else ()))
            e_o_tr = b.bank_read(bk, "vector", "tensor_copy",
                                 o_all[:, h * T + grp * 128: h * T + (grp + 4) * 128],
                                 b.pbank_bf[bk][:, 0:512])
    pl.p = mark
    e_all = b.fence()
    wob = pl.bf(8 * 1024)
    upd = pl.f32(512)
    e_wo = wload(b, wob, W["l2_wo_t"], wait_edges=e_all)
    e_upd = None
    for do in range(DT):
        for cch, sl in enumerate(CH512):
            bk = b.rr(7)
            b.mm_group(bk, [(b.pbank[bk][:, :], wob[:, kp * 1024 + do * 128:kp * 1024 + do * 128 + 128],
                             o_all[:, kp * T + sl.start: kp * T + sl.stop])
                            for kp in range(8)], wait_edges=[e_wo])
            b.bank_read(bk, "vector", "tensor_mul", upd, b.pbank[bk][:, :],
                        sg[do][:, sl], wait_edges=list(in_edges))
            e_upd = b.op("vector", "tensor_add", xT[do][:, sl], xT[do][:, sl], upd,
                         inc=True)
    return [e_upd]



HID_PAD = 2816
NH = HID_PAD // 128


def emit_l2_ffn(b, g, W, xn, sgr, sg, pl, xn_edges):
    y = pl.bf(8 * T)
    yt = [y[:, d * T:(d + 1) * T] for d in range(8)]
    shc0 = pl.bf(4 * 512)
    mark0 = pl.p
    idb = g["ident_bf"]
    rot64 = g["l2_rot64"]
    e64 = g["e64"]
    e64t = g["e64_t"]
    onesr_f = g["ones_f32_1x128"]
    ones_f = g["ones_f32_128x1"]
    qsks = W["l2f_qsks_sb"]
    lng = W["l2f_lng_sb"]
    lnb = W["l2f_lnb_sb"]
    b1a = W["l2f_b1a_sb"]
    b1g = W["l2f_b1g_sb"]
    b2c = W["l2f_b2_sb"]
    e_y = []
    for d in range(8):
        e_y.append(b.op("vector", "tensor_copy", yt[d], xn[d][:, :],
                        wait_edges=list(xn_edges) if d == 0 else [], inc=True))
    e_upd = e_y

    def build_shift_c0(waits):
        ee = []
        for j in range(4):
            b.op("vector", "memset", shc0[:, j * 512:j * 512 + 1], 0.0,
                 wait_edges=waits if j == 0 else [])
            ee.append(b.op("vector", "tensor_copy",
                           shc0[:, j * 512 + 1:(j + 1) * 512],
                           yt[4 + j][:, 0:511], inc=True))
        return ee

    def hs(d, lo, hi):
        if d < 4:
            return yt[d][:, lo:hi]
        if lo == 0:
            assert hi <= 512
            return shc0[:, (d - 4) * 512:(d - 4) * 512 + (hi - lo)]
        return yt[d][:, lo - 1:hi - 1]

    for l in range(2):
        b.mark(f"l2f.attn{l}")
        pl.p = mark0
        qk = pl.bf(2 * T)
        v_tok = pl.bf(16 * 128)
        P = pl.bf(2 * 512)
        P2l = pl.bf(2 * 512)
        diag2 = pl.bf(128)
        PT = pl.bf(8 * 128)
        o_all = sgr[:, 0:8192]
        qnb = pl.bf(512)
        qraw = pl.f32(512)
        t1 = pl.f32(512)
        sq = pl.f32(512)
        n2 = pl.bf(T)
        sums = pl.f32(16)
        diag = pl.bf(128)
        wqb = pl.bf(8 * 128)
        wkb = pl.bf(8 * 128)
        wvb = pl.bf(8 * 128)
        trg = sgr[:, 8192:8192 + 4096]
        band = sgr[:, 12288:12288 + 1408]
        e_band = b.dma(band, W["ct_l2_band"], queue="sync")
        e_sh = build_shift_c0(e_upd)
        e_q_mm = None
        e_pv_last = None
        e_tr_last = None
        e_sum = None
        e_sc_last = None
        wsrc = W["l2f_wqkv"]
        for pair in range(4):
            wed = [e for e in (e_q_mm,) if e]
            e_wq = wload(b, wqb, wsrc[l * 1024:(l + 1) * 1024, pair * 128:(pair + 1) * 128],
                         wait_edges=wed)
            e_wk = wload(b, wkb, wsrc[l * 1024:(l + 1) * 1024, 512 + pair * 128:512 + (pair + 1) * 128],
                         wait_edges=wed)
            e_wv = wload(b, wvb, wsrc[l * 1024:(l + 1) * 1024, 1024 + pair * 128:1024 + (pair + 1) * 128],
                         wait_edges=wed)
            for ti in range(16):
                bk = b.rr(7)
                b.mm_group(bk, [(b.pbank[bk][:, 0:128],
                                 hs(k, ti * 128, (ti + 1) * 128),
                                 wvb[:, k * 128:(k + 1) * 128]) for k in range(DT)],
                           wait_edges=[e_wv] + e_sh)
                b.bank_read(bk, "vector", "tensor_copy",
                            v_tok[:, ti * 128:(ti + 1) * 128], b.pbank[bk][:, 0:128])
            for which, (wbuf, e_w, dst_off) in enumerate(
                    ((wqb, e_wq, 0), (wkb, e_wk, T))):
                for cch, sl in enumerate(CH512):
                    bk = b.rr(7)
                    e_q_mm = b.mm_group(bk, [(b.pbank[bk][:, :],
                                              wbuf[:, k * 128:(k + 1) * 128],
                                              hs(k, sl.start, sl.stop))
                                             for k in range(DT)],
                                        wait_edges=[e_w] + e_sh)
                    dsl = slice(dst_off + sl.start, dst_off + sl.stop)
                    e_r = b.bank_read(bk, "vector", "tensor_copy", qk[:, dsl],
                                      b.pbank[bk][:, :],
                                      wait_edges=[e_sc_last] if e_sc_last else [])
                    e_sq = b.op("scalar", "activation", sq, qk[:, dsl], AF.Square,
                                wait_edges=[e_r], inc=True)
                    bk2 = b.rr(7)
                    b.mm_group(bk2, [(b.pbank[bk2][0:2, :], e64t, sq)],
                               wait_edges=[e_sq])
                    b.bank_read(bk2, "vector", "tensor_copy", n2[0:2, sl],
                                b.pbank[bk2][0:2, :])
                e_n2 = b.op("scalar", "activation", n2[0:2, 0:T], n2[0:2, 0:T],
                            AF.Sqrt, inc=True)
                e_rn = b.op("vector", "tensor_scalar", n2[0:2, 0:T], n2[0:2, 0:T],
                            1e-12, None, ALU.max, wait_edges=[e_n2], inc=True)
                e_tq = [b.dma(trg[:, 0:T], W["ct_l2_cos" + ("q" if which == 0 else "k")],
                              queue="sync"),
                        b.dma(trg[:, T:2 * T], W["ct_l2_sin" + ("q" if which == 0 else "k")],
                              queue="sync")]
                for cch, sl in enumerate(CH512):
                    dsl = slice(dst_off + sl.start, dst_off + sl.stop)
                    e_rc = b.op("vector", "reciprocal", sq[0:2, :], n2[0:2, sl],
                                wait_edges=[e_rn], inc=True)
                    bk2 = b.rr(7)
                    b.mm_group(bk2, [(b.pbank[bk2][:, :], e64, sq[0:2, :])],
                               wait_edges=[e_rc])
                    e_rep = b.bank_read(bk2, "vector", "tensor_copy", qraw,
                                        b.pbank[bk2][:, :])
                    e_qn = b.op("vector", "scalar_tensor_tensor", qnb, qk[:, dsl],
                                qsks[:, 2 * l + which:2 * l + which + 1], qraw,
                                ALU.mult, ALU.mult, wait_edges=[e_rep], inc=True)
                    bk3 = b.rr(7)
                    b.mm_group(bk3, [(b.pbank[bk3][:, :], rot64, qnb)],
                               wait_edges=[e_qn])
                    b.op("vector", "tensor_mul", t1, qnb, trg[:, sl.start:sl.stop],
                         wait_edges=e_tq)
                    e_rt = b.bank_read(bk3, "vector", "scalar_tensor_tensor", qraw,
                                       b.pbank[bk3][:, :], 1.0,
                                       trg[:, T + sl.start:T + sl.stop],
                                       ALU.mult, ALU.mult)
                    e_sc_last = b.op("vector", "tensor_add", qk[:, dsl], t1, qraw,
                                     wait_edges=[e_rt], inc=True)
            e_rope = b.fence(engines=("vector",))
            for hh in range(2):
                h = 2 * pair + hh
                r0 = hh * 64
                qrows = qk[r0:r0 + 64, 0:T]
                krows = qk[r0:r0 + 64, T:2 * T]
                pvbk = 7
                prev = None
                for ti in range(17):
                    if ti < 16:
                        w = ti // 4
                        v_ = ti % 4
                        nch = 1 if w == 0 else 2
                        par = ti % 2
                        Pb = P if par == 0 else P2l
                        dg = diag if par == 0 else diag2
                        sac, ssm, src = (0, 8, 9) if par == 0 else (4, 14, 15)
                        smx, sbi = (10, 11) if par == 0 else (12, 13)
                        for ci in range(nch):
                            kc = w - 1 + ci if w > 0 else 0
                            moff = (3 - v_) * 128 + (512 if (w == 0 or ci == 1) else 0)
                            bk = ci
                            b.mm_group(bk, [(b.pbank[bk][:, :],
                                             qrows[:, ti * 128:(ti + 1) * 128],
                                             krows[:, kc * 512:(kc + 1) * 512]),
                                            (b.pbank[bk][:, :], idb,
                                             band[:, moff:moff + 512])],
                                       wait_edges=(list(e_rope) + [e_band])
                                       if (ti == 0 and ci == 0) else ())
                        b.op("vector", "reduce_max", sums[:, smx:smx + 1],
                             b.pbig[:, 0:nch * 512], AX.X,
                             wait_edges=[b.last_op["tensor"]])
                        e_nm = b.op("vector", "tensor_scalar", sums[:, sbi:sbi + 1],
                                    sums[:, smx:smx + 1], -1.0, None, ALU.mult,
                                    inc=True)
                        e_exp = None
                        for ci in range(nch):
                            e_exp = b.bank_read(ci, "scalar", "activation",
                                                Pb[:, ci * 512:(ci + 1) * 512],
                                                b.pbank[ci][:, :], AF.Exp,
                                                bias=sums[:, sbi:sbi + 1], scale=1.0,
                                                accum_out=sums[:, sac + ci:sac + ci + 1],
                                                wait_edges=[e_nm])
                        b.op("vector", "reduce_sum", sums[:, ssm:ssm + 1],
                             sums[:, sac:sac + nch], AX.X, wait_edges=[e_exp])
                        b.op("vector", "reciprocal", sums[:, src:src + 1],
                             sums[:, ssm:ssm + 1])
                        e_diag = b.op("vector", "tensor_scalar", dg, idb,
                                      sums[:, src:src + 1], None, ALU.mult,
                                      inc=True)
                        cur = (Pb, dg, e_diag, ti)
                    else:
                        cur = None
                    if prev is not None:
                        Pp, dgp, e_dgp, tj = prev
                        wj = tj // 4
                        vj = tj % 4
                        ntj = (1 if wj == 0 else 2) * 4
                        for grp in range(0, ntj, 4):
                            bk = 4 + (grp // 4) % 2
                            for j in range(4):
                                b.mm_group(bk, [(b.pbank[bk][:, j * 128:(j + 1) * 128],
                                                 Pp[:, (grp + j) * 128:(grp + j + 1) * 128],
                                                 dgp)],
                                           wait_edges=[e_dgp] if j == 0 else ())
                            e_tr_last = b.bank_read(bk, "vector", "tensor_copy",
                                                    PT[:, grp * 128:(grp + 4) * 128],
                                                    b.pbank[bk][:, :])
                        kbase = 4 * (wj - 1) if wj > 0 else 0
                        b.mm_group(pvbk, [
                            (b.pbank[pvbk][0:64, vj * 128:vj * 128 + 128],
                             v_tok[:, (kbase + tjt) * 128 + r0:(kbase + tjt) * 128 + r0 + 64],
                             PT[:, tjt * 128:(tjt + 1) * 128]) for tjt in range(ntj)],
                            wait_edges=[e_tr_last])
                        if vj == 3:
                            blk = tj // 4
                            e_pv_last = b.bank_read(
                                pvbk, "vector", "tensor_copy",
                                o_all[r0:r0 + 64, pair * T + blk * 512:pair * T + (blk + 1) * 512],
                                b.pbank[pvbk][0:64, :])
                    prev = cur
        e_allo = b.fence()
        wo2 = sgr[:, 8192:8192 + 4096]
        tevu = pl.f32(512)
        e_wo = wload(b, wo2, W["l2f_wo"][l * 512:(l + 1) * 512, :], wait_edges=e_allo)
        e_upd = []
        for do in range(DT):
            for cch, sl in enumerate(CH512):
                bk = b.rr(7)
                b.mm_group(bk, [(b.pbank[bk][:, :],
                                 wo2[:, kp * 1024 + do * 128:kp * 1024 + do * 128 + 128],
                                 o_all[:, kp * T + sl.start:kp * T + sl.stop])
                                for kp in range(4)], wait_edges=[e_wo])
                b.bank_read(bk, "vector", "tensor_copy", tevu, b.pbank[bk][:, :])
                e_u = b.op("vector", "tensor_add", yt[do][:, sl], yt[do][:, sl],
                           tevu, inc=True)
                e_upd.append(e_u)
        b.mark(f"l2f.mlp{l}")
        pl.p = mark0
        xn2 = sgr[:, 0:4096]
        hbuf = sgr[:, 4096:4096 + 5632]
        w1b = sgr[:, 9728:9728 + 1024]
        w2b = sgr[:, 10752:10752 + 1408]
        rows = pl.f32(2 * 512)
        rep = pl.f32(2 * 512)
        sqw = pl.f32(512)
        tev = pl.f32(512)
        tev2 = pl.f32(512)
        e_sh2 = build_shift_c0(e_upd)
        A = rows[0:1, 0:512]
        B = rows[0:1, 512:1024]
        tmp2 = rep[0:1, 0:512]
        e_w1_mm = None
        e_w2_mm = None
        for cch, sl in enumerate(CH512):
            b.mm_group(0, [(b.pbank[0][0:1, :], g["ones_bf_128x1"],
                            hs(d, sl.start, sl.stop)) for d in range(8)],
                       wait_edges=(e_sh2 + e_upd) if cch == 0 else ())
            b.bank_read(0, "vector", "tensor_copy", A, b.pbank[0][0:1, :])
            last_mm = None
            for d in range(8):
                e_s2 = b.op("scalar", "activation", sqw, hs(d, sl.start, sl.stop),
                            AF.Square, inc=True)
                last_mm = b.mm_step(b.pbank[1][0:1, :], ones_f, sqw,
                                    start=(d == 0), stop=(d == 7),
                                    wait_edges=[e_s2])
            e_b = b.op("vector", "tensor_copy", B, b.pbank[1][0:1, :],
                       wait_edges=[last_mm], inc=True)
            b.op("vector", "tensor_scalar", A, A, -1.0 / D, None, ALU.mult)
            b.op("vector", "tensor_scalar", B, B, 1.0 / D, None, ALU.mult)
            b.op("vector", "tensor_mul", tmp2, A, A)
            b.op("vector", "tensor_sub", B, B, tmp2)
            e_var = b.op("vector", "tensor_scalar", B, B, 1e-5, None, ALU.add,
                         inc=True)
            e_sqr = b.op("scalar", "activation", B, B, AF.Sqrt, wait_edges=[e_var],
                         inc=True)
            b.op("vector", "reciprocal", B, B, wait_edges=[e_sqr])
            e_stats = b.op("vector", "tensor_mul", A, A, B, inc=True)
            bk = b.rr(6)
            b.mm_group(bk, [(b.pbank[bk][:, :], onesr_f, B)], wait_edges=[e_stats])
            e_r1 = b.bank_read(bk, "vector", "tensor_copy", rep[:, 0:512],
                               b.pbank[bk][:, :])
            bk = b.rr(6)
            b.mm_group(bk, [(b.pbank[bk][:, :], onesr_f, A)], wait_edges=[e_stats])
            e_r2 = b.bank_read(bk, "vector", "tensor_copy", rep[:, 512:1024],
                               b.pbank[bk][:, :])
            e_xn2 = None
            for d in range(8):
                b.op("vector", "tensor_mul", tev, hs(d, sl.start, sl.stop),
                     rep[:, 0:512], wait_edges=[e_r1, e_r2])
                b.op("vector", "tensor_add", tev, tev, rep[:, 512:1024])
                e_xn2 = b.op("vector", "tensor_scalar", xn2[:, d * 512:(d + 1) * 512],
                             tev, lng[:, l * 8 + d:l * 8 + d + 1],
                             lnb[:, l * 8 + d:l * 8 + d + 1], ALU.mult, ALU.add,
                             inc=True)
            for half in range(2):
                hk0 = half * 11
                for i in range(11):
                    hi = hk0 + i
                    e_wa = wload(b, w1b[:, 0:8 * 128],
                                 W["l2f_w1a"][l * 1024:(l + 1) * 1024,
                                              hi * 128:(hi + 1) * 128],
                                 wait_edges=[e for e in (e_w1_mm,) if e])
                    bkA = b.rr(6)
                    b.mm_group(bkA, [(b.pbank[bkA][:, :],
                                      w1b[:, k * 128:(k + 1) * 128],
                                      xn2[:, k * 512:(k + 1) * 512])
                                     for k in range(8)],
                               wait_edges=[e_wa, e_xn2])
                    e_ta = b.bank_read(bkA, "vector", "tensor_scalar", tev,
                                       b.pbank[bkA][:, :],
                                       b1a[:, l * 22 + hi:l * 22 + hi + 1], None,
                                       ALU.add)
                    e_wg2 = wload(b, w2b[:, 0:8 * 128] if False else w1b[:, 0:8 * 128],
                                  W["l2f_w1g"][l * 1024:(l + 1) * 1024,
                                               hi * 128:(hi + 1) * 128],
                                  wait_edges=[e_ta])
                    bkG = b.rr(6)
                    e_w1_mm = b.mm_group(bkG, [(b.pbank[bkG][:, :],
                                                w1b[:, k * 128:(k + 1) * 128],
                                                xn2[:, k * 512:(k + 1) * 512])
                                               for k in range(8)],
                                         wait_edges=[e_wg2, e_xn2])
                    e_gg = b.bank_read(bkG, "scalar", "activation", tev2,
                                       b.pbank[bkG][:, :], AF.Gelu,
                                       bias=b1g[:, l * 22 + hi:l * 22 + hi + 1])
                    b.op("vector", "tensor_mul", hbuf[:, i * 512:(i + 1) * 512],
                         tev, tev2, wait_edges=[e_ta, e_gg], inc=True)
                e_h = b.fence(engines=("vector",))
                for do in range(DT):
                    e_wd = wload(b, w2b,
                                 W["l2f_w2"][l * HID_PAD + hk0 * 128:
                                             l * HID_PAD + (hk0 + 11) * 128,
                                             do * 128:(do + 1) * 128],
                                 wait_edges=[e for e in (e_w2_mm,) if e])
                    bk = b.rr(6)
                    e_w2_mm = b.mm_group(bk, [
                        (b.pbank[bk][:, :], w2b[:, k * 128:(k + 1) * 128],
                         hbuf[:, k * 512:(k + 1) * 512]) for k in range(11)],
                        wait_edges=[e_wd] + e_h)
                    if half == 0:
                        b.bank_read(bk, "vector", "tensor_copy", tev,
                                    b.pbank[bk][:, :])
                    else:
                        b.bank_read(bk, "vector", "tensor_scalar", tev,
                                    b.pbank[bk][:, :],
                                    b2c[:, l * 8 + do:l * 8 + do + 1], None,
                                    ALU.add)
                    e_u = b.op("vector", "tensor_add", yt[do][:, sl], yt[do][:, sl],
                               tev, inc=True)
            e_upd = [e_u]
    e_fin = b.fence(engines=("vector",))
    edges = [b.op("scalar", "activation", sg[d][:, :], y[:, d * T:(d + 1) * T],
                  AF.Silu, wait_edges=e_fin, inc=True) for d in range(DT)]
    return edges



def emit_l3(b, g, W, xT, pl, in_edges):
    CL = 256
    NCH = T // CL
    NB = 4
    idb = g["ident_bf"]
    onesb = g["ones_bf_128x1"]
    ones_f = g["ones_f32_128x1"]
    onesr_f = g["ones_f32_1x128"]
    sel16_buf = pl.bf(2048)
    sel16 = sel16_buf[0:16, :]
    e_sel = b.dma(sel16, W["ct_sel16"], queue="sync")
    xnc = pl.bf(8 * CL)
    xln = pl.bf(8 * CL)
    sgc = pl.bf(8 * CL)
    Psb = pl.bf(2 * CL)
    Qsb = pl.bf(2 * CL)
    xcb = pl.bf(16 * (CL + 8))
    zs = pl.bf(16 * CL)
    xconv = pl.bf(16 * CL)
    ygated = pl.bf(16 * CL)
    prd = pl.bf(CL)
    hst = prd
    prB = pl.bf(CL)
    hst2 = prB
    prC = pl.bf(CL)
    dt = pl.f32(CL)
    sp1 = pl.f32(CL)
    sp2 = pl.f32(CL)
    dtu = pl.f32(CL)
    dA = pl.bf(NB * CL)
    dBu = pl.bf(NB * CL)
    hsc = pl.bf(NB * CL)
    tmpb = pl.f32(NB * CL)
    pt12 = pl.f32(2 * CL)
    pt1 = pt12[:, 0:CL]
    pt2 = pt12[:, CL:2 * CL]
    Brep = pl.bf(16 * CL)
    Crep = pl.bf(16 * CL)
    yac = pl.f32(CL)
    state = pl.f32(256)
    rows = pl.f32(2 * CL)
    rep = pl.f32(2 * CL)
    sqw = pl.f32(CL)
    tev = pl.f32(CL)
    asb = pl.f32(256)
    wu1 = pl.bf(8 * 256)
    wv1 = pl.bf(2 * 4096)
    wu2r = [pl.bf(256), pl.bf(256)]
    wv2 = pl.bf(2 * 1024)
    winb = pl.bf(8 * 512)
    wxp = pl.bf(16 * 96)
    wdt = pl.bf(2048)
    wob = pl.bf(16 * 128)
    lnw = W["l3_lnw_sb"]
    lnbc = W["l3_lnb_sb"]
    cwc = W["l3_convw_sb"]
    cbc = W["l3_convb_sb"]
    bdtc = W["l3_bdt_sb"]
    dskc = W["l3_dsk_sb"]
    f1b = W["l3f_b1_sb"]
    f2b = W["l3f_b2_sb"]
    e_al = b.dma(asb.rearrange("p (a n) -> p a n", a=16),
                 W["l3_alog"].rearrange("(a p) n -> p a n", p=128),
                 queue="sync")
    e_ax = b.op("scalar", "activation", asb, asb, AF.Exp, wait_edges=[e_al],
                inc=True)
    e_A = b.op("vector", "tensor_scalar", asb, asb, -1.0, None, ALU.mult,
               wait_edges=[e_ax], inc=True)
    b.op("vector", "memset", state, 0.0)
    e_st0 = b.op("vector", "nop", inc=True)
    e_wu1 = wload(b, wu1, W["l3f_u1"])
    e_wv1 = wload(b, wv1, W["l3f_v1"])

    e_wv2 = wload(b, wv2, W["l3f_v2"])
    e_wxp = wload(b, wxp, W["l3_wxp_t"])
    e_wdt = b.dma(wdt[0:64, :], W["l3_wdt_t"], queue="gpsimd")
    A_ = rows[0:1, 0:CL]
    B_ = rows[0:1, CL:2 * CL]
    tmp2 = rep[0:1, 0:CL]
    e_upd = list(in_edges)
    e_win_mm = None
    e_wo_mm = None
    chunks = [slice(c * CL, (c + 1) * CL) for c in range(NCH)]
    for cch, sl in enumerate(chunks):
        def ln_chunk(src_tiles, dst, gcol, bcol, waits):
            ones1 = onesb if src_tiles(0).dtype == BF16 else ones_f
            b.mm_group(0, [(b.pbank[0][0:1, 0:CL], ones1, src_tiles(d))
                           for d in range(8)], wait_edges=waits)
            b.bank_read(0, "vector", "tensor_copy", A_, b.pbank[0][0:1, 0:CL])
            last_mm = None
            for d in range(8):
                e_s2 = b.op("scalar", "activation", sqw, src_tiles(d), AF.Square,
                            wait_edges=waits if d == 0 else [], inc=True)
                last_mm = b.mm_step(b.pbank[1][0:1, 0:CL], ones_f, sqw,
                                    start=(d == 0), stop=(d == 7),
                                    wait_edges=[e_s2])
            b.op("vector", "tensor_copy", B_, b.pbank[1][0:1, 0:CL],
                 wait_edges=[last_mm])
            b.op("vector", "tensor_scalar", A_, A_, -1.0 / D, None, ALU.mult)
            b.op("vector", "tensor_scalar", B_, B_, 1.0 / D, None, ALU.mult)
            b.op("vector", "tensor_mul", tmp2, A_, A_)
            b.op("vector", "tensor_sub", B_, B_, tmp2)
            e_var = b.op("vector", "tensor_scalar", B_, B_, 1e-5, None, ALU.add,
                         inc=True)
            e_sqr = b.op("scalar", "activation", B_, B_, AF.Sqrt,
                         wait_edges=[e_var], inc=True)
            b.op("vector", "reciprocal", B_, B_, wait_edges=[e_sqr])
            e_stats = b.op("vector", "tensor_mul", A_, A_, B_, inc=True)
            bk = b.rr(6)
            b.mm_group(bk, [(b.pbank[bk][:, 0:CL], onesr_f, B_)],
                       wait_edges=[e_stats])
            e_r1 = b.bank_read(bk, "vector", "tensor_copy", rep[:, 0:CL],
                               b.pbank[bk][:, 0:CL])
            bk = b.rr(6)
            b.mm_group(bk, [(b.pbank[bk][:, 0:CL], onesr_f, A_)],
                       wait_edges=[e_stats])
            e_r2 = b.bank_read(bk, "vector", "tensor_copy", rep[:, CL:2 * CL],
                               b.pbank[bk][:, 0:CL])
            ee = None
            for d in range(8):
                b.op("vector", "tensor_mul", tev, src_tiles(d), rep[:, 0:CL],
                     wait_edges=[e_r1, e_r2])
                if gcol is None:
                    ee = b.op("vector", "tensor_add", dst(d), tev,
                              rep[:, CL:2 * CL], inc=True)
                else:
                    b.op("vector", "tensor_add", tev, tev, rep[:, CL:2 * CL])
                    ee = b.op("vector", "tensor_scalar", dst(d), tev,
                              gcol(d), bcol(d), ALU.mult, ALU.add, inc=True)
            return ee
        e_xnc = ln_chunk(lambda d: xT[d][:, sl],
                         lambda d: xnc[:, d * CL:(d + 1) * CL], None, None,
                         e_upd)
        for m in range(2):
            bk = b.rr(6)
            b.mm_group(bk, [(b.pbank[bk][:, 0:CL],
                             wu1[:, k * 256 + m * 128:k * 256 + m * 128 + 128],
                             xnc[:, k * CL:(k + 1) * CL]) for k in range(8)],
                       wait_edges=[e_wu1, e_xnc])
            b.bank_read(bk, "vector", "tensor_copy", Psb[:, m * CL:(m + 1) * CL],
                        b.pbank[bk][:, 0:CL])
        e_P = b.fence(engines=("vector",))
        last_q = [None, None]
        prevF = None
        for mt in range(33):
            if mt < 32:
                bkH = b.rr(6)
                b.mm_group(bkH, [(b.pbank[bkH][:, 0:CL],
                                  wv1[:, k * 4096 + mt * 128:k * 4096 + mt * 128 + 128],
                                  Psb[:, k * CL:(k + 1) * CL]) for k in range(2)],
                           wait_edges=[e_wv1] + e_P)
                hb = hst if mt % 2 == 0 else hst2
                e_g = b.bank_read(bkH, "scalar", "activation", hb,
                                  b.pbank[bkH][:, 0:CL], AF.Gelu_apprx_tanh,
                                  bias=f1b[:, mt:mt + 1])
                wbuf = wu2r[mt % 2]
                e_wu2 = wload(b, wbuf, W["l3f_u2"][mt * 128:(mt + 1) * 128, :],
                              wait_edges=[last_q[1]] if last_q[1] is not None else [])
                curF = (hb, wbuf, e_g, e_wu2, mt)
            else:
                curF = None
            if prevF is not None:
                hbp, wbp, e_gp, e_w2p, mtp = prevF
                for m in range(2):
                    last_q[m] = b.mm_step(
                        b.pbank[6 + m][:, 0:CL],
                        wbp[:, m * 128:(m + 1) * 128],
                        hbp, start=(mtp == 0), stop=(mtp == 31),
                        wait_edges=[e_w2p, e_gp])
            prevF = curF
        for m in range(2):
            last_q[m].consumed = True
            b.op("vector", "tensor_copy", Qsb[:, m * CL:(m + 1) * CL],
                 b.pbank[6 + m][:, 0:CL], wait_edges=[last_q[m]], inc=True)
        e_Q = b.fence(engines=("vector",))
        for do in range(DT):
            bk = b.rr(6)
            b.mm_group(bk, [(b.pbank[bk][:, 0:CL],
                             wv2[:, k * 1024 + do * 128:k * 1024 + do * 128 + 128],
                             Qsb[:, k * CL:(k + 1) * CL]) for k in range(2)],
                       wait_edges=[e_wv2] + e_Q)
            e_t = b.bank_read(bk, "vector", "tensor_scalar", tev,
                              b.pbank[bk][:, 0:CL], f2b[:, do:do + 1], None,
                              ALU.add)
            b.op("scalar", "activation", sgc[:, do * CL:(do + 1) * CL], tev,
                 AF.Silu, wait_edges=[e_t], inc=True)
        e_sgc = b.fence(engines=("scalar",))
        e_xln = ln_chunk(lambda d: xnc[:, d * CL:(d + 1) * CL],
                         lambda d: xln[:, d * CL:(d + 1) * CL],
                         lambda d: lnw[:, d:d + 1], lambda d: lnbc[:, d:d + 1],
                         [e_xnc])
        e_carry = []
        for dti in range(16):
            c0 = dti * (CL + 8)
            if cch == 0:
                e_c0 = b.op("vector", "memset", xcb[:, c0:c0 + 3], 0.0, inc=True)
            else:
                e_c0 = b.op("vector", "tensor_copy", xcb[:, c0:c0 + 3],
                            xcb[:, c0 + CL:c0 + CL + 3], inc=True)
            e_carry.append(e_c0)
        for slab in range(8):
            e_w = wload(b, winb, W["l3_win_t"][:, slab * 512:(slab + 1) * 512],
                        wait_edges=[e for e in (e_win_mm,) if e])
            for mt in range(4):
                gi = slab * 4 + mt
                bk = b.rr(6)
                e_win_mm = b.mm_group(bk, [
                    (b.pbank[bk][:, 0:CL],
                     winb[:, k * 512 + mt * 128:k * 512 + mt * 128 + 128],
                     xln[:, k * CL:(k + 1) * CL]) for k in range(8)],
                    wait_edges=[e_w, e_xln])
                if gi < 16:
                    b.bank_read(bk, "vector", "tensor_copy",
                                xcb[:, gi * (CL + 8) + 3:gi * (CL + 8) + 3 + CL],
                                b.pbank[bk][:, 0:CL],
                                wait_edges=[e_carry[gi]])
                else:
                    t_ = gi - 16
                    b.bank_read(bk, "scalar", "activation",
                                zs[:, t_ * CL:(t_ + 1) * CL],
                                b.pbank[bk][:, 0:CL], AF.Silu)
        e_xz = b.fence()
        e_conv = []
        for dti in range(16):
            c0 = dti * (CL + 8)
            b.op("vector", "tensor_scalar", tev, xcb[:, c0:c0 + CL],
                 cwc[:, dti * 4:dti * 4 + 1], None, ALU.mult,
                 wait_edges=e_xz if dti == 0 else [])
            for kk in range(1, 4):
                b.op("vector", "scalar_tensor_tensor", tev,
                     xcb[:, c0 + kk:c0 + kk + CL],
                     cwc[:, dti * 4 + kk:dti * 4 + kk + 1], tev,
                     ALU.mult, ALU.add)
            e_c = b.op("scalar", "activation", xconv[:, dti * CL:(dti + 1) * CL],
                       tev, AF.Silu, bias=cbc[:, dti:dti + 1], inc=True)
            e_conv.append(e_c)
        bkd = b.rr(6)
        b.mm_group(bkd, [(b.pbank[bkd][0:64, 0:CL], wxp[:, k * 96:k * 96 + 64],
                          xconv[:, k * CL:(k + 1) * CL]) for k in range(16)],
                   wait_edges=[e_wxp, e_conv[-1]])
        b.bank_read(bkd, "vector", "tensor_copy", prd[0:64, :],
                    b.pbank[bkd][0:64, 0:CL])
        bkB = b.rr(6)
        b.mm_group(bkB, [(b.pbank[bkB][0:16, 0:CL], wxp[:, k * 96 + 64:k * 96 + 80],
                          xconv[:, k * CL:(k + 1) * CL]) for k in range(16)],
                   wait_edges=[e_wxp])
        b.bank_read(bkB, "vector", "tensor_copy", prB[0:16, :],
                    b.pbank[bkB][0:16, 0:CL])
        bkC = b.rr(6)
        b.mm_group(bkC, [(b.pbank[bkC][0:16, 0:CL], wxp[:, k * 96 + 80:k * 96 + 96],
                          xconv[:, k * CL:(k + 1) * CL]) for k in range(16)],
                   wait_edges=[e_wxp])
        b.bank_read(bkC, "vector", "tensor_copy", prC[0:16, :],
                    b.pbank[bkC][0:16, 0:CL])
        e_pr = b.fence(engines=("vector",))
        e_brep = None
        for n in range(16):
            bk = b.rr(6)
            b.mm_group(bk, [(b.pbank[bk][:, 0:CL], sel16[:, n * 128:(n + 1) * 128],
                             prB[0:16, :])], wait_edges=e_pr + [e_sel])
            b.bank_read(bk, "scalar", "copy", Brep[:, n * CL:(n + 1) * CL],
                        b.pbank[bk][:, 0:CL])
            bk = b.rr(6)
            b.mm_group(bk, [(b.pbank[bk][:, 0:CL], sel16[:, n * 128:(n + 1) * 128],
                             prC[0:16, :])], wait_edges=e_pr)
            e_brep = b.bank_read(bk, "scalar", "copy", Crep[:, n * CL:(n + 1) * CL],
                                 b.pbank[bk][:, 0:CL])
        e_yg = None
        for dti in range(16):
            bk = b.rr(6)
            b.mm_group(bk, [(b.pbank[bk][:, 0:CL], wdt[0:64, dti * 128:(dti + 1) * 128],
                             prd[0:64, :])], wait_edges=[e_wdt] + e_pr)
            e_t0 = b.bank_read(bk, "vector", "tensor_scalar", sp1,
                               b.pbank[bk][:, 0:CL], bdtc[:, dti:dti + 1], None,
                               ALU.add)
            e_a1 = b.op("scalar", "activation", sp2, sp1, AF.Abs,
                        wait_edges=[e_t0], inc=True)
            b.op("scalar", "activation", sp2, sp2, AF.Exp, scale=-1.0)
            e_a2 = b.op("scalar", "activation", sp2, sp2, AF.Ln, bias=1.0,
                        inc=True)
            b.op("vector", "tensor_relu", dt, sp1, wait_edges=[e_a2])
            e_dt = b.op("vector", "tensor_add", dt, dt, sp2, inc=True)
            e_dtu = b.op("gpsimd", "tensor_mul", dtu, dt,
                         xconv[:, dti * CL:(dti + 1) * CL],
                         wait_edges=[e_dt, e_conv[dti]], inc=True)
            for nb in range(16 // NB):
                for j in range(NB):
                    n = nb * NB + j
                    b.op("scalar", "activation", dA[:, j * CL:(j + 1) * CL],
                         dt, AF.Exp, scale=asb[:, dti * 16 + n:dti * 16 + n + 1],
                         wait_edges=[e_dt, e_A], inc=True)
                dtu3 = dtu.rearrange("p (o t) -> p o t", o=1).broadcast_to(
                    (128, NB, CL))
                dBu3 = dBu.rearrange("p (o t) -> p o t", o=NB)
                Brep3 = Brep[:, nb * NB * CL:(nb + 1) * NB * CL].rearrange(
                    "p (o t) -> p o t", o=NB)
                e_dbu = b.op("vector", "tensor_tensor", dBu3, dtu3, Brep3,
                             ALU.mult, wait_edges=[e_dtu, e_brep], inc=True)
                e_scan = []
                for j in range(NB):
                    n = nb * NB + j
                    e_s = b.op("vector", "tensor_tensor_scan",
                               hsc[:, j * CL:(j + 1) * CL],
                               dA[:, j * CL:(j + 1) * CL],
                               dBu[:, j * CL:(j + 1) * CL],
                               state[:, dti * 16 + n:dti * 16 + n + 1],
                               ALU.mult, ALU.add,
                               wait_edges=[e_dbu, e_st0], inc=True)
                    e_scan.append(e_s)
                hsc3 = hsc.rearrange("p (o t) -> p o t", o=NB)
                c0 = dti * 16 + nb * NB
                b.op("vector", "tensor_copy",
                     state[:, c0:c0 + NB].rearrange("p (o t) -> p o t", o=NB),
                     hsc3[:, :, CL - 1:CL], wait_edges=e_scan)
                tmp3o = tmpb.rearrange("p (o t) -> p o t", o=NB)
                Crep3 = Crep[:, nb * NB * CL:(nb + 1) * NB * CL].rearrange(
                    "p (o t) -> p o t", o=NB)
                e_tm = b.op("gpsimd", "tensor_tensor", tmp3o, hsc3, Crep3,
                            ALU.mult, wait_edges=e_scan, inc=True)
                e_p12 = b.op("gpsimd", "tensor_add", pt12, tmpb[:, 0:2 * CL],
                             tmpb[:, 2 * CL:4 * CL], wait_edges=[e_tm], inc=True)
                if nb == 0:
                    e_y = b.op("vector", "tensor_add", yac, pt1, pt2,
                               wait_edges=[e_p12], inc=True)
                else:
                    b.op("vector", "tensor_add", pt1, pt1, pt2,
                         wait_edges=[e_p12])
                    e_y = b.op("vector", "tensor_add", yac, yac, pt1, inc=True)
            b.op("vector", "scalar_tensor_tensor", yac,
                 xconv[:, dti * CL:(dti + 1) * CL], dskc[:, dti:dti + 1], yac,
                 ALU.mult, ALU.add, wait_edges=[e_y])
            e_yg = b.op("vector", "tensor_mul", ygated[:, dti * CL:(dti + 1) * CL],
                        yac, zs[:, dti * CL:(dti + 1) * CL], inc=True)
        e_fin = b.fence(engines=("vector",))
        for do in range(DT):
            e_wo = wload(b, wob, W["l3_wout_t"][:, do * 128:(do + 1) * 128],
                         wait_edges=[e for e in (e_wo_mm,) if e])
            bk = b.rr(6)
            e_wo_mm = b.mm_group(bk, [
                (b.pbank[bk][:, 0:CL], wob[:, k * 128:(k + 1) * 128],
                 ygated[:, k * CL:(k + 1) * CL]) for k in range(16)],
                wait_edges=[e_wo] + e_fin)
            b.bank_read(bk, "vector", "tensor_add", tev, b.pbank[bk][:, 0:CL],
                        xnc[:, do * CL:(do + 1) * CL])
            b.op("vector", "tensor_mul", tev, tev, sgc[:, do * CL:(do + 1) * CL],
                 wait_edges=e_sgc)
            e_u = b.op("vector", "tensor_add", xT[do][:, sl], xT[do][:, sl], tev,
                       inc=True)
        e_upd = [e_u]
    return e_upd
